# version 19
# baseline (speedup 1.0000x reference)
import numpy as np
import ml_dtypes
from contextlib import ExitStack
import concourse.bass as bass
import concourse.mybir as mybir
from concourse.bass_utils import run_bass_kernel_spmd
from concourse.alu_op_type import AluOpType as OP

F32 = mybir.dt.float32
BF16 = mybir.dt.bfloat16
AF = mybir.ActivationFunctionType
BF = ml_dtypes.bfloat16

T = 2048
D = 1024
NTL = 8
SEG = 512
NSEG = T // SEG
LN_EPS = 1e-5

LAUNCH_PLAN = [["l0_attn", "l0_ffn", "l1_attn", "l1_ffn"]]


class Prog:

    def __init__(self):
        self.nc = bass.Bass("TRN2", target_bir_lowering=False)
        self.es = ExitStack()
        self.sems = {}
        self.cnt = {}
        self.inputs_used = []

    def sem(self, *names):
        for n in names:
            self.sems[n] = self.es.enter_context(self.nc.semaphore(n))
            self.cnt[n] = 0

    def inc(self, instr, name, n=1):
        instr.then_inc(self.sems[name], n)
        self.cnt[name] += n

    def wait(self, eng, name, val=None):
        eng.wait_ge(self.sems[name], self.cnt[name] if val is None else val)

    def dma(self, eng, sem, gate=True, **kw):
        if gate and self.cnt[sem] > 0:
            eng.wait_ge(self.sems[sem], self.cnt[sem])
        ins = eng.dma_start(**kw)
        self.inc(ins, sem, 16)
        return ins

    def sbuf(self, name, shape, dtype):
        return self.es.enter_context(self.nc.sbuf_tensor(name, shape, dtype))

    def carve_reset(self):
        self.car = 0

    def sb(self, name, shape, dtype):
        cols = shape[1]
        self.car = (self.car + 15) & ~15
        if dtype == BF16:
            v = self.big[:, self.car:self.car + cols]
            self.car += cols
        else:
            assert dtype == F32
            o2 = self.car // 2
            v = self.big_f[:, o2:o2 + cols]
            self.car += 2 * cols
        assert self.car <= self.big.shape[1], f"arena overflow at {name}: {self.car}"
        if shape[0] != 128:
            v = v[0:shape[0], :]
        return v

    def bkwait(self, *idxs):
        for i in idxs:
            s, v = self.bk[i]
            if s is not None:
                self.wait(self.nc.tensor, s, v)

    def bkset(self, idx, sem):
        self.bk[idx] = (sem, self.cnt[sem])

    def barrier(self):
        nc = self.nc
        for eng in (nc.tensor, nc.vector, nc.scalar, nc.gpsimd, nc.sync):
            for s in self.sems:
                eng.wait_ge(self.sems[s], self.cnt[s])

    def psum(self, name, shape, dtype=F32):
        return self.es.enter_context(self.nc.psum_tensor(name, shape, dtype))

    def din(self, name, shape, dtype=F32):
        if not hasattr(self, "_dins"):
            self._dins = {}
        if name not in self._dins:
            self.inputs_used.append(name)
            self._dins[name] = self.nc.dram_tensor(name, shape, dtype, kind="ExternalInput").ap()
        return self._dins[name]

    def dout(self, name, shape, dtype=F32):
        return self.nc.dram_tensor(name, shape, dtype, kind="ExternalOutput").ap()

    def dtmp(self, name, shape, dtype):
        return self.nc.dram_tensor(name, shape, dtype, kind="Internal").ap()

    def dump(self, name, ap_2d):
        if not getattr(self, "debug", False):
            return
        w = ap_2d.shape[-1]
        d = self.nc.dram_tensor("dbg_" + name, [ap_2d.shape[0], w], ap_2d.dtype,
                                kind="ExternalOutput").ap()
        self.wait(self.nc.gpsimd, 'V')
        self.wait(self.nc.gpsimd, 'A')
        self.wait(self.nc.gpsimd, 'T')
        self.dma(self.nc.gpsimd, 'O', out=d, in_=ap_2d)



def emit_ln_stats(p, src_f32_slices, xb, sq, ones_col_bf, pS, pQ, seg_w):
    nc = p.nc
    for i in range(NTL):
        nc.scalar.activation(out=xb[:, i * seg_w:(i + 1) * seg_w], in_=src_f32_slices[i],
                             func=AF.Identity)
        ins = nc.scalar.activation(out=sq[:, i * seg_w:(i + 1) * seg_w], in_=src_f32_slices[i],
                                   func=AF.Square)
    p.inc(ins, 'A')
    p.wait(nc.tensor, 'A')
    p.wait(nc.tensor, 'L', p.const_L)
    for i in range(NTL):
        ins = nc.tensor.matmul(pS[0:1, :seg_w], ones_col_bf[:, 0:1],
                               xb[:, i * seg_w:(i + 1) * seg_w],
                               start=(i == 0), stop=(i == NTL - 1))
    p.inc(ins, 'T')
    for i in range(NTL):
        ins = nc.tensor.matmul(pQ[0:1, :seg_w], ones_col_bf[:, 0:1],
                               sq[:, i * seg_w:(i + 1) * seg_w],
                               start=(i == 0), stop=(i == NTL - 1))
    p.inc(ins, 'T')


def emit_ln_rows(p, pS, pQ, rM, rA, rB, rT, seg_w, double_ln=False):
    nc = p.nc

    def af(ins):
        p.inc(ins, 'A')
        p.wait(nc.scalar, 'A')

    def vf(ins):
        p.inc(ins, 'V')
        p.wait(nc.vector, 'V')

    p.wait(nc.scalar, 'T')
    nc.scalar.activation(out=rM[0:1, :seg_w], in_=pS[0:1, :seg_w], func=AF.Identity,
                         scale=-1.0 / D)
    nc.scalar.activation(out=rT[0:1, :seg_w], in_=pS[0:1, :seg_w], func=AF.Square,
                         scale=1.0 / D)
    ins = nc.scalar.activation(out=rA[0:1, :seg_w], in_=pQ[0:1, :seg_w], func=AF.Identity,
                               scale=1.0 / D)
    af(ins)
    p.wait(nc.vector, 'A')
    vf(nc.vector.tensor_tensor(out=rA[0:1, :seg_w], in0=rA[0:1, :seg_w], in1=rT[0:1, :seg_w],
                               op=OP.subtract))
    p.wait(nc.scalar, 'V')
    if not double_ln:
        nc.scalar.activation(out=rT[0:1, :seg_w], in_=rA[0:1, :seg_w], func=AF.Ln,
                             bias=LN_EPS)
        af(nc.scalar.activation(out=rA[0:1, :seg_w], in_=rT[0:1, :seg_w], func=AF.Exp,
                                scale=-0.5))
        p.wait(nc.vector, 'A')
    else:
        nc.scalar.activation(out=rT[0:1, :seg_w], in_=rA[0:1, :seg_w], func=AF.Ln,
                             bias=LN_EPS)
        af(nc.scalar.activation(out=rT[0:1, :seg_w], in_=rT[0:1, :seg_w], func=AF.Exp,
                                scale=-0.5))
        p.wait(nc.vector, 'A')
        vf(nc.vector.tensor_tensor(out=rA[0:1, :seg_w], in0=rA[0:1, :seg_w], in1=rT[0:1, :seg_w],
                                   op=OP.mult))
        vf(nc.vector.tensor_tensor(out=rA[0:1, :seg_w], in0=rA[0:1, :seg_w], in1=rT[0:1, :seg_w],
                                   op=OP.mult))
        p.wait(nc.scalar, 'V')
        nc.scalar.activation(out=rA[0:1, :seg_w], in_=rA[0:1, :seg_w], func=AF.Ln,
                             bias=LN_EPS)
        af(nc.scalar.activation(out=rA[0:1, :seg_w], in_=rA[0:1, :seg_w], func=AF.Exp,
                                scale=-0.5))
        p.wait(nc.vector, 'A')
        vf(nc.vector.tensor_tensor(out=rA[0:1, :seg_w], in0=rA[0:1, :seg_w], in1=rT[0:1, :seg_w],
                                   op=OP.mult))
    vf(nc.vector.tensor_tensor(out=rB[0:1, :seg_w], in0=rM[0:1, :seg_w], in1=rA[0:1, :seg_w],
                               op=OP.mult))


def emit_ln_bcast(p, rA, rB, pA, pB, ones_row_f, seg_w, lhsA=None, lhsB=None):
    nc = p.nc
    p.wait(nc.tensor, 'V')
    p.wait(nc.tensor, 'L', p.const_L)
    ins = nc.tensor.matmul(pA[:, :seg_w], lhsA if lhsA is not None else ones_row_f[0:1, :],
                           rA[0:1, :seg_w], start=True, stop=True)
    ins = nc.tensor.matmul(pB[:, :seg_w], lhsB if lhsB is not None else ones_row_f[0:1, :],
                           rB[0:1, :seg_w], start=True, stop=True)
    p.inc(ins, 'T')



def emit_l0_attn(p, xs):
    nc = p.nc

    d_wk = p.din("l0a_WkT", [D, D])
    d_wv = p.din("l0a_WvT", [D, D])
    d_wr = p.din("l0a_WrT", [D, D])
    d_wo = p.din("l0a_WoT", [D, D])
    d_vec = p.din("l0a_vecs", [128, 7 * NTL])

    wk = p.sb("wk_sb", [128, NTL * D], BF16)
    wv = p.sb("wv_sb", [128, NTL * D], BF16)
    wr = p.sb("wr_sb", [128, NTL * D], BF16)
    wo = p.sb("wo_sb", [128, NTL * D], BF16)
    vec = p.sb("vec_sb", [128, 7 * NTL], F32)
    vtd = p.sb("vtd", [128, NTL], F32)
    vetf = p.sb("vetf", [128, NTL], F32)
    tdcur = p.sb("tdcur", [128, SEG], F32)
    xn = p.sb("xn_seg", [128, NTL * SEG], BF16)
    sq = p.sb("sq_seg", [128, NTL * SEG], BF16)
    kin = p.sb("kin_seg", [128, NTL * SEG], BF16)
    xb = p.sb("xb_seg", [128, NTL * SEG], BF16)
    vin = p.sb("vin_seg", [128, NTL * SEG], BF16)
    rin = p.sb("rin_seg", [128, NTL * SEG], BF16)
    rwkv = p.sb("rwkv_seg", [128, NTL * SEG], BF16)
    attn = kin
    pcol = p.sb("pcol", [128, NTL], BF16)
    carN = p.sb("carN", [128, NTL], F32)
    carD = p.sb("carD", [128, NTL], F32)
    ekB = [p.sb("ek_s0", [128, SEG], F32), p.sb("ek_s1", [128, SEG], F32)]
    vvB = [p.sb("vv_s0", [128, SEG], F32), p.sb("vv_s1", [128, SEG], F32)]
    rrB = [p.sb("rr_s0", [128, SEG], F32), p.sb("rr_s1", [128, SEG], F32)]
    snumB = [p.sb("snum_s0", [128, SEG], F32), p.sb("snum_s1", [128, SEG], F32)]
    ek, vv, rr, snum = ekB[0], vvB[0], rrB[0], snumB[0]
    sden = tdcur
    lnt = ek
    rM = ek[0:1, :]
    rA = vv[0:1, :]
    rB = rr[0:1, :]
    rT = snum[0:1, :]
    onesc = p.onesc
    onesr = p.onesr

    pS, pQ, pA, pB = p.bank[0], p.bank[1], p.bank[2], p.bank[3]
    pK = [p.bank[4], p.bank[5]]
    pV = [p.bank[6], p.bank[7]]

    ins = p.dma(nc.sync, 'L', out=vec[:, :], in_=d_vec)
    for w_sb, d_w in ((wk, d_wk), (wv, d_wv), (wr, d_wr), (wo, d_wo)):
        for k in range(NTL):
            ins = p.dma(nc.gpsimd, 'W', gate=(k == 0), out=w_sb[:, k * D:(k + 1) * D],
                                      in_=d_w[k * 128:(k + 1) * 128, :])

    p.wait(nc.scalar, 'L')
    nc.scalar.activation(out=vtd[:, :], in_=vec[:, 0:NTL], func=AF.Exp)
    nc.scalar.activation(out=vtd[:, :], in_=vtd[:, :], func=AF.Exp, scale=-1.0)
    ins = nc.scalar.activation(out=vetf[:, :], in_=vec[:, NTL:2 * NTL], func=AF.Exp)
    p.inc(ins, 'A')

    nc.vector.memset(pcol[:, :], 0.0)
    nc.vector.memset(carN[:, :], 0.0)
    ins = nc.vector.memset(carD[:, :], 0.0)
    p.inc(ins, 'V')

    tmk = vec[:, 2 * NTL:3 * NTL]
    tmv = vec[:, 3 * NTL:4 * NTL]
    tmr = vec[:, 4 * NTL:5 * NTL]
    bo = vec[:, 5 * NTL:6 * NTL]

    wait_x = p.cnt['L']

    def emit_ln1(s, vwait=None):
        c0 = s * SEG

        def xsl(i):
            return xs[:, i * T + c0: i * T + c0 + SEG]
        p.wait(nc.scalar, 'L', wait_x)
        if vwait is not None:
            p.wait(nc.scalar, 'V', vwait)
        emit_ln_stats(p, [xsl(i) for i in range(NTL)], xb, sq, onesc, pS, pQ, SEG)
        emit_ln_rows(p, pS, pQ, rM, rA, rB, rT, SEG)
        emit_ln_bcast(p, rA, rB, pA, pB, onesr, SEG)
        p.wait(nc.vector, 'T')
        for i in range(NTL):
            nc.vector.tensor_tensor(out=lnt[:, :], in0=xsl(i), in1=pA[:, :], op=OP.mult)
            ins = nc.vector.tensor_tensor(out=xn[:, i * SEG:(i + 1) * SEG], in0=lnt[:, :],
                                          in1=pB[:, :], op=OP.add)
        p.inc(ins, 'V')
        p.bkset(2, 'V')
        p.bkset(3, 'V')

    emit_ln1(0)
    for s in range(NSEG):
        c0 = s * SEG

        def xsl(i):
            return xs[:, i * T + c0: i * T + c0 + SEG]

        if s > 0:
            p.wait(nc.vector, 'T', prev_kvr_reads)
        for i in range(NTL):
            a = i * SEG
            nc.vector.tensor_tensor(out=sq[:, a:a + 1], in0=xn[:, a:a + 1],
                                    in1=pcol[:, i:i + 1], op=OP.subtract)
            nc.vector.tensor_tensor(out=sq[:, a + 1:a + SEG], in0=xn[:, a + 1:a + SEG],
                                    in1=xn[:, a:a + SEG - 1], op=OP.subtract)
            for dst, tm in ((kin, tmk), (vin, tmv), (rin, tmr)):
                nc.vector.scalar_tensor_tensor(out=dst[:, a:a + 1], in0=sq[:, a:a + 1],
                                               scalar=tm[:, i:i + 1], in1=pcol[:, i:i + 1],
                                               op0=OP.mult, op1=OP.add)
                nc.vector.scalar_tensor_tensor(out=dst[:, a + 1:a + SEG], in0=sq[:, a + 1:a + SEG],
                                               scalar=tm[:, i:i + 1], in1=xn[:, a:a + SEG - 1],
                                               op0=OP.mult, op1=OP.add)
            ins = nc.vector.tensor_copy(out=pcol[:, i:i + 1], in_=xn[:, a + SEG - 1:a + SEG])
        p.inc(ins, 'V')
        mix_done = p.cnt['V']

        wkv_chain = []
        for m in range(NTL):
            bK = pK[m % 2]
            bV = pV[m % 2]
            ek, vv, rr, snum = ekB[m % 2], vvB[m % 2], rrB[m % 2], snumB[m % 2]
            sden = tdcur
            if s == 0 and m == 0:
                p.wait(nc.tensor, 'W')
            p.wait(nc.tensor, 'V', mix_done)
            p.bkwait(4 + m % 2, 6 + m % 2)
            for k in range(NTL):
                ins = nc.tensor.matmul(bK[:, :], wk[:, k * D + m * 128: k * D + (m + 1) * 128],
                                       kin[:, k * SEG:(k + 1) * SEG],
                                       start=(k == 0), stop=(k == NTL - 1))
            for k in range(NTL):
                ins = nc.tensor.matmul(bV[:, :], wv[:, k * D + m * 128: k * D + (m + 1) * 128],
                                       vin[:, k * SEG:(k + 1) * SEG],
                                       start=(k == 0), stop=(k == NTL - 1))
            p.inc(ins, 'T')
            tKV = p.cnt['T']
            p.wait(nc.scalar, 'T', tKV)
            if m >= 2:
                p.wait(nc.scalar, 'V', wkv_chain[m - 2])
            elif s > 0:
                p.wait(nc.scalar, 'V', seg_wkv_last)
            nc.scalar.activation(out=ek[:, :], in_=bK[:, :], func=AF.Exp)
            ins = nc.scalar.activation(out=vv[:, :], in_=bV[:, :], func=AF.Identity)
            p.inc(ins, 'A')
            eKV = p.cnt['A']
            p.bkset(6 + m % 2, 'A')
            p.wait(nc.tensor, 'A', eKV)
            for k in range(NTL):
                ins = nc.tensor.matmul(bK[:, :], wr[:, k * D + m * 128: k * D + (m + 1) * 128],
                                       rin[:, k * SEG:(k + 1) * SEG],
                                       start=(k == 0), stop=(k == NTL - 1))
            p.inc(ins, 'T')
            if m == NTL - 1:
                prev_kvr_reads = p.cnt['T']
            p.wait(nc.scalar, 'T')
            ins = nc.scalar.activation(out=rr[:, :], in_=bK[:, :], func=AF.Exp, scale=-1.0)
            p.inc(ins, 'A')
            p.bkset(4 + m % 2, 'A')
            ins = nc.scalar.activation(out=rr[:, :], in_=rr[:, :], func=AF.Ln, bias=1.0)
            p.inc(ins, 'A')
            ins = nc.scalar.activation(out=rr[:, :], in_=rr[:, :], func=AF.Exp, scale=-1.0)
            p.inc(ins, 'A')

            p.wait(nc.vector, 'A', eKV)
            nc.vector.tensor_scalar(out=tdcur[:, :], in0=xs[:, 0:SEG], scalar1=0.0,
                                    scalar2=vtd[:, m:m + 1], op0=OP.mult, op1=OP.add)
            nc.vector.tensor_tensor(out=vv[:, :], in0=ek[:, :], in1=vv[:, :], op=OP.mult)
            nc.vector.tensor_tensor_scan(out=snum[:, :], data0=tdcur[:, :],
                                         data1=vv[:, :], initial=carN[:, m:m + 1],
                                         op0=OP.mult, op1=OP.add)
            nc.vector.tensor_tensor_scan(out=sden[:, :], data0=tdcur[:, :],
                                         data1=ek[:, :], initial=carD[:, m:m + 1],
                                         op0=OP.mult, op1=OP.add)
            nc.vector.scalar_tensor_tensor(out=vv[:, 0:1], in0=vv[:, 0:1],
                                           scalar=vetf[:, m:m + 1], in1=carN[:, m:m + 1],
                                           op0=OP.mult, op1=OP.add)
            nc.vector.scalar_tensor_tensor(out=vv[:, 1:SEG], in0=vv[:, 1:SEG],
                                           scalar=vetf[:, m:m + 1], in1=snum[:, 0:SEG - 1],
                                           op0=OP.mult, op1=OP.add)
            nc.vector.scalar_tensor_tensor(out=ek[:, 0:1], in0=ek[:, 0:1],
                                           scalar=vetf[:, m:m + 1], in1=carD[:, m:m + 1],
                                           op0=OP.mult, op1=OP.add)
            nc.vector.scalar_tensor_tensor(out=ek[:, 1:SEG], in0=ek[:, 1:SEG],
                                           scalar=vetf[:, m:m + 1], in1=sden[:, 0:SEG - 1],
                                           op0=OP.mult, op1=OP.add)
            nc.vector.tensor_copy(out=carN[:, m:m + 1], in_=snum[:, SEG - 1:SEG])
            ins = nc.vector.tensor_copy(out=carD[:, m:m + 1], in_=sden[:, SEG - 1:SEG])
            p.inc(ins, 'V')
            p.wait(nc.scalar, 'V')
            nc.scalar.activation(out=ek[:, :], in_=ek[:, :], func=AF.Ln)
            ins = nc.scalar.activation(out=ek[:, :], in_=ek[:, :], func=AF.Exp, scale=-1.0)
            p.inc(ins, 'A')
            p.wait(nc.vector, 'A')
            nc.vector.tensor_tensor(out=vv[:, :], in0=vv[:, :], in1=ek[:, :], op=OP.mult)
            p.wait(nc.vector, 'A')
            ins = nc.vector.tensor_tensor(out=rwkv[:, m * SEG:(m + 1) * SEG], in0=vv[:, :],
                                          in1=rr[:, :], op=OP.mult)
            p.inc(ins, 'V')
            wkv_chain.append(p.cnt['V'])
        seg_wkv_last = wkv_chain[-1]
        if s + 1 < NSEG:
            emit_ln1(s + 1, vwait=mix_done)
        rwkv_done = p.cnt['V']
        for j in range(NTL):
            bO = pK[j % 2]
            p.wait(nc.tensor, 'V', rwkv_done)
            p.bkwait(4 + j % 2)
            for k in range(NTL):
                ins = nc.tensor.matmul(bO[:, :], wo[:, k * D + j * 128: k * D + (j + 1) * 128],
                                       rwkv[:, k * SEG:(k + 1) * SEG],
                                       start=(k == 0), stop=(k == NTL - 1))
            p.inc(ins, 'T')
            p.wait(nc.scalar, 'T')
            ins = nc.scalar.activation(out=attn[:, j * SEG:(j + 1) * SEG], in_=bO[:, :],
                                       func=AF.Identity, bias=bo[:, j:j + 1])
            p.inc(ins, 'A')
            p.bkset(4 + j % 2, 'A')

        p.wait(nc.scalar, 'A')
        emit_ln_stats(p, [attn[:, i * SEG:(i + 1) * SEG] for i in range(NTL)],
                      xb, sq, onesc, pS, pQ, SEG)
        emit_ln_rows(p, pS, pQ, rM, rA, rB, rT, SEG)
        emit_ln_bcast(p, rA, rB, pA, pB, onesr, SEG)
        p.wait(nc.vector, 'T')
        for i in range(NTL):
            asl = attn[:, i * SEG:(i + 1) * SEG]
            nc.vector.tensor_tensor(out=asl, in0=asl, in1=pA[:, :], op=OP.mult)
            nc.vector.tensor_tensor(out=asl, in0=asl, in1=pB[:, :], op=OP.add)
            ins = nc.vector.tensor_tensor(out=xsl(i), in0=xsl(i), in1=asl, op=OP.add)
        p.inc(ins, 'V')
        p.bkset(2, 'V')
        p.bkset(3, 'V')
        p.wait(nc.scalar, 'V')



def emit_l0_ffn(p, xs):
    nc = p.nc
    E, S, HD, ES = 16, 16, 256, 256
    CH = 512
    NCH = T // CH

    d_mu = p.din("l0f_mu2d", [D, ES])
    d_w1 = p.din("l0f_w1", [E * D, HD])
    d_w2 = p.din("l0f_w2", [E * HD, D])
    d_gb = p.din("l0f_gb", [128, 2 * NTL])
    d_b1 = p.din("l0f_b1p", [128, 2 * E])
    d_b2 = p.din("l0f_b2p", [128, 8 * E])
    d_sc = p.din("l0f_scale_col", [128, 1])
    d_id = p.din("c_ident", [128, 128], BF16)
    s_xn2 = p.dtmp("s_xn2", [D, T], BF16)
    s_dsp = p.dtmp("s_dsp", [ES, T], BF16)

    mu = p.sb("mu_sb", [128, NTL * ES], F32)
    muS = p.sb("muS_sb", [128, NTL * ES], BF16)
    gb = p.sb("gb_sb", [128, 2 * NTL], F32)
    b1s = p.sb("b1_sb", [128, 2 * E], F32)
    b2s = p.sb("b2_sb", [128, 8 * E], F32)
    scol = p.sb("scol", [128, 1], F32)
    nscol = p.sb("nscol", [128, 1], F32)
    identb = p.sb("identb", [128, 128], BF16)
    xn2 = p.sb("xn2_sb", [128, NTL * T], BF16)
    sqf = p.sb("fsq", [128, NTL * CH], BF16)
    expLs = [p.sb("expL0", [128, T], BF16), p.sb("expL1", [128, T], BF16)]
    dsp = p.sb("dsp", [128, 2 * T], BF16)
    lgt = p.sb("lgt", [128, CH], F32)
    slotin = p.sb("slotin", [128, NTL * ES], BF16)
    rnb = p.sb("rnb", [128, CH], F32)
    w1c = [p.sb("w1c0", [128, 2 * NTL * HD], BF16),
           p.sb("w1c1", [128, 2 * NTL * HD], BF16)]
    w2c = [p.sb("w2c0", [128, 2 * 2 * D], BF16),
           p.sb("w2c1", [128, 2 * 2 * D], BF16)]
    hh = p.sb("hh", [128, 2 * E + 16], BF16)
    soT = p.sb("soT", [128, NTL * ES], BF16)
    soE = p.sb("soE", [128, 2 * D], BF16)
    fout = p.sb("fout2", [128, NTL * CH], BF16)
    zt = p.sb("ztf", [128, 2], F32)
    rLn = p.sb("rLn", [1, T], F32)
    rZe = p.sb("rZe", [1, T], F32)
    fM = mu[0:1, 0:CH]
    fA = mu[0:1, CH:2 * CH]
    fB = mu[0:1, 2 * CH:3 * CH]
    fT = mu[0:1, 3 * CH:4 * CH]

    pS, pQ, pA, pB = p.bank[0], p.bank[1], p.bank[2], p.bank[3]

    ins = p.dma(nc.sync, 'L', out=mu[:, :], in_=d_mu.rearrange("(k q) s -> q k s", q=128))
    for dst, dd in ((gb, d_gb), (b1s, d_b1), (b2s, d_b2), (scol, d_sc), (identb, d_id)):
        ins = p.dma(nc.sync, 'L', out=dst[:, :], in_=dd)
    ld_const = p.cnt['L']
    def load_pair(pr):
        ins = p.dma(nc.gpsimd, 'W', 
            out=w1c[pr % 2][:, :],
            in_=d_w1[pr * 2 * D:(pr + 1) * 2 * D, :].rearrange("(g q) h -> q g h", q=128))
        ins = p.dma(nc.gpsimd, 'W', 
            out=w2c[pr % 2][:, :],
            in_=d_w2[pr * 2 * HD:(pr + 1) * 2 * HD, :].rearrange("(g q) d2 -> q g d2", q=128))
    load_pair(0)
    load_pair(1)
    pair_ld = [p.cnt['W'] - 32, p.cnt['W']] + [0] * 6

    p.wait(nc.scalar, 'L', ld_const)
    ins = nc.scalar.activation(out=nscol[:, :], in_=scol[:, :], func=AF.Identity, scale=-1.0)
    p.inc(ins, 'A')

    p.wait(nc.scalar, 'V')
    for i in range(NTL):
        ins = nc.scalar.activation(out=sqf[:, i * ES:(i + 1) * ES], in_=mu[:, i * ES:(i + 1) * ES],
                                   func=AF.Square)
    p.inc(ins, 'A')
    p.wait(nc.tensor, 'A')
    p.wait(nc.tensor, 'L', p.const_L)
    for i in range(NTL):
        ins = nc.tensor.matmul(pQ[0:1, 0:ES], p.onesc[:, 0:1], sqf[:, i * ES:(i + 1) * ES],
                               start=(i == 0), stop=(i == NTL - 1))
    p.inc(ins, 'T')
    p.wait(nc.scalar, 'T')
    ins = nc.scalar.activation(out=rZe[0:1, 0:ES], in_=pQ[0:1, 0:ES], func=AF.Sqrt)
    p.inc(ins, 'A')
    p.wait(nc.vector, 'A')
    ins = nc.vector.tensor_scalar(out=rZe[0:1, 0:ES], in0=rZe[0:1, 0:ES], scalar1=1e-12, scalar2=None,
                                  op0=OP.max, op1=OP.bypass)
    p.inc(ins, 'V')
    p.wait(nc.vector, 'V')
    ins = nc.vector.reciprocal(out=rZe[0:1, 0:ES], in_=rZe[0:1, 0:ES])
    p.inc(ins, 'V')
    p.wait(nc.tensor, 'V')
    ins = nc.tensor.matmul(pA[:, 0:ES], p.onesr[0:1, :], rZe[0:1, 0:ES], start=True, stop=True)
    p.inc(ins, 'T')
    p.wait(nc.vector, 'T')
    for i in range(NTL):
        ins = nc.vector.tensor_tensor(out=muS[:, i * ES:(i + 1) * ES], in0=mu[:, i * ES:(i + 1) * ES],
                                      in1=pA[:, 0:ES], op=OP.mult)
    p.inc(ins, 'V')
    muS_done = p.cnt['V']

    for c in range(NCH):
        def xsl(i):
            return xs[:, i * T + c * CH: i * T + c * CH + CH]
        p.wait(nc.scalar, 'V')
        emit_ln_stats(p, [xsl(i) for i in range(NTL)], fout, sqf, p.onesc, pS, pQ, CH)
        emit_ln_rows(p, pS, pQ, fM, fA, fB, fT, CH, double_ln=True)
        emit_ln_bcast(p, fA, fB, pA, pB, p.onesr, CH)
        p.wait(nc.vector, 'T')
        for i in range(NTL):
            nc.vector.tensor_tensor(out=lgt[:, :], in0=xsl(i), in1=pA[:, :], op=OP.mult)
            nc.vector.tensor_tensor(out=lgt[:, :], in0=lgt[:, :], in1=pB[:, :], op=OP.add)
            ins = nc.vector.tensor_scalar(out=xn2[:, i * T + c * CH: i * T + c * CH + CH],
                                          in0=lgt[:, :], scalar1=gb[:, i:i + 1],
                                          scalar2=gb[:, NTL + i:NTL + i + 1],
                                          op0=OP.mult, op1=OP.add)
        p.inc(ins, 'V')
        p.bkset(2, 'V')
        p.bkset(3, 'V')
        p.wait(nc.scalar, 'V')
        for i in range(NTL):
            ins = nc.scalar.activation(out=sqf[:, i * CH:(i + 1) * CH],
                                       in_=xn2[:, i * T + c * CH: i * T + c * CH + CH],
                                       func=AF.Square)
        p.inc(ins, 'A')
        p.wait(nc.tensor, 'A')
        for i in range(NTL):
            ins = nc.tensor.matmul(pQ[0:1, :CH], p.onesc[:, 0:1], sqf[:, i * CH:(i + 1) * CH],
                                   start=(i == 0), stop=(i == NTL - 1))
        p.inc(ins, 'T')
        p.wait(nc.scalar, 'T')
        ins = nc.scalar.activation(out=fT[0:1, :], in_=pQ[0:1, :CH], func=AF.Sqrt)
        p.inc(ins, 'A')
        p.wait(nc.vector, 'A')
        ins = nc.vector.tensor_scalar(out=fT[0:1, :], in0=fT[0:1, :], scalar1=1e-12, scalar2=None,
                                      op0=OP.max, op1=OP.bypass)
        p.inc(ins, 'V')
        p.wait(nc.vector, 'V')
        ins = nc.vector.reciprocal(out=fT[0:1, :], in_=fT[0:1, :])
        p.inc(ins, 'V')
        p.wait(nc.vector, 'V')
        ins = nc.vector.tensor_scalar(out=rLn[0:1, c * CH:(c + 1) * CH], in0=fT[0:1, :],
                                      scalar1=scol[0:1, 0:1], scalar2=None,
                                      op0=OP.mult, op1=OP.bypass)
        p.inc(ins, 'V')

    p.dump("xn2", xn2[:, :])
    p.dump("rLn", rLn[0:1, :])
    p.dump("muS", muS[:, :])
    p.wait(nc.sync, 'V')
    for k in range(NTL):
        ins = p.dma(nc.sync, 'L', gate=(k == 0), out=s_xn2[k * 128:(k + 1) * 128, :],
                    in_=xn2[:, k * T:(k + 1) * T])
    xn2_store = p.cnt['L']

    for c in range(NCH):
        p.wait(nc.tensor, 'V')
        ins = nc.tensor.matmul(pA[:, :CH], p.onesr[0:1, :], rLn[0:1, c * CH:(c + 1) * CH],
                               start=True, stop=True)
        p.inc(ins, 'T')
        p.wait(nc.vector, 'T')
        ins = nc.vector.tensor_copy(out=rnb[:, :], in_=pA[:, :CH])
        p.inc(ins, 'V')
        for m in range(2):
            pL = p.bank[4 + m]
            p.bkwait(4 + m)
            for k in range(NTL):
                ins = nc.tensor.matmul(pL[:, :CH], muS[:, k * ES + m * 128: k * ES + (m + 1) * 128],
                                       xn2[:, k * T + c * CH: k * T + c * CH + CH],
                                       start=(k == 0), stop=(k == NTL - 1))
            p.inc(ins, 'T')
            p.wait(nc.vector, 'T')
            p.wait(nc.vector, 'A')
            ins = nc.vector.tensor_tensor(out=lgt[:, :], in0=pL[:, :CH], in1=rnb[:, :], op=OP.mult)
            p.inc(ins, 'V')
            p.bkset(4 + m, 'V')
            p.wait(nc.scalar, 'V')
            ins = nc.scalar.activation(out=expLs[m][:, c * CH: c * CH + CH],
                                       in_=lgt[:, :], func=AF.Exp, bias=nscol[:, 0:1])
            p.inc(ins, 'A')
        p.wait(nc.tensor, 'A')
        for m in range(2):
            ins = nc.tensor.matmul(pB[0:1, :CH], p.onesc[:, 0:1],
                                   expLs[m][:, c * CH: c * CH + CH],
                                   start=(m == 0), stop=(m == 1))
        p.inc(ins, 'T')
        p.wait(nc.vector, 'T')
        ins = nc.vector.reciprocal(out=rZe[0:1, c * CH:(c + 1) * CH], in_=pB[0:1, :CH])
        p.inc(ins, 'V')

    p.dump("expL0", expLs[0][:, :])
    p.dump("expL1", expLs[1][:, :])
    for m in range(2):
        ins = nc.vector.tensor_reduce(out=zt[:, m:m + 1], in_=expLs[m][:, :],
                                      axis=mybir.AxisListType.X, op=OP.add)
        p.inc(ins, 'V')
        p.wait(nc.vector, 'V')
        ins = nc.vector.reciprocal(out=zt[:, m:m + 1], in_=zt[:, m:m + 1])
        p.inc(ins, 'V')
        p.wait(nc.vector, 'V')
        ins = nc.vector.tensor_scalar(out=dsp[:, m * T:(m + 1) * T], in0=expLs[m][:, :],
                                      scalar1=zt[:, m:m + 1], scalar2=None,
                                      op0=OP.mult, op1=OP.bypass)
    p.inc(ins, 'V')
    dsp_done = p.cnt['V']
    for c in range(NCH):
        p.wait(nc.tensor, 'V')
        ins = nc.tensor.matmul(pA[:, :CH], p.onesr[0:1, :], rZe[0:1, c * CH:(c + 1) * CH],
                               start=True, stop=True)
        p.inc(ins, 'T')
        p.wait(nc.vector, 'T')
        for m in range(2):
            ins = nc.vector.tensor_tensor(out=expLs[m][:, c * CH: c * CH + CH],
                                          in0=expLs[m][:, c * CH: c * CH + CH],
                                          in1=pA[:, :CH], op=OP.mult)
        p.inc(ins, 'V')

    p.dump("dsp", dsp[:, :])
    p.dump("cmb0", expLs[0][:, :])
    p.dump("cmb1", expLs[1][:, :])
    p.wait(nc.sync, 'V', dsp_done)
    for m in range(2):
        ins = p.dma(nc.sync, 'L', out=s_dsp[m * 128:(m + 1) * 128, :], in_=dsp[:, m * T:(m + 1) * T])
    dsp_store = p.cnt['L']
    p.wait(nc.sync, 'T')
    p.wait(nc.sync, 'L', xn2_store)
    xtok = xn2
    for tt in range(16):
        ins = p.dma(nc.sync, 'L', gate=(tt == 0), out=xtok[:, tt * D:(tt + 1) * D],
                    in_=s_xn2[:, tt * 128:(tt + 1) * 128], transpose=True)
    p.wait(nc.sync, 'L', dsp_store)
    dspT = dsp
    for tt in range(16):
        ins = p.dma(nc.sync, 'L', gate=(tt == 0), out=dspT[:, tt * ES:(tt + 1) * ES],
                    in_=s_dsp[:, tt * 128:(tt + 1) * 128], transpose=True)
    tok_ld = p.cnt['L']

    p.wait(nc.tensor, 'L', tok_ld)
    for md in range(NTL):
        pSI = p.bank[4 + md % 2]
        p.bkwait(4 + md % 2)
        for tt in range(16):
            ins = nc.tensor.matmul(pSI[:, 0:ES], xtok[:, tt * D + md * 128: tt * D + (md + 1) * 128],
                                   dspT[:, tt * ES:(tt + 1) * ES],
                                   start=(tt == 0), stop=(tt == 15))
        p.inc(ins, 'T')
        p.wait(nc.scalar, 'T')
        ins = nc.scalar.activation(out=slotin[:, md * ES:(md + 1) * ES], in_=pSI[:, 0:ES],
                                   func=AF.Identity)
        p.inc(ins, 'A')
        p.bkset(4 + md % 2, 'A')

    p.wait(nc.gpsimd, 'A')
    p.dump("slotin", slotin[:, :])
    p.dump("xtok", xtok[:, :])
    p.dump("dspT", dspT[:, :])
    slot_ready = p.cnt['A']
    for e in range(E):
        pr, e2 = e // 2, e % 2
        pX = p.bank[6 + e % 2]
        p.wait(nc.tensor, 'A', slot_ready)
        p.wait(nc.tensor, 'W', pair_ld[pr])
        p.bkwait(6 + e % 2)
        wb = w1c[pr % 2]
        for mh in range(2):
            for k in range(NTL):
                ins = nc.tensor.matmul(pX[:, mh * 16:(mh + 1) * 16],
                                       wb[:, e2 * 2048 + k * HD + mh * 128: e2 * 2048 + k * HD + (mh + 1) * 128],
                                       slotin[:, k * ES + e * 16: k * ES + e * 16 + 16],
                                       start=(k == 0), stop=(k == NTL - 1))
        p.inc(ins, 'T')
        p.wait(nc.scalar, 'T')
        for mh in range(2):
            ins = nc.scalar.activation(out=hh[:, mh * 16:(mh + 1) * 16], in_=pX[:, mh * 16:(mh + 1) * 16],
                                       func=AF.Gelu, bias=b1s[:, mh * 16 + e:mh * 16 + e + 1])
        p.inc(ins, 'A')
        wb2 = w2c[pr % 2]
        p.wait(nc.tensor, 'A')
        for md in range(NTL):
            for kh in range(2):
                ins = nc.tensor.matmul(pX[:, 32 + md * 16: 48 + md * 16],
                                       wb2[:, e2 * 2048 + kh * D + md * 128: e2 * 2048 + kh * D + (md + 1) * 128],
                                       hh[:, kh * 16:(kh + 1) * 16],
                                       start=(kh == 0), stop=(kh == 1))
        p.inc(ins, 'T')
        p.wait(nc.scalar, 'T')
        for md in range(NTL):
            ins = nc.scalar.activation(out=soT[:, md * ES + e * 16: md * ES + e * 16 + 16],
                                       in_=pX[:, 32 + md * 16: 48 + md * 16],
                                       func=AF.Identity, bias=b2s[:, md * 16 + e:md * 16 + e + 1])
        p.inc(ins, 'A')
        p.bkset(6 + e % 2, 'A')
        if e2 == 1 and pr + 2 <= 7:
            p.wait(nc.gpsimd, 'T')
            load_pair(pr + 2)
            pair_ld[pr + 2] = p.cnt['W']

    p.wait(nc.gpsimd, 'A')
    p.dump("soT", soT[:, :])
    soT_done = p.cnt['A']
    pTb = p.bank[4].bitcast(BF16)
    p.wait(nc.tensor, 'A', soT_done)
    for mes in range(2):
        for md in range(NTL):
            ins = nc.tensor.transpose(pTb[:, 0:128], soT[:, md * ES + mes * 128: md * ES + (mes + 1) * 128],
                                      identb[:, :])
            p.inc(ins, 'T')
            p.wait(nc.scalar, 'T')
            ins = nc.scalar.activation(out=soE[:, mes * D + md * 128: mes * D + (md + 1) * 128],
                                       in_=pTb[:, 0:128], func=AF.Identity)
            p.inc(ins, 'A')
            p.wait(nc.tensor, 'A')

    p.wait(nc.tensor, 'V')
    for c in range(NCH):
        for md in range(NTL):
            pF = p.bank[4 + md % 2]
            p.bkwait(4 + md % 2)
            for kes in range(2):
                ins = nc.tensor.matmul(pF[:, :CH],
                                       soE[:, kes * D + md * 128: kes * D + (md + 1) * 128],
                                       expLs[kes][:, c * CH: c * CH + CH],
                                       start=(kes == 0), stop=(kes == 1))
            p.inc(ins, 'T')
            p.wait(nc.scalar, 'T')
            ins = nc.scalar.activation(out=fout[:, md * CH:(md + 1) * CH], in_=pF[:, :CH],
                                       func=AF.Identity)
            p.inc(ins, 'A')
            p.bkset(4 + md % 2, 'A')
        p.wait(nc.scalar, 'A')
        p.wait(nc.scalar, 'V')
        for i in range(NTL):
            ins = nc.scalar.activation(out=sqf[:, i * CH:(i + 1) * CH], in_=fout[:, i * CH:(i + 1) * CH],
                                       func=AF.Square)
        p.inc(ins, 'A')
        p.wait(nc.tensor, 'A')
        for i in range(NTL):
            ins = nc.tensor.matmul(pS[0:1, :CH], p.onesc[:, 0:1], fout[:, i * CH:(i + 1) * CH],
                                   start=(i == 0), stop=(i == NTL - 1))
        p.inc(ins, 'T')
        for i in range(NTL):
            ins = nc.tensor.matmul(pQ[0:1, :CH], p.onesc[:, 0:1], sqf[:, i * CH:(i + 1) * CH],
                                   start=(i == 0), stop=(i == NTL - 1))
        p.inc(ins, 'T')
        emit_ln_rows(p, pS, pQ, fM, fA, fB, fT, CH)
        emit_ln_bcast(p, fA, fB, pA, pB, p.onesr, CH)
        p.wait(nc.vector, 'T')
        for i in range(NTL):
            fsl = fout[:, i * CH:(i + 1) * CH]
            xslc = xs[:, i * T + c * CH: i * T + c * CH + CH]
            nc.vector.tensor_tensor(out=fsl, in0=fsl, in1=pA[:, :CH], op=OP.mult)
            nc.vector.tensor_tensor(out=fsl, in0=fsl, in1=pB[:, :CH], op=OP.add)
            ins = nc.vector.tensor_tensor(out=xslc, in0=xslc, in1=fsl, op=OP.add)
        p.inc(ins, 'V')
        p.bkset(2, 'V')
        p.bkset(3, 'V')
        p.wait(nc.scalar, 'V')



def emit_l1_attn(p, xs, last):
    nc = p.nc
    NH, HD2, NPAIR, NB = 16, 64, 8, 8
    LH, NL, LD = 8, 64, 512
    CH = 512
    NCH = T // CH

    d_wq = p.din("l1a_wq", [D, D])
    d_wk = p.din("l1a_wk", [D, D])
    d_wv = p.din("l1a_wv", [D, D])
    d_wg = p.din("l1a_wg3", [3 * D, LD])
    d_row = p.din("l1a_rout", [LD, D])
    d_wout = p.din("l1a_wout", [D, D])
    d_gp = p.din("l1a_gatep", [128, 2 * NH])
    d_a2 = p.din("l1a_a2", [2, 128], BF16)
    d_mask = p.din("l1a_mask", [128, 1024], BF16)
    d_lmask = p.din("l1a_lmask", [64, T], BF16)
    s_x = p.dtmp("s_x_l1", [D, T], F32)
    s_comb = p.dtmp("s_comb", [D, T], BF16)

    xsb = xs.bitcast(BF16)
    xn = xsb[:, 0:NTL * T]
    GQT = xsb[:, NTL * T:NTL * T + 4 * T]
    remT = xsb[:, NTL * T + 4 * T:NTL * T + 8 * T]

    xchunk = p.sb("l1_xchunk", [128, NTL * CH], F32)
    xbc = p.sb("l1_xbc", [128, NTL * CH], BF16)
    sq = p.sb("l1_sq", [128, NTL * CH], BF16)
    wg = p.sb("l1_wg", [128, NTL * LD], BF16)
    wro = p.sb("l1_wro", [128, 4 * D], BF16)
    wpr = [p.sb("l1_wpr0", [128, 3 * D], BF16),
           p.sb("l1_wpr1", [128, 3 * D], BF16)]
    Kp = [p.sb("l1_Kp0", [128, T], BF16), p.sb("l1_Kp1", [128, T], BF16)]
    Qp = [p.sb("l1_Qp0", [128, T], BF16), p.sb("l1_Qp1", [128, T], BF16)]
    Vp = [p.sb("l1_Vp0", [128, 2 * T], BF16), p.sb("l1_Vp1", [128, 2 * T], BF16)]
    locT = p.sb("l1_locT", [128, T], BF16)
    rem2p = p.sb("l1_rem2p", [128, T], BF16)
    combp = p.sb("l1_combp", [128, T], BF16)
    rzs = p.sb("l1_rzs", [64, 512], F32)
    poolT = p.sb("l1_poolT", [128, NTL * NL], F32)
    poolB = p.sb("l1_poolB", [128, NTL * NL], BF16)
    GKT = p.sb("l1_GKT", [128, 4 * NL], BF16)
    GVo = p.sb("l1_GVo", [64, 8 * 128], BF16)
    explsc = p.sb("l1_explsc", [64, T], BF16)
    lrz = p.sb("l1_lrz", [64, CH], F32)

    gpv = p.sb("l1_gpv", [128, 2 * NH], BF16)
    a2b = p.sb("l1_a2b", [2, 128], BF16)
    maskb = p.sb("l1_maskb", [128, 1024], BF16)
    lmaskb = p.sb("l1_lmaskb", [64, T], BF16)
    pstg = [sq[:, 0:1024], sq[:, 1024:2048]]
    glp = xbc[0:2, 0:T]
    grp = xbc[0:2, T:2 * T]
    woutb = [Kp[0], Kp[1], Qp[0], Qp[1]]
    rM = p.sb("l1_rM", [1, CH], F32)
    rA = p.sb("l1_rA", [1, CH], F32)
    rB = p.sb("l1_rB", [1, CH], F32)
    rT = p.sb("l1_rT", [1, CH], F32)

    pS, pQ, pA, pB = p.bank[0], p.bank[1], p.bank[2], p.bank[3]
    bS1, bS2 = p.bank[4], p.bank[5]
    bPV = [p.bank[6], p.bank[7]]

    for dst, dd in ((a2b, d_a2), (maskb, d_mask), (lmaskb, d_lmask)):
        ins = p.dma(nc.sync, 'L', out=dst[:, :], in_=dd)
    ins = p.dma(nc.gpsimd, 'W', out=gpv[:, :], in_=d_gp)
    ins = p.dma(nc.gpsimd, 'W', out=wro[:, :], in_=d_row.rearrange("(k q) o -> q k o", q=128))

    p.wait(nc.gpsimd, 'L')
    p.wait(nc.gpsimd, 'V')
    p.wait(nc.gpsimd, 'A')
    for i in range(NTL):
        ins = p.dma(nc.gpsimd, 'W', gate=(i == 0), out=s_x[i * 128:(i + 1) * 128, :],
                    in_=xs[:, i * T:(i + 1) * T])
    spill_done = p.cnt['W']

    for c in range(NCH):
        p.wait(nc.sync, 'W', spill_done)
        if c > 0:
            p.wait(nc.sync, 'V')
        ins = p.dma(nc.sync, 'L', out=xchunk[:, :],
                                in_=s_x[:, c * CH:(c + 1) * CH].rearrange("(i q) t -> q i t", q=128))
        p.wait(nc.scalar, 'L')
        p.wait(nc.scalar, 'V')
        emit_ln_stats(p, [xchunk[:, i * CH:(i + 1) * CH] for i in range(NTL)], xbc, sq,
                      p.onesc, pS, pQ, CH)
        emit_ln_rows(p, pS, pQ, rM, rA, rB, rT, CH)
        emit_ln_bcast(p, rA, rB, pA, pB, p.onesr, CH)
        p.wait(nc.vector, 'T')
        if c == 0:
            p.wait(nc.vector, 'W', spill_done)
        for i in range(NTL):
            nc.vector.tensor_tensor(out=xchunk[:, i * CH:(i + 1) * CH],
                                    in0=xchunk[:, i * CH:(i + 1) * CH], in1=pA[:, :CH], op=OP.mult)
            ins = nc.vector.tensor_tensor(out=xn[:, i * T + c * CH: i * T + c * CH + CH],
                                          in0=xchunk[:, i * CH:(i + 1) * CH], in1=pB[:, :CH],
                                          op=OP.add)
        p.inc(ins, 'V')
        p.bkset(2, 'V')
        p.bkset(3, 'V')

    p.wait(nc.vector, 'V')
    for i in range(NTL):
        ins = nc.vector.tensor_reduce(out=poolT[:, i * NL:(i + 1) * NL],
                                      in_=xn[:, i * T:(i + 1) * T].rearrange("q (l c) -> q l c", c=32),
                                      axis=mybir.AxisListType.X, op=OP.add)
    p.inc(ins, 'V')
    p.wait(nc.vector, 'V')
    for i in range(NTL):
        ins = nc.vector.tensor_scalar(out=poolB[:, i * NL:(i + 1) * NL],
                                      in0=poolT[:, i * NL:(i + 1) * NL], scalar1=1.0 / 32,
                                      scalar2=None, op0=OP.mult, op1=OP.bypass)
    p.inc(ins, 'V')
    pool_done = p.cnt['V']
    ins = p.dma(nc.gpsimd, 'W', out=wg[:, :], in_=d_wg[0:D, :].rearrange("(k q) o -> q k o", q=128))
    p.wait(nc.tensor, 'W')
    p.wait(nc.tensor, 'V', pool_done)
    for mt in range(4):
        p.bkwait(2)
        for k in range(NTL):
            ins = nc.tensor.matmul(pA[:, 0:NL], wg[:, k * LD + mt * 128: k * LD + (mt + 1) * 128],
                                   poolB[:, k * NL:(k + 1) * NL], start=(k == 0), stop=(k == NTL - 1))
        p.inc(ins, 'T')
        p.wait(nc.scalar, 'T')
        ins = nc.scalar.activation(out=GKT[:, mt * NL:(mt + 1) * NL], in_=pA[:, 0:NL],
                                   func=AF.Identity)
        p.inc(ins, 'A')
        p.bkset(2, 'A')
    p.wait(nc.gpsimd, 'T')
    ins = p.dma(nc.gpsimd, 'W', out=wg[:, :], in_=d_wg[D:2 * D, :].rearrange("(k q) o -> q k o", q=128))
    p.wait(nc.vector, 'A')
    ins = nc.vector.memset(GVo[:, :], 1.0)
    p.inc(ins, 'V')
    p.wait(nc.tensor, 'W')
    p.wait(nc.tensor, 'V')
    p.bkwait(3)
    for k in range(NTL):
        ins = nc.tensor.matmul(pB[0:NL, 0:LD], poolB[:, k * NL:(k + 1) * NL],
                               wg[:, k * LD:(k + 1) * LD], start=(k == 0), stop=(k == NTL - 1))
    p.inc(ins, 'T')
    p.wait(nc.vector, 'T')
    ins = nc.vector.tensor_copy(
        out=GVo[:, :].rearrange("q (l c) -> q l c", c=128)[:, :, 0:64],
        in_=pB[0:NL, 0:LD].rearrange("q (l c) -> q l c", c=64))
    p.inc(ins, 'V')
    p.bkset(3, 'V')
    p.wait(nc.gpsimd, 'T')
    ins = p.dma(nc.gpsimd, 'W', out=wg[:, :], in_=d_wg[2 * D:3 * D, :].rearrange("(k q) o -> q k o", q=128))
    p.wait(nc.tensor, 'W')
    for mt in range(4):
        for c in range(NCH):
            bb = pA if (mt * NCH + c) % 2 == 0 else pB
            p.bkwait(2 + (mt * NCH + c) % 2)
            for k in range(NTL):
                ins = nc.tensor.matmul(bb[:, :CH], wg[:, k * LD + mt * 128: k * LD + (mt + 1) * 128],
                                       xn[:, k * T + c * CH: k * T + c * CH + CH],
                                       start=(k == 0), stop=(k == NTL - 1))
            p.inc(ins, 'T')
            p.wait(nc.scalar, 'T')
            ins = nc.scalar.activation(out=GQT[:, mt * T + c * CH: mt * T + c * CH + CH],
                                       in_=bb[:, :CH], func=AF.Identity)
            p.inc(ins, 'A')
            p.bkset(2 + (mt * NCH + c) % 2, 'A')
    for lh in range(LH):
        mt, half = lh // 2, (lh % 2) * 64
        for c in range(NCH):
            bb = pA if c % 2 == 0 else pB
            p.bkwait(2 + c % 2)
            ins = nc.tensor.matmul(bb[0:64, :CH], GKT[half:half + 64, mt * NL:(mt + 1) * NL],
                                   GQT[half:half + 64, mt * T + c * CH: mt * T + c * CH + CH],
                                   start=True, stop=True)
            p.inc(ins, 'T')
            p.wait(nc.scalar, 'T')
            ins = nc.scalar.activation(out=explsc[:, c * CH:(c + 1) * CH], in_=bb[0:64, :CH],
                                       func=AF.Exp, scale=0.125)
            p.inc(ins, 'A')
            p.bkset(2 + c % 2, 'A')
        p.wait(nc.vector, 'A')
        ins = nc.vector.tensor_tensor(out=explsc[:, :], in0=explsc[:, :], in1=lmaskb[:, :],
                                      op=OP.mult)
        p.inc(ins, 'V')
        for c in range(NCH):
            bb = bS1 if c % 2 == 0 else bS2
            p.bkwait(4 + c % 2)
            p.wait(nc.tensor, 'V')
            ins = nc.tensor.matmul(bb[:, :CH], GVo[:, lh * 128:(lh + 1) * 128],
                                   explsc[:, c * CH:(c + 1) * CH], start=True, stop=True)
            p.inc(ins, 'T')
            p.wait(nc.scalar, 'T')
            ins = nc.scalar.activation(out=lrz[:, :], in_=bb[64:128, :CH], func=AF.Ln)
            p.inc(ins, 'A')
            p.wait(nc.scalar, 'A')
            ins = nc.scalar.activation(out=lrz[:, :], in_=lrz[:, :], func=AF.Exp, scale=-1.0)
            p.inc(ins, 'A')
            p.wait(nc.vector, 'A')
            ins = nc.vector.tensor_tensor(
                out=remT[half:half + 64, mt * T + c * CH: mt * T + c * CH + CH],
                in0=bb[0:64, :CH], in1=lrz[:, :], op=OP.mult)
            p.inc(ins, 'V')
            p.bkset(4 + c % 2, 'V')

    def load_pair(pr):
        buf = wpr[pr % 2]
        for wi, dd in enumerate((d_wq, d_wk, d_wv)):
            ins = p.dma(nc.gpsimd, 'W', gate=(wi == 0),
                out=buf[:, wi * D:(wi + 1) * D],
                in_=dd[:, pr * 128:(pr + 1) * 128].rearrange("(k q) c -> q k c", q=128))
    load_pair(0)
    pair_ld = [p.cnt['W']] + [0] * 7
    p.combw = 0

    for pr in range(NPAIR):
        if pr + 1 < NPAIR:
            p.wait(nc.gpsimd, 'T')
            load_pair(pr + 1)
            pair_ld[pr + 1] = p.cnt['W']
        buf = wpr[pr % 2]
        kp, qp, vp = Kp[pr % 2], Qp[pr % 2], Vp[pr % 2]
        p.wait(nc.tensor, 'W', pair_ld[pr])
        p.wait(nc.scalar, 'V')
        for which, dstt in ((0, qp), (1, kp)):
            for c in range(NCH):
                bb = pA if c % 2 == 0 else pB
                p.bkwait(2 + c % 2)
                for k in range(NTL):
                    ins = nc.tensor.matmul(bb[:, :CH],
                                           buf[:, which * D + k * 128: which * D + (k + 1) * 128],
                                           xn[:, k * T + c * CH: k * T + c * CH + CH],
                                           start=(k == 0), stop=(k == NTL - 1))
                p.inc(ins, 'T')
                p.wait(nc.vector, 'T')
                ins = nc.vector.tensor_copy(out=dstt[:, c * CH:(c + 1) * CH], in_=bb[:, :CH])
                p.inc(ins, 'V')
                p.bkset(2 + c % 2, 'V')
        p.wait(nc.vector, 'T')
        ins = nc.vector.memset(vp[:, :], 1.0)
        p.inc(ins, 'V')
        for tt in range(16):
            bb = pA if tt % 2 == 0 else pB
            p.bkwait(2 + tt % 2)
            p.wait(nc.tensor, 'V')
            for k in range(NTL):
                ins = nc.tensor.matmul(bb[:, 0:128], xn[:, k * T + tt * 128: k * T + (tt + 1) * 128],
                                       buf[:, 2 * D + k * 128: 2 * D + (k + 1) * 128],
                                       start=(k == 0), stop=(k == NTL - 1))
            p.inc(ins, 'T')
            p.wait(nc.vector, 'T')
            ins = nc.vector.tensor_copy(
                out=vp[:, tt * 256:tt * 256 + 192].rearrange("q (g c) -> q g c", c=64)[:, 0::2, :],
                in_=bb[:, 0:128].rearrange("q (g c) -> q g c", c=64))
            p.inc(ins, 'V')
            p.bkset(2 + tt % 2, 'V')
        qev = p.cnt['V']
        for gi, gdst in ((0, glp), (1, grp)):
            for cc in range(4):
                p.wait(nc.tensor, 'V', qev)
                p.bkwait(4)
                ins = nc.tensor.matmul(bS1[0:2, 0:512],
                                       gpv[:, gi * NH + pr * 2: gi * NH + pr * 2 + 2],
                                       qp[:, cc * 512:(cc + 1) * 512], start=True, stop=True)
                p.inc(ins, 'T')
                p.wait(nc.scalar, 'T')
                ins = nc.scalar.activation(out=gdst[:, cc * 512:(cc + 1) * 512],
                                           in_=bS1[0:2, 0:512], func=AF.Sigmoid)
                p.inc(ins, 'A')
                p.bkset(4, 'A')
        for c in range(NCH):
            p.bkwait(5)
            for kl in range(4):
                ins = nc.tensor.matmul(bS2[:, :CH], wro[:, kl * D + pr * 128: kl * D + (pr + 1) * 128],
                                       remT[:, kl * T + c * CH: kl * T + c * CH + CH],
                                       start=(kl == 0), stop=(kl == 3))
            p.inc(ins, 'T')
            p.wait(nc.scalar, 'T')
            ins = nc.scalar.activation(out=rem2p[:, c * CH:(c + 1) * CH], in_=bS2[:, :CH],
                                       func=AF.Identity)
            p.inc(ins, 'A')
            p.bkset(5, 'A')
        qkv_done = p.cnt['V']
        SLOT_BANKS = [(4, 5), (6, 7), (0, 1)]
        hbs = [(h2, b) for h2 in range(2) for b in range(NB)]
        NP3 = 3
        st_scT = {}
        st_expA = {}
        st_maskV = {}
        st_pvT = {}
        st_rzA = {}

        def lstage_sc(i):
            h2, b = hbs[i]
            hoff = h2 * 64
            bx, by = SLOT_BANKS[i % NP3]
            jts = list(range(2, 4)) if b == 0 else list(range(4))
            p.wait(nc.tensor, 'V', qkv_done)
            p.bkwait(bx, by)
            for jt in jts:
                gt = (b - 1) * 256 + jt * 128
                bb = p.bank[bx] if jt < 2 else p.bank[by]
                ins = nc.tensor.matmul(bb[:, (jt % 2) * 256:(jt % 2) * 256 + 256],
                                       kp[hoff:hoff + 64, gt:gt + 128],
                                       qp[hoff:hoff + 64, b * 256:(b + 1) * 256],
                                       start=True, stop=True)
            p.inc(ins, 'T')
            st_scT[i] = p.cnt['T']

        def lstage_exp(i):
            h2, b = hbs[i]
            bx, by = SLOT_BANKS[i % NP3]
            stg = pstg[i % 2]
            jts = list(range(2, 4)) if b == 0 else list(range(4))
            p.wait(nc.scalar, 'T', st_scT[i])
            if i >= 2:
                p.wait(nc.scalar, 'V', st_maskV.get(i - 2, 0))
            if b != 0:
                ins = nc.scalar.activation(out=stg[:, 0:512], in_=p.bank[bx][:, 0:512],
                                           func=AF.Exp, scale=0.125)
            ins = nc.scalar.activation(out=stg[:, 512:1024], in_=p.bank[by][:, 0:512],
                                       func=AF.Exp, scale=0.125)
            p.inc(ins, 'A')
            st_expA[i] = p.cnt['A']
            p.bkset(bx, 'A')
            p.bkset(by, 'A')

        def lstage_maskpv(i):
            h2, b = hbs[i]
            bx, by = SLOT_BANKS[i % NP3]
            bv = p.bank[bx]
            stg = pstg[i % 2]
            jts = list(range(2, 4)) if b == 0 else list(range(4))
            p.wait(nc.vector, 'A', st_expA[i])
            if b == 0:
                nc.vector.memset(stg[:, 0:512], 0.0)
            ins = nc.vector.tensor_tensor(out=stg[:, :], in0=stg[:, :], in1=maskb[:, :],
                                          op=OP.mult)
            p.inc(ins, 'V')
            st_maskV[i] = p.cnt['V']
            p.wait(nc.tensor, 'V', st_maskV[i])
            for jt in jts:
                gtt = (b - 1) * 2 + jt
                ins = nc.tensor.matmul(bv[:, 0:256],
                                       vp[:, gtt * 256 + h2 * 128: gtt * 256 + h2 * 128 + 128],
                                       stg[:, jt * 256:(jt + 1) * 256],
                                       start=(jt == jts[0]), stop=(jt == jts[-1]))
            p.inc(ins, 'T')
            st_pvT[i] = p.cnt['T']

        def lstage_rzout(i):
            h2, b = hbs[i]
            hoff = h2 * 64
            bx, by = SLOT_BANKS[i % NP3]
            bv = p.bank[bx]
            rz = rzs[:, (i % 2) * 256:(i % 2) * 256 + 256]
            p.wait(nc.scalar, 'T', st_pvT[i])
            nc.scalar.activation(out=rz, in_=bv[64:128, 0:256], func=AF.Ln)
            ins = nc.scalar.activation(out=rz, in_=rz, func=AF.Exp, scale=-1.0)
            p.inc(ins, 'A')
            st_rzA[i] = p.cnt['A']
            p.wait(nc.vector, 'A', st_rzA[i])
            ins = nc.vector.tensor_tensor(out=locT[hoff:hoff + 64, b * 256:(b + 1) * 256],
                                          in0=bv[0:64, 0:256], in1=rz, op=OP.mult)
            p.inc(ins, 'V')
            p.bkset(bx, 'V')

        stages = [lstage_sc, lstage_exp, lstage_maskpv, lstage_rzout]
        NS6 = len(stages)
        for step in range(len(hbs) + NS6 - 1):
            for si in range(NS6 - 1, -1, -1):
                i = step - si
                if 0 <= i < len(hbs):
                    stages[si](i)
        loc_done = p.cnt['V']
        if pr > 0:
            p.wait(nc.vector, 'W', p.combw)
        for c in range(NCH):
            p.bkwait(2, 3)
            ins = nc.tensor.matmul(pA[:, :CH], a2b[:, :], glp[:, c * CH:(c + 1) * CH],
                                   start=True, stop=True)
            ins = nc.tensor.matmul(pB[:, :CH], a2b[:, :], grp[:, c * CH:(c + 1) * CH],
                                   start=True, stop=True)
            p.inc(ins, 'T')
            p.wait(nc.vector, 'T')
            nc.vector.tensor_tensor(out=combp[:, c * CH:(c + 1) * CH],
                                    in0=pA[:, :CH], in1=locT[:, c * CH:(c + 1) * CH], op=OP.mult)
            nc.vector.tensor_tensor(out=rem2p[:, c * CH:(c + 1) * CH],
                                    in0=pB[:, :CH], in1=rem2p[:, c * CH:(c + 1) * CH], op=OP.mult)
            ins = nc.vector.tensor_tensor(out=combp[:, c * CH:(c + 1) * CH],
                                          in0=combp[:, c * CH:(c + 1) * CH],
                                          in1=rem2p[:, c * CH:(c + 1) * CH], op=OP.add)
            p.inc(ins, 'V')
            p.bkset(2, 'V')
            p.bkset(3, 'V')
        p.wait(nc.gpsimd, 'V')
        ins = p.dma(nc.gpsimd, 'W', out=s_comb[pr * 128:(pr + 1) * 128, :], in_=combp[:, :])
        p.combw = p.cnt['W']

    p.wait(nc.gpsimd, 'T')
    p.wait(nc.gpsimd, 'A')
    for gg in range(4):
        ins = p.dma(nc.gpsimd, 'W', gate=(gg == 0), out=woutb[gg][:, :],
                                  in_=d_wout[gg * 256:(gg + 1) * 256, :]
                                  .rearrange("(k q) o -> q k o", q=128))
    wout_ld = p.cnt['W']
    cc_free = [0, 0]
    for c in range(NCH):
        cc = Vp[c % 2]
        p.wait(nc.sync, 'W', p.combw)
        p.wait(nc.sync, 'T', cc_free[c % 2] if c >= 2 else p.cnt['T'])
        ins = p.dma(nc.sync, 'L', out=cc[:, 0:NTL * CH],
                                in_=s_comb[:, c * CH:(c + 1) * CH].rearrange("(g q) t -> q g t", q=128))
        ldc = p.cnt['L']
        for j in range(NTL):
            bb = bS1 if j % 2 == 0 else bS2
            p.wait(nc.tensor, 'L', ldc)
            p.wait(nc.tensor, 'W', wout_ld)
            p.bkwait(4 + j % 2)
            for g in range(NTL):
                ins = nc.tensor.matmul(bb[:, :CH],
                                       woutb[g // 2][:, (g % 2) * D + j * 128:
                                                     (g % 2) * D + (j + 1) * 128],
                                       cc[:, g * CH:(g + 1) * CH], start=(g == 0), stop=(g == NTL - 1))
            p.inc(ins, 'T')
            p.wait(nc.scalar, 'T')
            if j == 0:
                p.wait(nc.scalar, 'V')
            ins = nc.scalar.activation(out=xbc[:, j * CH:(j + 1) * CH], in_=bb[:, :CH],
                                       func=AF.Identity)
            p.inc(ins, 'A')
            p.bkset(4 + j % 2, 'A')
        cc_free[c % 2] = p.cnt['T']
        p.wait(nc.scalar, 'A')
        for i in range(NTL):
            ins = nc.scalar.activation(out=sq[:, i * CH:(i + 1) * CH], in_=xbc[:, i * CH:(i + 1) * CH],
                                       func=AF.Square)
        p.inc(ins, 'A')
        p.wait(nc.tensor, 'A')
        for i in range(NTL):
            ins = nc.tensor.matmul(pS[0:1, :CH], p.onesc[:, 0:1], xbc[:, i * CH:(i + 1) * CH],
                                   start=(i == 0), stop=(i == NTL - 1))
        p.inc(ins, 'T')
        for i in range(NTL):
            ins = nc.tensor.matmul(pQ[0:1, :CH], p.onesc[:, 0:1], sq[:, i * CH:(i + 1) * CH],
                                   start=(i == 0), stop=(i == NTL - 1))
        p.inc(ins, 'T')
        emit_ln_rows(p, pS, pQ, rM, rA, rB, rT, CH)
        emit_ln_bcast(p, rA, rB, pA, pB, p.onesr, CH)
        p.wait(nc.sync, 'W')
        p.wait(nc.sync, 'V')
        ins = p.dma(nc.sync, 'L', out=xchunk[:, :],
                                in_=s_x[:, c * CH:(c + 1) * CH].rearrange("(i q) t -> q i t", q=128))
        p.wait(nc.vector, 'L')
        p.wait(nc.vector, 'T')
        for i in range(NTL):
            asl = xbc[:, i * CH:(i + 1) * CH]
            nc.vector.tensor_tensor(out=asl, in0=asl, in1=pA[:, :CH], op=OP.mult)
            nc.vector.tensor_tensor(out=asl, in0=asl, in1=pB[:, :CH], op=OP.add)
            ins = nc.vector.tensor_tensor(out=xchunk[:, i * CH:(i + 1) * CH],
                                          in0=xchunk[:, i * CH:(i + 1) * CH], in1=asl, op=OP.add)
        p.inc(ins, 'V')
        p.bkset(2, 'V')
        p.bkset(3, 'V')
        x3_done = p.cnt['V']
        if last:
            p.wait(nc.gpsimd, 'V', x3_done)
            for i in range(NTL):
                ins = p.dma(nc.gpsimd, 'O', gate=(i == 0),
                            out=p.d_y[i * 128:(i + 1) * 128, c * CH:(c + 1) * CH],
                            in_=xchunk[:, i * CH:(i + 1) * CH])
        else:
            p.wait(nc.gpsimd, 'V', x3_done)
            ins = p.dma(nc.gpsimd, 'W', 
                out=s_x[:, c * CH:(c + 1) * CH].rearrange("(i q) t -> q i t", q=128),
                in_=xchunk[:, :])
            p.wait(nc.scalar, 'V', x3_done)
            emit_ln_stats(p, [xchunk[:, i * CH:(i + 1) * CH] for i in range(NTL)], xbc, sq,
                          p.onesc, pS, pQ, CH)
            emit_ln_rows(p, pS, pQ, rM, rA, rB, rT, CH)
            emit_ln_bcast(p, rA, rB, pA, pB, p.onesr, CH)
            p.wait(nc.vector, 'T')
            for i in range(NTL):
                nc.vector.tensor_tensor(out=xbc[:, i * CH:(i + 1) * CH],
                                        in0=xchunk[:, i * CH:(i + 1) * CH], in1=pA[:, :CH],
                                        op=OP.mult)
                ins = nc.vector.tensor_tensor(out=xn[:, i * T + c * CH: i * T + c * CH + CH],
                                              in0=xbc[:, i * CH:(i + 1) * CH], in1=pB[:, :CH],
                                              op=OP.add)
            p.inc(ins, 'V')
            p.bkset(2, 'V')
            p.bkset(3, 'V')
    if last:
        p.output_written = True
    p.x_dram = s_x



def emit_l1_ffn(p, xs, last, from_attn=False):
    nc = p.nc
    FF = 4096
    CH = 256
    NCH = T // CH
    NTOT = float(FF * D)

    d_w1t = p.din("l1f_W1T", [D, FF])
    d_w2t = p.din("l1f_W2T", [FF, D])
    d_b1 = p.din("l1f_b1p", [128, 32])
    d_b2 = p.din("l1f_b2p", [128, NTL])
    d_of = p.din("c_ones_f32col", [128, 1])
    s_x = p.x_dram if from_attn else p.dtmp("s_x_l1f", [D, T], F32)

    xsb = xs.bitcast(BF16)
    xn = xsb[:, 0:NTL * T]
    harea = xsb[:, 16384:24576]
    fout = xsb[:, 24576:26624]
    sq = xsb[:, 26624:28672]
    xchunk = xs[:, 14336:16384]
    wtmpB = [xs[:, 8192:12288], xs[:, 12288:16384]]
    wtmp = wtmpB[0]

    wq1 = p.sb("l1f_wq1", [128, NTL * FF], BF16)
    wq2 = p.sb("l1f_wq2", [128, 32 * D], BF16)
    b1s = p.sb("l1f_b1s", [128, 32], F32)
    b2s = p.sb("l1f_b2s", [128, NTL], F32)
    onef = p.sb("l1f_onef", [128, 1], F32)
    gacc = p.sb("l1f_gacc", [128, 40], F32)
    gth = p.sb("l1f_gth", [128, 4], F32)
    grow = p.sb("l1f_grow", [1, 4], F32)
    qtmp = p.sb("l1f_qtmp", [128, 4096], BF16)
    rM = p.sb("l1f_rM", [1, CH], F32)
    rA = p.sb("l1f_rA", [1, CH], F32)
    rB = p.sb("l1f_rB", [1, CH], F32)
    rT = p.sb("l1f_rT", [1, CH], F32)

    pS, pQ, pA, pB = p.bank[0], p.bank[1], p.bank[2], p.bank[3]
    bM1 = [p.bank[4], p.bank[5]]
    bM2 = [p.bank[6], p.bank[7]]

    for dst, dd in ((b1s, d_b1), (b2s, d_b2), (onef, d_of)):
        ins = p.dma(nc.sync, 'L', out=dst[:, :], in_=dd)

    if not from_attn:
        p.wait(nc.gpsimd, 'L')
        p.wait(nc.gpsimd, 'V')
        p.wait(nc.gpsimd, 'A')
        for i in range(NTL):
            ins = p.dma(nc.gpsimd, 'W', gate=(i == 0), out=s_x[i * 128:(i + 1) * 128, :],
                        in_=xs[:, i * T:(i + 1) * T])
        spill_done = p.cnt['W']
    for c in range(NCH if not from_attn else 0):
        p.wait(nc.sync, 'W', spill_done)
        if c > 0:
            p.wait(nc.sync, 'V')
        ins = p.dma(nc.sync, 'L', out=xchunk,
                    in_=s_x[:, c * CH:(c + 1) * CH].rearrange("(i q) t -> q i t", q=128))
        p.wait(nc.scalar, 'L')
        p.wait(nc.scalar, 'V')
        if c == 0:
            p.wait(nc.scalar, 'W', spill_done)
        emit_ln_stats(p, [xchunk[:, i * CH:(i + 1) * CH] for i in range(NTL)], fout, sq,
                      p.onesc, pS, pQ, CH)
        emit_ln_rows(p, pS, pQ, rM, rA, rB, rT, CH)
        emit_ln_bcast(p, rA, rB, pA, pB, p.onesr, CH)
        p.wait(nc.vector, 'T')
        if c == 0:
            p.wait(nc.vector, 'W', spill_done)
        for i in range(NTL):
            nc.vector.tensor_tensor(out=xchunk[:, i * CH:(i + 1) * CH],
                                    in0=xchunk[:, i * CH:(i + 1) * CH], in1=pA[:, :CH], op=OP.mult)
            ins = nc.vector.tensor_tensor(out=xn[:, i * T + c * CH: i * T + c * CH + CH],
                                          in0=xchunk[:, i * CH:(i + 1) * CH], in1=pB[:, :CH],
                                          op=OP.add)
        p.inc(ins, 'V')

    red_done = [0, 0]
    gi = 0
    for wi, (dd, ntile, wcols, coff) in enumerate(((d_w1t, NTL, FF, 0), (d_w2t, 32, D, 8))):
        for ti in range(ntile):
            par = gi % 2
            wb = wtmpB[par]
            qeng, qsem = (nc.sync, 'L') if par == 0 else (nc.gpsimd, 'W')
            if gi >= 2:
                p.wait(qeng, 'V', red_done[par])
            p.dma(qeng, qsem, gate=False, out=wb[:, 0:wcols], in_=dd[ti * 128:(ti + 1) * 128, :])
            p.wait(nc.vector, 'L')
            p.wait(nc.vector, 'W')
            ins = nc.vector.tensor_reduce(out=gacc[:, coff + ti:coff + ti + 1], in_=wb[:, 0:wcols],
                                          axis=mybir.AxisListType.X, op=OP.add,
                                          apply_absolute_value=True)
            p.inc(ins, 'V')
            red_done[par] = p.cnt['V']
            gi += 1
    p.wait(nc.tensor, 'V')
    p.wait(nc.tensor, 'L')
    ins = nc.tensor.matmul(pA[0:1, 0:40], onef[:, 0:1], gacc[:, :], start=True, stop=True)
    p.inc(ins, 'T')
    p.wait(nc.vector, 'T')
    ins = nc.vector.tensor_reduce(out=grow[0:1, 0:1], in_=pA[0:1, 0:8],
                                  axis=mybir.AxisListType.X, op=OP.add)
    ins = nc.vector.tensor_reduce(out=grow[0:1, 1:2], in_=pA[0:1, 8:40],
                                  axis=mybir.AxisListType.X, op=OP.add)
    p.inc(ins, 'V')
    p.wait(nc.vector, 'V')
    nc.vector.tensor_scalar(out=grow[0:1, 2:4], in0=grow[0:1, 0:2], scalar1=-0.5 / NTOT,
                            scalar2=-0.5e-5, op0=OP.mult, op1=OP.add)
    ins = nc.vector.tensor_scalar(out=grow[0:1, 0:2], in0=grow[0:1, 0:2], scalar1=0.5 / NTOT,
                                  scalar2=0.5e-5, op0=OP.mult, op1=OP.add)
    p.inc(ins, 'V')
    p.wait(nc.tensor, 'V')
    ins = nc.tensor.matmul(pB[:, 0:4], p.onesr[0:1, :], grow[0:1, 0:4], start=True, stop=True)
    p.inc(ins, 'T')
    p.wait(nc.vector, 'T')
    ins = nc.vector.tensor_copy(out=gth[:, :], in_=pB[:, 0:4])
    p.inc(ins, 'V')
    p.wait(nc.vector, 'V')

    w2tmpB = [xs[:, 12288:13312], xs[:, 13312:14336]]
    for wi, (dd, ntile, wcols, wdst, dstw) in enumerate(
            ((d_w1t, NTL, FF, wq1, FF), (d_w2t, 32, D, wq2, D))):
        thc = gth[:, wi:wi + 1]
        nthc = gth[:, 2 + wi:3 + wi]
        bufs = wtmpB if wi == 0 else w2tmpB
        for ti in range(ntile):
            par = gi % 2
            wb = bufs[par]
            qeng, qsem = (nc.sync, 'L') if par == 0 else (nc.gpsimd, 'W')
            p.wait(qeng, 'V', wq1_done if (wi == 1 and ti < 2) else red_done[par])
            p.dma(qeng, qsem, gate=False, out=wb[:, 0:wcols], in_=dd[ti * 128:(ti + 1) * 128, :])
            p.wait(nc.vector, 'L')
            p.wait(nc.vector, 'W')
            nc.vector.tensor_scalar(out=qtmp[:, 0:wcols], in0=wb[:, 0:wcols], scalar1=thc,
                                    scalar2=-1.0, op0=OP.is_ge, op1=OP.add)
            ins = nc.vector.scalar_tensor_tensor(out=wdst[:, ti * dstw:(ti + 1) * dstw],
                                                 in0=wb[:, 0:wcols], scalar=nthc,
                                                 in1=qtmp[:, 0:wcols],
                                                 op0=OP.is_ge, op1=OP.add)
            p.inc(ins, 'V')
            red_done[gi % 2] = p.cnt['V']
            gi += 1
        if wi == 0:
            wq1_done = p.cnt['V']

    wq2_done = p.cnt['V']
    for c in range(NCH):
        for mf in range(32):
            bb = bM1[mf % 2]
            p.wait(nc.tensor, 'V', wq1_done)
            p.bkwait(4 + mf % 2)
            for k in range(NTL):
                ins = nc.tensor.matmul(bb[:, :CH], wq1[:, k * FF + mf * 128: k * FF + (mf + 1) * 128],
                                       xn[:, k * T + c * CH: k * T + c * CH + CH],
                                       start=(k == 0), stop=(k == NTL - 1))
            p.inc(ins, 'T')
            p.wait(nc.scalar, 'T')
            ins = nc.scalar.activation(out=harea[:, mf * CH:(mf + 1) * CH], in_=bb[:, :CH],
                                       func=AF.Relu, bias=b1s[:, mf:mf + 1])
            p.inc(ins, 'A')
            p.bkset(4 + mf % 2, 'A')
        h_done = p.cnt['A']
        for md in range(NTL):
            bb = bM2[md % 2]
            p.wait(nc.tensor, 'V', wq2_done)
            p.wait(nc.tensor, 'A', h_done)
            p.bkwait(6 + md % 2)
            for kf in range(32):
                ins = nc.tensor.matmul(bb[:, :CH], wq2[:, kf * D + md * 128: kf * D + (md + 1) * 128],
                                       harea[:, kf * CH:(kf + 1) * CH],
                                       start=(kf == 0), stop=(kf == 31))
            p.inc(ins, 'T')
            p.wait(nc.scalar, 'T')
            ins = nc.scalar.activation(out=fout[:, md * CH:(md + 1) * CH], in_=bb[:, :CH],
                                       func=AF.Identity, bias=b2s[:, md:md + 1])
            p.inc(ins, 'A')
            p.bkset(6 + md % 2, 'A')
        p.wait(nc.scalar, 'A')
        p.wait(nc.scalar, 'V')
        for i in range(NTL):
            ins = nc.scalar.activation(out=sq[:, i * CH:(i + 1) * CH], in_=fout[:, i * CH:(i + 1) * CH],
                                       func=AF.Square)
        p.inc(ins, 'A')
        p.wait(nc.tensor, 'A')
        for i in range(NTL):
            ins = nc.tensor.matmul(pS[0:1, :CH], p.onesc[:, 0:1], fout[:, i * CH:(i + 1) * CH],
                                   start=(i == 0), stop=(i == NTL - 1))
        p.inc(ins, 'T')
        for i in range(NTL):
            ins = nc.tensor.matmul(pQ[0:1, :CH], p.onesc[:, 0:1], sq[:, i * CH:(i + 1) * CH],
                                   start=(i == 0), stop=(i == NTL - 1))
        p.inc(ins, 'T')
        emit_ln_rows(p, pS, pQ, rM, rA, rB, rT, CH)
        emit_ln_bcast(p, rA, rB, pA, pB, p.onesr, CH)
        p.wait(nc.sync, 'T')
        p.wait(nc.sync, 'O')
        ins = p.dma(nc.sync, 'L', out=xchunk,
                    in_=s_x[:, c * CH:(c + 1) * CH].rearrange("(i q) t -> q i t", q=128))
        p.wait(nc.vector, 'L')
        p.wait(nc.vector, 'T')
        for i in range(NTL):
            fsl = fout[:, i * CH:(i + 1) * CH]
            nc.vector.tensor_tensor(out=fsl, in0=fsl, in1=pA[:, :CH], op=OP.mult)
            nc.vector.tensor_tensor(out=fsl, in0=fsl, in1=pB[:, :CH], op=OP.add)
            ins = nc.vector.tensor_tensor(out=xchunk[:, i * CH:(i + 1) * CH],
                                          in0=xchunk[:, i * CH:(i + 1) * CH], in1=fsl, op=OP.add)
        p.inc(ins, 'V')
        p.bkset(2, 'V')
        p.bkset(3, 'V')
        p.wait(nc.gpsimd, 'V')
        ins = p.dma(nc.gpsimd, 'O',
                    out=p.d_y[:, c * CH:(c + 1) * CH].rearrange("(i q) t -> q i t", q=128),
                    in_=xchunk)
    p.output_written = True



EMITTERS = {}


def _register():
    EMITTERS["l0_attn"] = lambda p, xs, last: emit_l0_attn(p, xs)
    EMITTERS["l0_ffn"] = lambda p, xs, last: emit_l0_ffn(p, xs)
    EMITTERS["l1_attn"] = emit_l1_attn
    EMITTERS["l1_ffn"] = emit_l1_ffn


_register()


def build_program(block_names):
    p = Prog()
    nc = p.nc
    with p.es:
        d_x = p.din("xT", [D, T])
        d_y = p.dout("yT", [D, T])
        d_onesb = p.din("c_ones_bf", [128, 1], BF16)
        d_onesf = p.din("c_ones_f", [1, 128], F32)
        d_eps = p.din("c_eps", [128, 1], F32)
        xs = p.sbuf("xs", [128, NTL * T], F32)
        p.big = p.sbuf("big_arena", [128, 72960], BF16)
        p.big_f = p.big.bitcast(F32)
        p.bank = [p.psum("bank%d" % i, [128, 512]) for i in range(8)]
        p.bk = [(None, 0)] * 8
        p.onesc = p.sbuf("onesc_bf", [128, 1], BF16)
        p.onesr = p.sbuf("onesr_f", [1, 128], F32)
        p.epscol = p.sbuf("epscol", [128, 1], F32)
        nc.const_aps.register(F32, LN_EPS, p.epscol[:, 0:1])
        p.onecf = p.sbuf("onecf", [128, 1], F32)
        nc.const_aps.register(F32, 1.0, p.onecf[:, 0:1])
        p.sem('L', 'W', 'T', 'V', 'A', 'G', 'O')
        with nc.Block():
            ins = p.dma(nc.sync, 'L', out=p.onesc[:, :], in_=d_onesb)
            ins = p.dma(nc.sync, 'L', out=p.onesr[:, :], in_=d_onesf)
            ins = p.dma(nc.sync, 'L', out=p.epscol[:, :], in_=d_eps)
            ins = p.dma(nc.sync, 'L', out=p.onecf[:, :], in_=p.din("c_ones_f32col", [128, 1]))
            p.const_L = p.cnt['L']
            for i in range(NTL):
                ins = p.dma(nc.sync, 'L', gate=(i == 0), out=xs[:, i * T:(i + 1) * T],
                                        in_=d_x[i * 128:(i + 1) * 128, :])
            p.output_written = False
            p.d_y = d_y
            for bi, b in enumerate(block_names):
                p.carve_reset()
                if bi > 0:
                    p.barrier()
                    if getattr(p, "debug", False):
                        p.dump("xs_after_%d" % bi, xs[:, :])
                if b == "l1_ffn" and bi > 0 and block_names[bi - 1] == "l1_attn":
                    emit_l1_ffn(p, xs, True, from_attn=True)
                else:
                    EMITTERS[b](p, xs, bi == len(block_names) - 1)
            if not p.output_written:
                p.wait(nc.gpsimd, 'V')
                p.wait(nc.gpsimd, 'A')
                for i in range(NTL):
                    ins = p.dma(nc.gpsimd, 'O', gate=(i == 0), out=d_y[i * 128:(i + 1) * 128, :],
                                              in_=xs[:, i * T:(i + 1) * T])
            p.wait(nc.gpsimd, 'O')
    return p



def _pack_vec(v):
    return np.ascontiguousarray(np.asarray(v, dtype=np.float32).reshape(NTL, 128).T)


def _host_tensors(inputs):
    t = {}
    vecs = np.zeros((128, 7 * NTL), np.float32)
    vecs[:, 0:NTL] = _pack_vec(inputs["L0_attn_time_decay"])
    vecs[:, NTL:2 * NTL] = _pack_vec(inputs["L0_attn_time_first"])
    vecs[:, 2 * NTL:3 * NTL] = _pack_vec(np.asarray(inputs["L0_attn_time_mix_key"]).ravel())
    vecs[:, 3 * NTL:4 * NTL] = _pack_vec(np.asarray(inputs["L0_attn_time_mix_value"]).ravel())
    vecs[:, 4 * NTL:5 * NTL] = _pack_vec(np.asarray(inputs["L0_attn_time_mix_receptance"]).ravel())
    vecs[:, 5 * NTL:6 * NTL] = _pack_vec(inputs["L0_attn_bo"])
    t["l0a_vecs"] = vecs
    t["l0a_WkT"] = np.ascontiguousarray(np.asarray(inputs["L0_attn_Wk"], np.float32).T)
    t["l0a_WvT"] = np.ascontiguousarray(np.asarray(inputs["L0_attn_Wv"], np.float32).T)
    t["l0a_WrT"] = np.ascontiguousarray(np.asarray(inputs["L0_attn_Wr"], np.float32).T)
    t["l0a_WoT"] = np.ascontiguousarray(np.asarray(inputs["L0_attn_Wo"], np.float32).T)
    E, S, HD, ES = 16, 16, 256, 256
    t["l0f_mu2d"] = np.ascontiguousarray(np.asarray(inputs["L0_ffn_mu"], np.float32).reshape(D, ES))
    t["l0f_w1"] = np.ascontiguousarray(np.asarray(inputs["L0_ffn_w1"], np.float32).reshape(E * D, HD))
    t["l0f_w2"] = np.ascontiguousarray(np.asarray(inputs["L0_ffn_w2"], np.float32).reshape(E * HD, D))
    gbp = np.zeros((128, 2 * NTL), np.float32)
    gbp[:, 0:NTL] = _pack_vec(inputs["L0_ffn_ln_gamma"])
    gbp[:, NTL:2 * NTL] = _pack_vec(inputs["L0_ffn_ln_beta"])
    t["l0f_gb"] = gbp
    b1 = np.asarray(inputs["L0_ffn_b1"], np.float32)
    b1p = np.zeros((128, 2 * E), np.float32)
    for mh in range(2):
        for e in range(E):
            b1p[:, mh * 16 + e] = b1[e, mh * 128:(mh + 1) * 128]
    t["l0f_b1p"] = b1p
    b2 = np.asarray(inputs["L0_ffn_b2"], np.float32)
    b2p = np.zeros((128, 8 * E), np.float32)
    for md in range(8):
        for e in range(E):
            b2p[:, md * 16 + e] = b2[e, md * 128:(md + 1) * 128]
    t["l0f_b2p"] = b2p
    t["l0f_scale_col"] = np.broadcast_to(np.asarray(inputs["L0_ffn_scale"], np.float32).reshape(1, 1),
                                         (128, 1)).copy()
    t["c_ident"] = np.eye(128, dtype=np.float32).astype(BF)
    t["l1a_wq"] = np.ascontiguousarray(np.asarray(inputs["L1_attn_wq"], np.float32))
    t["l1a_wk"] = np.ascontiguousarray(np.asarray(inputs["L1_attn_wk"], np.float32))
    t["l1a_wv"] = np.ascontiguousarray(np.asarray(inputs["L1_attn_wv"], np.float32))
    t["l1a_wg3"] = np.ascontiguousarray(np.concatenate([
        np.asarray(inputs["L1_attn_wgk"], np.float32),
        np.asarray(inputs["L1_attn_wgv"], np.float32),
        np.asarray(inputs["L1_attn_wgq"], np.float32)], axis=0))
    t["l1a_rout"] = np.ascontiguousarray(np.asarray(inputs["L1_attn_r_out_w"], np.float32))
    t["l1a_wout"] = np.ascontiguousarray(np.asarray(inputs["L1_attn_w_out"], np.float32))
    gl_w = np.asarray(inputs["L1_attn_gate_local_w"], np.float32)
    gr_w = np.asarray(inputs["L1_attn_gate_remote_w"], np.float32)
    gp = np.zeros((128, 32), np.float32)
    for gi, gw in ((0, gl_w), (1, gr_w)):
        for pr in range(8):
            for h2 in range(2):
                gp[h2 * 64:(h2 + 1) * 64, gi * 16 + pr * 2 + h2] = gw[pr * 2 + h2]
    t["l1a_gatep"] = gp
    a2 = np.zeros((2, 128), np.float32)
    a2[0, 0:64] = 1.0
    a2[1, 64:128] = 1.0
    t["l1a_a2"] = a2.astype(BF)
    jj = np.arange(512)[:, None]
    ii = np.arange(256)[None, :]
    allowed = (jj > ii) & (jj <= ii + 256)
    mk = np.zeros((128, 1024), np.float32)
    for jt in range(4):
        mk[:, jt * 256:(jt + 1) * 256] = allowed[jt * 128:(jt + 1) * 128, :]
    t["l1a_mask"] = mk.astype(BF)
    lm = (np.arange(64)[:, None] <= (np.arange(2048) // 32)[None, :]).astype(np.float32)
    t["l1a_lmask"] = lm.astype(BF)
    t["l1f_W1T"] = np.ascontiguousarray(np.asarray(inputs["L1_ffn_W1"], np.float32).T)
    t["l1f_W2T"] = np.ascontiguousarray(np.asarray(inputs["L1_ffn_W2"], np.float32).T)
    b1f = np.asarray(inputs["L1_ffn_b1"], np.float32)
    t["l1f_b1p"] = np.ascontiguousarray(b1f.reshape(32, 128).T)
    b2f = np.asarray(inputs["L1_ffn_b2"], np.float32)
    t["l1f_b2p"] = np.ascontiguousarray(b2f.reshape(NTL, 128).T)
    t["c_ones_f32col"] = np.ones((128, 1), np.float32)
    return t


def kernel(**inputs):
    x = np.asarray(inputs["L0_attn_x"], np.float32)
    host = _host_tensors(inputs)
    host["c_ones_bf"] = np.ones((128, 1), BF)
    host["c_ones_f"] = np.ones((1, 128), np.float32)
    host["c_eps"] = np.full((128, 1), LN_EPS, np.float32)
    xT = [np.ascontiguousarray(x[b].T) for b in range(8)]
    for blocks in LAUNCH_PLAN:
        p = build_program(blocks)
        in_maps = []
        for b in range(8):
            m = {"xT": xT[b]}
            for name in p.inputs_used:
                if name != "xT":
                    m[name] = host[name]
            in_maps.append(m)
        res = run_bass_kernel_spmd(p.nc, in_maps, core_ids=list(range(8)))
        xT = [np.asarray(res.results[b]["yT"], np.float32) for b in range(8)]
    out = np.stack([np.ascontiguousarray(xT[b].T) for b in range(8)], axis=0)
    return out


# version 20
# speedup vs baseline: 1.0426x; 1.0426x over previous
import numpy as np
import ml_dtypes
from contextlib import ExitStack
import concourse.bass as bass
import concourse.mybir as mybir
from concourse.bass_utils import run_bass_kernel_spmd
from concourse.alu_op_type import AluOpType as OP

F32 = mybir.dt.float32
BF16 = mybir.dt.bfloat16
AF = mybir.ActivationFunctionType
BF = ml_dtypes.bfloat16

T = 2048
D = 1024
NTL = 8
SEG = 512
NSEG = T // SEG
LN_EPS = 1e-5

LAUNCH_PLAN = [["l0_attn", "l0_ffn", "l1_attn", "l1_ffn"]]


class Prog:

    def __init__(self):
        self.nc = bass.Bass("TRN2", target_bir_lowering=False)
        self.es = ExitStack()
        self.sems = {}
        self.cnt = {}
        self.inputs_used = []

    def sem(self, *names):
        for n in names:
            self.sems[n] = self.es.enter_context(self.nc.semaphore(n))
            self.cnt[n] = 0

    def inc(self, instr, name, n=1):
        instr.then_inc(self.sems[name], n)
        self.cnt[name] += n

    def wait(self, eng, name, val=None):
        eng.wait_ge(self.sems[name], self.cnt[name] if val is None else val)

    def dma(self, eng, sem, gate=True, **kw):
        if gate and self.cnt[sem] > 0:
            eng.wait_ge(self.sems[sem], self.cnt[sem])
        ins = eng.dma_start(**kw)
        self.inc(ins, sem, 16)
        return ins

    def sbuf(self, name, shape, dtype):
        return self.es.enter_context(self.nc.sbuf_tensor(name, shape, dtype))

    def carve_reset(self):
        self.car = 0

    def sb(self, name, shape, dtype):
        cols = shape[1]
        self.car = (self.car + 15) & ~15
        if dtype == BF16:
            v = self.big[:, self.car:self.car + cols]
            self.car += cols
        else:
            assert dtype == F32
            o2 = self.car // 2
            v = self.big_f[:, o2:o2 + cols]
            self.car += 2 * cols
        assert self.car <= self.big.shape[1], f"arena overflow at {name}: {self.car}"
        if shape[0] != 128:
            v = v[0:shape[0], :]
        return v

    def bkwait(self, *idxs):
        for i in idxs:
            s, v = self.bk[i]
            if s is not None:
                self.wait(self.nc.tensor, s, v)

    def bkset(self, idx, sem):
        self.bk[idx] = (sem, self.cnt[sem])

    def barrier(self):
        nc = self.nc
        for eng in (nc.tensor, nc.vector, nc.scalar, nc.gpsimd, nc.sync):
            for s in self.sems:
                eng.wait_ge(self.sems[s], self.cnt[s])

    def psum(self, name, shape, dtype=F32):
        return self.es.enter_context(self.nc.psum_tensor(name, shape, dtype))

    def din(self, name, shape, dtype=F32):
        if not hasattr(self, "_dins"):
            self._dins = {}
        if name not in self._dins:
            self.inputs_used.append(name)
            self._dins[name] = self.nc.dram_tensor(name, shape, dtype, kind="ExternalInput").ap()
        return self._dins[name]

    def dout(self, name, shape, dtype=F32):
        return self.nc.dram_tensor(name, shape, dtype, kind="ExternalOutput").ap()

    def dtmp(self, name, shape, dtype):
        return self.nc.dram_tensor(name, shape, dtype, kind="Internal").ap()

    def dump(self, name, ap_2d):
        if not getattr(self, "debug", False):
            return
        w = ap_2d.shape[-1]
        d = self.nc.dram_tensor("dbg_" + name, [ap_2d.shape[0], w], ap_2d.dtype,
                                kind="ExternalOutput").ap()
        self.wait(self.nc.gpsimd, 'V')
        self.wait(self.nc.gpsimd, 'A')
        self.wait(self.nc.gpsimd, 'T')
        self.dma(self.nc.gpsimd, 'O', out=d, in_=ap_2d)



def emit_ln_stats(p, src_f32_slices, xb, sq, ones_col_bf, pS, pQ, seg_w):
    nc = p.nc
    for i in range(NTL):
        nc.scalar.activation(out=xb[:, i * seg_w:(i + 1) * seg_w], in_=src_f32_slices[i],
                             func=AF.Identity)
        ins = nc.scalar.activation(out=sq[:, i * seg_w:(i + 1) * seg_w], in_=src_f32_slices[i],
                                   func=AF.Square)
    p.inc(ins, 'A')
    p.wait(nc.tensor, 'A')
    p.wait(nc.tensor, 'L', p.const_L)
    for i in range(NTL):
        ins = nc.tensor.matmul(pS[0:1, :seg_w], ones_col_bf[:, 0:1],
                               xb[:, i * seg_w:(i + 1) * seg_w],
                               start=(i == 0), stop=(i == NTL - 1))
    p.inc(ins, 'T')
    for i in range(NTL):
        ins = nc.tensor.matmul(pQ[0:1, :seg_w], ones_col_bf[:, 0:1],
                               sq[:, i * seg_w:(i + 1) * seg_w],
                               start=(i == 0), stop=(i == NTL - 1))
    p.inc(ins, 'T')


def emit_ln_rows(p, pS, pQ, rM, rA, rB, rT, seg_w, double_ln=False):
    nc = p.nc

    def af(ins):
        p.inc(ins, 'A')
        p.wait(nc.scalar, 'A')

    def vf(ins):
        p.inc(ins, 'V')
        p.wait(nc.vector, 'V')

    p.wait(nc.scalar, 'T')
    nc.scalar.activation(out=rM[0:1, :seg_w], in_=pS[0:1, :seg_w], func=AF.Identity,
                         scale=-1.0 / D)
    nc.scalar.activation(out=rT[0:1, :seg_w], in_=pS[0:1, :seg_w], func=AF.Square,
                         scale=1.0 / D)
    ins = nc.scalar.activation(out=rA[0:1, :seg_w], in_=pQ[0:1, :seg_w], func=AF.Identity,
                               scale=1.0 / D)
    af(ins)
    p.wait(nc.vector, 'A')
    vf(nc.vector.tensor_tensor(out=rA[0:1, :seg_w], in0=rA[0:1, :seg_w], in1=rT[0:1, :seg_w],
                               op=OP.subtract))
    p.wait(nc.scalar, 'V')
    if not double_ln:
        nc.scalar.activation(out=rT[0:1, :seg_w], in_=rA[0:1, :seg_w], func=AF.Ln,
                             bias=LN_EPS)
        af(nc.scalar.activation(out=rA[0:1, :seg_w], in_=rT[0:1, :seg_w], func=AF.Exp,
                                scale=-0.5))
        p.wait(nc.vector, 'A')
    else:
        nc.scalar.activation(out=rT[0:1, :seg_w], in_=rA[0:1, :seg_w], func=AF.Ln,
                             bias=LN_EPS)
        af(nc.scalar.activation(out=rT[0:1, :seg_w], in_=rT[0:1, :seg_w], func=AF.Exp,
                                scale=-0.5))
        p.wait(nc.vector, 'A')
        vf(nc.vector.tensor_tensor(out=rA[0:1, :seg_w], in0=rA[0:1, :seg_w], in1=rT[0:1, :seg_w],
                                   op=OP.mult))
        vf(nc.vector.tensor_tensor(out=rA[0:1, :seg_w], in0=rA[0:1, :seg_w], in1=rT[0:1, :seg_w],
                                   op=OP.mult))
        p.wait(nc.scalar, 'V')
        nc.scalar.activation(out=rA[0:1, :seg_w], in_=rA[0:1, :seg_w], func=AF.Ln,
                             bias=LN_EPS)
        af(nc.scalar.activation(out=rA[0:1, :seg_w], in_=rA[0:1, :seg_w], func=AF.Exp,
                                scale=-0.5))
        p.wait(nc.vector, 'A')
        vf(nc.vector.tensor_tensor(out=rA[0:1, :seg_w], in0=rA[0:1, :seg_w], in1=rT[0:1, :seg_w],
                                   op=OP.mult))
    vf(nc.vector.tensor_tensor(out=rB[0:1, :seg_w], in0=rM[0:1, :seg_w], in1=rA[0:1, :seg_w],
                               op=OP.mult))


def emit_ln_bcast(p, rA, rB, pA, pB, ones_row_f, seg_w, lhsA=None, lhsB=None):
    nc = p.nc
    p.wait(nc.tensor, 'V')
    p.wait(nc.tensor, 'L', p.const_L)
    ins = nc.tensor.matmul(pA[:, :seg_w], lhsA if lhsA is not None else ones_row_f[0:1, :],
                           rA[0:1, :seg_w], start=True, stop=True)
    ins = nc.tensor.matmul(pB[:, :seg_w], lhsB if lhsB is not None else ones_row_f[0:1, :],
                           rB[0:1, :seg_w], start=True, stop=True)
    p.inc(ins, 'T')



def emit_l0_attn(p, xs):
    nc = p.nc

    d_wk = p.din("l0a_WkT", [D, D])
    d_wv = p.din("l0a_WvT", [D, D])
    d_wr = p.din("l0a_WrT", [D, D])
    d_wo = p.din("l0a_WoT", [D, D])
    d_vec = p.din("l0a_vecs", [128, 7 * NTL])

    wk = p.sb("wk_sb", [128, NTL * D], BF16)
    wv = p.sb("wv_sb", [128, NTL * D], BF16)
    wr = p.sb("wr_sb", [128, NTL * D], BF16)
    wo = p.sb("wo_sb", [128, NTL * D], BF16)
    vec = p.sb("vec_sb", [128, 7 * NTL], F32)
    vtd = p.sb("vtd", [128, NTL], F32)
    vetf = p.sb("vetf", [128, NTL], F32)
    tdcur = p.sb("tdcur", [128, SEG], F32)
    xn = p.sb("xn_seg", [128, NTL * SEG], BF16)
    sq = p.sb("sq_seg", [128, NTL * SEG], BF16)
    kin = p.sb("kin_seg", [128, NTL * SEG], BF16)
    xb = p.sb("xb_seg", [128, NTL * SEG], BF16)
    vin = p.sb("vin_seg", [128, NTL * SEG], BF16)
    rin = p.sb("rin_seg", [128, NTL * SEG], BF16)
    rwkv = p.sb("rwkv_seg", [128, NTL * SEG], BF16)
    attn = kin
    pcol = p.sb("pcol", [128, NTL], BF16)
    carN = p.sb("carN", [128, NTL], F32)
    carD = p.sb("carD", [128, NTL], F32)
    ekB = [p.sb("ek_s0", [128, SEG], F32), p.sb("ek_s1", [128, SEG], F32)]
    vvB = [p.sb("vv_s0", [128, SEG], F32), p.sb("vv_s1", [128, SEG], F32)]
    rrB = [p.sb("rr_s0", [128, SEG], F32), p.sb("rr_s1", [128, SEG], F32)]
    snumB = [p.sb("snum_s0", [128, SEG], F32), p.sb("snum_s1", [128, SEG], F32)]
    ek, vv, rr, snum = ekB[0], vvB[0], rrB[0], snumB[0]
    sden = tdcur
    lnt = ek
    rM = ek[0:1, :]
    rA = vv[0:1, :]
    rB = rr[0:1, :]
    rT = snum[0:1, :]
    onesc = p.onesc
    onesr = p.onesr

    pS, pQ, pA, pB = p.bank[0], p.bank[1], p.bank[2], p.bank[3]
    pK = [p.bank[4], p.bank[5]]
    pV = [p.bank[6], p.bank[7]]

    ins = p.dma(nc.sync, 'L', out=vec[:, :], in_=d_vec)
    for w_sb, d_w in ((wk, d_wk), (wv, d_wv), (wr, d_wr), (wo, d_wo)):
        for k in range(NTL):
            ins = p.dma(nc.gpsimd, 'W', gate=(k == 0), out=w_sb[:, k * D:(k + 1) * D],
                                      in_=d_w[k * 128:(k + 1) * 128, :])

    p.wait(nc.scalar, 'L')
    nc.scalar.activation(out=vtd[:, :], in_=vec[:, 0:NTL], func=AF.Exp)
    nc.scalar.activation(out=vtd[:, :], in_=vtd[:, :], func=AF.Exp, scale=-1.0)
    ins = nc.scalar.activation(out=vetf[:, :], in_=vec[:, NTL:2 * NTL], func=AF.Exp)
    p.inc(ins, 'A')

    nc.vector.memset(pcol[:, :], 0.0)
    nc.vector.memset(carN[:, :], 0.0)
    ins = nc.vector.memset(carD[:, :], 0.0)
    p.inc(ins, 'V')

    tmk = vec[:, 2 * NTL:3 * NTL]
    tmv = vec[:, 3 * NTL:4 * NTL]
    tmr = vec[:, 4 * NTL:5 * NTL]
    bo = vec[:, 5 * NTL:6 * NTL]

    wait_x = p.cnt['L']

    def emit_ln1(s, vwait=None):
        c0 = s * SEG

        def xsl(i):
            return xs[:, i * T + c0: i * T + c0 + SEG]
        p.wait(nc.scalar, 'L', wait_x)
        if vwait is not None:
            p.wait(nc.scalar, 'V', vwait)
        emit_ln_stats(p, [xsl(i) for i in range(NTL)], xb, sq, onesc, pS, pQ, SEG)
        emit_ln_rows(p, pS, pQ, rM, rA, rB, rT, SEG)
        emit_ln_bcast(p, rA, rB, pA, pB, onesr, SEG)
        p.wait(nc.vector, 'T')
        for i in range(NTL):
            nc.vector.tensor_tensor(out=lnt[:, :], in0=xsl(i), in1=pA[:, :], op=OP.mult)
            ins = nc.vector.tensor_tensor(out=xn[:, i * SEG:(i + 1) * SEG], in0=lnt[:, :],
                                          in1=pB[:, :], op=OP.add)
        p.inc(ins, 'V')
        p.bkset(2, 'V')
        p.bkset(3, 'V')

    emit_ln1(0)
    for s in range(NSEG):
        c0 = s * SEG

        def xsl(i):
            return xs[:, i * T + c0: i * T + c0 + SEG]

        if s > 0:
            p.wait(nc.vector, 'T', prev_kvr_reads)
        for i in range(NTL):
            a = i * SEG
            nc.vector.tensor_tensor(out=sq[:, a:a + 1], in0=xn[:, a:a + 1],
                                    in1=pcol[:, i:i + 1], op=OP.subtract)
            nc.vector.tensor_tensor(out=sq[:, a + 1:a + SEG], in0=xn[:, a + 1:a + SEG],
                                    in1=xn[:, a:a + SEG - 1], op=OP.subtract)
            for dst, tm in ((kin, tmk), (vin, tmv), (rin, tmr)):
                nc.vector.scalar_tensor_tensor(out=dst[:, a:a + 1], in0=sq[:, a:a + 1],
                                               scalar=tm[:, i:i + 1], in1=pcol[:, i:i + 1],
                                               op0=OP.mult, op1=OP.add)
                nc.vector.scalar_tensor_tensor(out=dst[:, a + 1:a + SEG], in0=sq[:, a + 1:a + SEG],
                                               scalar=tm[:, i:i + 1], in1=xn[:, a:a + SEG - 1],
                                               op0=OP.mult, op1=OP.add)
            ins = nc.vector.tensor_copy(out=pcol[:, i:i + 1], in_=xn[:, a + SEG - 1:a + SEG])
        p.inc(ins, 'V')
        mix_done = p.cnt['V']

        wkv_chain = []
        for m in range(NTL):
            bK = pK[m % 2]
            bV = pV[m % 2]
            ek, vv, rr, snum = ekB[m % 2], vvB[m % 2], rrB[m % 2], snumB[m % 2]
            sden = tdcur
            if s == 0 and m == 0:
                p.wait(nc.tensor, 'W')
            p.wait(nc.tensor, 'V', mix_done)
            p.bkwait(4 + m % 2, 6 + m % 2)
            for k in range(NTL):
                ins = nc.tensor.matmul(bK[:, :], wk[:, k * D + m * 128: k * D + (m + 1) * 128],
                                       kin[:, k * SEG:(k + 1) * SEG],
                                       start=(k == 0), stop=(k == NTL - 1))
            for k in range(NTL):
                ins = nc.tensor.matmul(bV[:, :], wv[:, k * D + m * 128: k * D + (m + 1) * 128],
                                       vin[:, k * SEG:(k + 1) * SEG],
                                       start=(k == 0), stop=(k == NTL - 1))
            p.inc(ins, 'T')
            tKV = p.cnt['T']
            p.wait(nc.scalar, 'T', tKV)
            if m >= 2:
                p.wait(nc.scalar, 'V', wkv_chain[m - 2])
            elif s > 0:
                p.wait(nc.scalar, 'V', seg_wkv_last)
            nc.scalar.activation(out=ek[:, :], in_=bK[:, :], func=AF.Exp)
            ins = nc.scalar.activation(out=vv[:, :], in_=bV[:, :], func=AF.Identity)
            p.inc(ins, 'A')
            eKV = p.cnt['A']
            p.bkset(6 + m % 2, 'A')
            p.wait(nc.tensor, 'A', eKV)
            for k in range(NTL):
                ins = nc.tensor.matmul(bK[:, :], wr[:, k * D + m * 128: k * D + (m + 1) * 128],
                                       rin[:, k * SEG:(k + 1) * SEG],
                                       start=(k == 0), stop=(k == NTL - 1))
            p.inc(ins, 'T')
            if m == NTL - 1:
                prev_kvr_reads = p.cnt['T']
            p.wait(nc.scalar, 'T')
            ins = nc.scalar.activation(out=rr[:, :], in_=bK[:, :], func=AF.Exp, scale=-1.0)
            p.inc(ins, 'A')
            p.bkset(4 + m % 2, 'A')
            ins = nc.scalar.activation(out=rr[:, :], in_=rr[:, :], func=AF.Ln, bias=1.0)
            p.inc(ins, 'A')
            ins = nc.scalar.activation(out=rr[:, :], in_=rr[:, :], func=AF.Exp, scale=-1.0)
            p.inc(ins, 'A')

            p.wait(nc.vector, 'A', eKV)
            nc.vector.tensor_scalar(out=tdcur[:, :], in0=xs[:, 0:SEG], scalar1=0.0,
                                    scalar2=vtd[:, m:m + 1], op0=OP.mult, op1=OP.add)
            nc.vector.tensor_tensor(out=vv[:, :], in0=ek[:, :], in1=vv[:, :], op=OP.mult)
            nc.vector.tensor_tensor_scan(out=snum[:, :], data0=tdcur[:, :],
                                         data1=vv[:, :], initial=carN[:, m:m + 1],
                                         op0=OP.mult, op1=OP.add)
            nc.vector.tensor_tensor_scan(out=sden[:, :], data0=tdcur[:, :],
                                         data1=ek[:, :], initial=carD[:, m:m + 1],
                                         op0=OP.mult, op1=OP.add)
            nc.vector.scalar_tensor_tensor(out=vv[:, 0:1], in0=vv[:, 0:1],
                                           scalar=vetf[:, m:m + 1], in1=carN[:, m:m + 1],
                                           op0=OP.mult, op1=OP.add)
            nc.vector.scalar_tensor_tensor(out=vv[:, 1:SEG], in0=vv[:, 1:SEG],
                                           scalar=vetf[:, m:m + 1], in1=snum[:, 0:SEG - 1],
                                           op0=OP.mult, op1=OP.add)
            nc.vector.scalar_tensor_tensor(out=ek[:, 0:1], in0=ek[:, 0:1],
                                           scalar=vetf[:, m:m + 1], in1=carD[:, m:m + 1],
                                           op0=OP.mult, op1=OP.add)
            nc.vector.scalar_tensor_tensor(out=ek[:, 1:SEG], in0=ek[:, 1:SEG],
                                           scalar=vetf[:, m:m + 1], in1=sden[:, 0:SEG - 1],
                                           op0=OP.mult, op1=OP.add)
            nc.vector.tensor_copy(out=carN[:, m:m + 1], in_=snum[:, SEG - 1:SEG])
            ins = nc.vector.tensor_copy(out=carD[:, m:m + 1], in_=sden[:, SEG - 1:SEG])
            p.inc(ins, 'V')
            p.wait(nc.scalar, 'V')
            nc.scalar.activation(out=ek[:, :], in_=ek[:, :], func=AF.Ln)
            ins = nc.scalar.activation(out=ek[:, :], in_=ek[:, :], func=AF.Exp, scale=-1.0)
            p.inc(ins, 'A')
            p.wait(nc.vector, 'A')
            nc.vector.tensor_tensor(out=vv[:, :], in0=vv[:, :], in1=ek[:, :], op=OP.mult)
            p.wait(nc.vector, 'A')
            ins = nc.vector.tensor_tensor(out=rwkv[:, m * SEG:(m + 1) * SEG], in0=vv[:, :],
                                          in1=rr[:, :], op=OP.mult)
            p.inc(ins, 'V')
            wkv_chain.append(p.cnt['V'])
        seg_wkv_last = wkv_chain[-1]
        if s + 1 < NSEG:
            emit_ln1(s + 1, vwait=mix_done)
        rwkv_done = p.cnt['V']
        for j in range(NTL):
            bO = pK[j % 2]
            p.wait(nc.tensor, 'V', rwkv_done)
            p.bkwait(4 + j % 2)
            for k in range(NTL):
                ins = nc.tensor.matmul(bO[:, :], wo[:, k * D + j * 128: k * D + (j + 1) * 128],
                                       rwkv[:, k * SEG:(k + 1) * SEG],
                                       start=(k == 0), stop=(k == NTL - 1))
            p.inc(ins, 'T')
            p.wait(nc.scalar, 'T')
            ins = nc.scalar.activation(out=attn[:, j * SEG:(j + 1) * SEG], in_=bO[:, :],
                                       func=AF.Identity, bias=bo[:, j:j + 1])
            p.inc(ins, 'A')
            p.bkset(4 + j % 2, 'A')

        p.wait(nc.scalar, 'A')
        emit_ln_stats(p, [attn[:, i * SEG:(i + 1) * SEG] for i in range(NTL)],
                      xb, sq, onesc, pS, pQ, SEG)
        emit_ln_rows(p, pS, pQ, rM, rA, rB, rT, SEG)
        emit_ln_bcast(p, rA, rB, pA, pB, onesr, SEG)
        p.wait(nc.vector, 'T')
        for i in range(NTL):
            asl = attn[:, i * SEG:(i + 1) * SEG]
            nc.vector.tensor_tensor(out=asl, in0=asl, in1=pA[:, :], op=OP.mult)
            nc.vector.tensor_tensor(out=asl, in0=asl, in1=pB[:, :], op=OP.add)
            ins = nc.vector.tensor_tensor(out=xsl(i), in0=xsl(i), in1=asl, op=OP.add)
        p.inc(ins, 'V')
        p.bkset(2, 'V')
        p.bkset(3, 'V')
        p.wait(nc.scalar, 'V')



def emit_l0_ffn(p, xs):
    nc = p.nc
    E, S, HD, ES = 16, 16, 256, 256
    CH = 512
    NCH = T // CH

    d_mu = p.din("l0f_mu2d", [D, ES])
    d_w1 = p.din("l0f_w1", [E * D, HD])
    d_w2 = p.din("l0f_w2", [E * HD, D])
    d_gb = p.din("l0f_gb", [128, 2 * NTL])
    d_b1 = p.din("l0f_b1p", [128, 2 * E])
    d_b2 = p.din("l0f_b2p", [128, 8 * E])
    d_sc = p.din("l0f_scale_col", [128, 1])
    d_id = p.din("c_ident", [128, 128], BF16)
    s_xn2 = p.dtmp("s_xn2", [D, T], BF16)
    s_dsp = p.dtmp("s_dsp", [ES, T], BF16)

    mu = p.sb("mu_sb", [128, NTL * ES], F32)
    muS = p.sb("muS_sb", [128, NTL * ES], BF16)
    gb = p.sb("gb_sb", [128, 2 * NTL], F32)
    b1s = p.sb("b1_sb", [128, 2 * E], F32)
    b2s = p.sb("b2_sb", [128, 8 * E], F32)
    scol = p.sb("scol", [128, 1], F32)
    nscol = p.sb("nscol", [128, 1], F32)
    identb = p.sb("identb", [128, 128], BF16)
    xn2 = p.sb("xn2_sb", [128, NTL * T], BF16)
    sqf = p.sb("fsq", [128, NTL * CH], BF16)
    expLs = [p.sb("expL0", [128, T], BF16), p.sb("expL1", [128, T], BF16)]
    dsp = p.sb("dsp", [128, 2 * T], BF16)
    lgt = p.sb("lgt", [128, CH], F32)
    slotin = p.sb("slotin", [128, NTL * ES], BF16)
    rnb = p.sb("rnb", [128, CH], F32)
    w1c = [p.sb("w1c0", [128, 2 * NTL * HD], BF16),
           p.sb("w1c1", [128, 2 * NTL * HD], BF16)]
    w2c = [p.sb("w2c0", [128, 2 * 2 * D], BF16),
           p.sb("w2c1", [128, 2 * 2 * D], BF16)]
    hh = p.sb("hh", [128, 2 * E + 16], BF16)
    soT = p.sb("soT", [128, NTL * ES], BF16)
    soE = p.sb("soE", [128, 2 * D], BF16)
    fout = p.sb("fout2", [128, NTL * CH], BF16)
    zt = p.sb("ztf", [128, 2], F32)
    rLn = p.sb("rLn", [1, T], F32)
    rZe = p.sb("rZe", [1, T], F32)
    fM = mu[0:1, 0:CH]
    fA = mu[0:1, CH:2 * CH]
    fB = mu[0:1, 2 * CH:3 * CH]
    fT = mu[0:1, 3 * CH:4 * CH]

    pS, pQ, pA, pB = p.bank[0], p.bank[1], p.bank[2], p.bank[3]

    ins = p.dma(nc.sync, 'L', out=mu[:, :], in_=d_mu.rearrange("(k q) s -> q k s", q=128))
    for dst, dd in ((gb, d_gb), (b1s, d_b1), (b2s, d_b2), (scol, d_sc), (identb, d_id)):
        ins = p.dma(nc.sync, 'L', out=dst[:, :], in_=dd)
    ld_const = p.cnt['L']
    def load_pair(pr):
        ins = p.dma(nc.gpsimd, 'W', 
            out=w1c[pr % 2][:, :],
            in_=d_w1[pr * 2 * D:(pr + 1) * 2 * D, :].rearrange("(g q) h -> q g h", q=128))
        ins = p.dma(nc.gpsimd, 'W', 
            out=w2c[pr % 2][:, :],
            in_=d_w2[pr * 2 * HD:(pr + 1) * 2 * HD, :].rearrange("(g q) d2 -> q g d2", q=128))
    load_pair(0)
    load_pair(1)
    pair_ld = [p.cnt['W'] - 32, p.cnt['W']] + [0] * 6

    p.wait(nc.scalar, 'L', ld_const)
    ins = nc.scalar.activation(out=nscol[:, :], in_=scol[:, :], func=AF.Identity, scale=-1.0)
    p.inc(ins, 'A')

    p.wait(nc.scalar, 'V')
    for i in range(NTL):
        ins = nc.scalar.activation(out=sqf[:, i * ES:(i + 1) * ES], in_=mu[:, i * ES:(i + 1) * ES],
                                   func=AF.Square)
    p.inc(ins, 'A')
    p.wait(nc.tensor, 'A')
    p.wait(nc.tensor, 'L', p.const_L)
    for i in range(NTL):
        ins = nc.tensor.matmul(pQ[0:1, 0:ES], p.onesc[:, 0:1], sqf[:, i * ES:(i + 1) * ES],
                               start=(i == 0), stop=(i == NTL - 1))
    p.inc(ins, 'T')
    p.wait(nc.scalar, 'T')
    ins = nc.scalar.activation(out=rZe[0:1, 0:ES], in_=pQ[0:1, 0:ES], func=AF.Sqrt)
    p.inc(ins, 'A')
    p.wait(nc.vector, 'A')
    ins = nc.vector.tensor_scalar(out=rZe[0:1, 0:ES], in0=rZe[0:1, 0:ES], scalar1=1e-12, scalar2=None,
                                  op0=OP.max, op1=OP.bypass)
    p.inc(ins, 'V')
    p.wait(nc.vector, 'V')
    ins = nc.vector.reciprocal(out=rZe[0:1, 0:ES], in_=rZe[0:1, 0:ES])
    p.inc(ins, 'V')
    p.wait(nc.tensor, 'V')
    ins = nc.tensor.matmul(pA[:, 0:ES], p.onesr[0:1, :], rZe[0:1, 0:ES], start=True, stop=True)
    p.inc(ins, 'T')
    p.wait(nc.vector, 'T')
    for i in range(NTL):
        ins = nc.vector.tensor_tensor(out=muS[:, i * ES:(i + 1) * ES], in0=mu[:, i * ES:(i + 1) * ES],
                                      in1=pA[:, 0:ES], op=OP.mult)
    p.inc(ins, 'V')
    muS_done = p.cnt['V']

    for c in range(NCH):
        def xsl(i):
            return xs[:, i * T + c * CH: i * T + c * CH + CH]
        p.wait(nc.scalar, 'V')
        emit_ln_stats(p, [xsl(i) for i in range(NTL)], fout, sqf, p.onesc, pS, pQ, CH)
        emit_ln_rows(p, pS, pQ, fM, fA, fB, fT, CH, double_ln=True)
        emit_ln_bcast(p, fA, fB, pA, pB, p.onesr, CH)
        p.wait(nc.vector, 'T')
        for i in range(NTL):
            nc.vector.tensor_tensor(out=lgt[:, :], in0=xsl(i), in1=pA[:, :], op=OP.mult)
            nc.vector.tensor_tensor(out=lgt[:, :], in0=lgt[:, :], in1=pB[:, :], op=OP.add)
            ins = nc.vector.tensor_scalar(out=xn2[:, i * T + c * CH: i * T + c * CH + CH],
                                          in0=lgt[:, :], scalar1=gb[:, i:i + 1],
                                          scalar2=gb[:, NTL + i:NTL + i + 1],
                                          op0=OP.mult, op1=OP.add)
        p.inc(ins, 'V')
        p.bkset(2, 'V')
        p.bkset(3, 'V')
        p.wait(nc.scalar, 'V')
        for i in range(NTL):
            ins = nc.scalar.activation(out=sqf[:, i * CH:(i + 1) * CH],
                                       in_=xn2[:, i * T + c * CH: i * T + c * CH + CH],
                                       func=AF.Square)
        p.inc(ins, 'A')
        p.wait(nc.tensor, 'A')
        for i in range(NTL):
            ins = nc.tensor.matmul(pQ[0:1, :CH], p.onesc[:, 0:1], sqf[:, i * CH:(i + 1) * CH],
                                   start=(i == 0), stop=(i == NTL - 1))
        p.inc(ins, 'T')
        p.wait(nc.scalar, 'T')
        ins = nc.scalar.activation(out=fT[0:1, :], in_=pQ[0:1, :CH], func=AF.Sqrt)
        p.inc(ins, 'A')
        p.wait(nc.vector, 'A')
        ins = nc.vector.tensor_scalar(out=fT[0:1, :], in0=fT[0:1, :], scalar1=1e-12, scalar2=None,
                                      op0=OP.max, op1=OP.bypass)
        p.inc(ins, 'V')
        p.wait(nc.vector, 'V')
        ins = nc.vector.reciprocal(out=fT[0:1, :], in_=fT[0:1, :])
        p.inc(ins, 'V')
        p.wait(nc.vector, 'V')
        ins = nc.vector.tensor_scalar(out=rLn[0:1, c * CH:(c + 1) * CH], in0=fT[0:1, :],
                                      scalar1=scol[0:1, 0:1], scalar2=None,
                                      op0=OP.mult, op1=OP.bypass)
        p.inc(ins, 'V')

    p.dump("xn2", xn2[:, :])
    p.dump("rLn", rLn[0:1, :])
    p.dump("muS", muS[:, :])
    p.wait(nc.sync, 'V')
    for k in range(NTL):
        ins = p.dma(nc.sync, 'L', gate=(k == 0), out=s_xn2[k * 128:(k + 1) * 128, :],
                    in_=xn2[:, k * T:(k + 1) * T])
    xn2_store = p.cnt['L']

    for c in range(NCH):
        p.wait(nc.tensor, 'V')
        ins = nc.tensor.matmul(pA[:, :CH], p.onesr[0:1, :], rLn[0:1, c * CH:(c + 1) * CH],
                               start=True, stop=True)
        p.inc(ins, 'T')
        p.wait(nc.vector, 'T')
        ins = nc.vector.tensor_copy(out=rnb[:, :], in_=pA[:, :CH])
        p.inc(ins, 'V')
        for m in range(2):
            pL = p.bank[4 + m]
            p.bkwait(4 + m)
            for k in range(NTL):
                ins = nc.tensor.matmul(pL[:, :CH], muS[:, k * ES + m * 128: k * ES + (m + 1) * 128],
                                       xn2[:, k * T + c * CH: k * T + c * CH + CH],
                                       start=(k == 0), stop=(k == NTL - 1))
            p.inc(ins, 'T')
            p.wait(nc.vector, 'T')
            p.wait(nc.vector, 'A')
            ins = nc.vector.tensor_tensor(out=lgt[:, :], in0=pL[:, :CH], in1=rnb[:, :], op=OP.mult)
            p.inc(ins, 'V')
            p.bkset(4 + m, 'V')
            p.wait(nc.scalar, 'V')
            ins = nc.scalar.activation(out=expLs[m][:, c * CH: c * CH + CH],
                                       in_=lgt[:, :], func=AF.Exp, bias=nscol[:, 0:1])
            p.inc(ins, 'A')
        p.wait(nc.tensor, 'A')
        for m in range(2):
            ins = nc.tensor.matmul(pB[0:1, :CH], p.onesc[:, 0:1],
                                   expLs[m][:, c * CH: c * CH + CH],
                                   start=(m == 0), stop=(m == 1))
        p.inc(ins, 'T')
        p.wait(nc.vector, 'T')
        ins = nc.vector.reciprocal(out=rZe[0:1, c * CH:(c + 1) * CH], in_=pB[0:1, :CH])
        p.inc(ins, 'V')

    p.dump("expL0", expLs[0][:, :])
    p.dump("expL1", expLs[1][:, :])
    for m in range(2):
        ins = nc.vector.tensor_reduce(out=zt[:, m:m + 1], in_=expLs[m][:, :],
                                      axis=mybir.AxisListType.X, op=OP.add)
        p.inc(ins, 'V')
        p.wait(nc.vector, 'V')
        ins = nc.vector.reciprocal(out=zt[:, m:m + 1], in_=zt[:, m:m + 1])
        p.inc(ins, 'V')
        p.wait(nc.vector, 'V')
        ins = nc.vector.tensor_scalar(out=dsp[:, m * T:(m + 1) * T], in0=expLs[m][:, :],
                                      scalar1=zt[:, m:m + 1], scalar2=None,
                                      op0=OP.mult, op1=OP.bypass)
    p.inc(ins, 'V')
    dsp_done = p.cnt['V']
    for c in range(NCH):
        p.wait(nc.tensor, 'V')
        ins = nc.tensor.matmul(pA[:, :CH], p.onesr[0:1, :], rZe[0:1, c * CH:(c + 1) * CH],
                               start=True, stop=True)
        p.inc(ins, 'T')
        p.wait(nc.vector, 'T')
        for m in range(2):
            ins = nc.vector.tensor_tensor(out=expLs[m][:, c * CH: c * CH + CH],
                                          in0=expLs[m][:, c * CH: c * CH + CH],
                                          in1=pA[:, :CH], op=OP.mult)
        p.inc(ins, 'V')

    p.dump("dsp", dsp[:, :])
    p.dump("cmb0", expLs[0][:, :])
    p.dump("cmb1", expLs[1][:, :])
    p.wait(nc.sync, 'V', dsp_done)
    for m in range(2):
        ins = p.dma(nc.sync, 'L', out=s_dsp[m * 128:(m + 1) * 128, :], in_=dsp[:, m * T:(m + 1) * T])
    dsp_store = p.cnt['L']
    p.wait(nc.sync, 'T')
    p.wait(nc.sync, 'L', xn2_store)
    xtok = xn2
    for tt in range(16):
        ins = p.dma(nc.sync, 'L', gate=(tt == 0), out=xtok[:, tt * D:(tt + 1) * D],
                    in_=s_xn2[:, tt * 128:(tt + 1) * 128], transpose=True)
    p.wait(nc.sync, 'L', dsp_store)
    dspT = dsp
    for tt in range(16):
        ins = p.dma(nc.sync, 'L', gate=(tt == 0), out=dspT[:, tt * ES:(tt + 1) * ES],
                    in_=s_dsp[:, tt * 128:(tt + 1) * 128], transpose=True)
    tok_ld = p.cnt['L']

    p.wait(nc.tensor, 'L', tok_ld)
    for md in range(NTL):
        pSI = p.bank[4 + md % 2]
        p.bkwait(4 + md % 2)
        for tt in range(16):
            ins = nc.tensor.matmul(pSI[:, 0:ES], xtok[:, tt * D + md * 128: tt * D + (md + 1) * 128],
                                   dspT[:, tt * ES:(tt + 1) * ES],
                                   start=(tt == 0), stop=(tt == 15))
        p.inc(ins, 'T')
        p.wait(nc.scalar, 'T')
        ins = nc.scalar.activation(out=slotin[:, md * ES:(md + 1) * ES], in_=pSI[:, 0:ES],
                                   func=AF.Identity)
        p.inc(ins, 'A')
        p.bkset(4 + md % 2, 'A')

    p.wait(nc.gpsimd, 'A')
    p.dump("slotin", slotin[:, :])
    p.dump("xtok", xtok[:, :])
    p.dump("dspT", dspT[:, :])
    slot_ready = p.cnt['A']
    for e in range(E):
        pr, e2 = e // 2, e % 2
        pX = p.bank[6 + e % 2]
        p.wait(nc.tensor, 'A', slot_ready)
        p.wait(nc.tensor, 'W', pair_ld[pr])
        p.bkwait(6 + e % 2)
        wb = w1c[pr % 2]
        for mh in range(2):
            for k in range(NTL):
                ins = nc.tensor.matmul(pX[:, mh * 16:(mh + 1) * 16],
                                       wb[:, e2 * 2048 + k * HD + mh * 128: e2 * 2048 + k * HD + (mh + 1) * 128],
                                       slotin[:, k * ES + e * 16: k * ES + e * 16 + 16],
                                       start=(k == 0), stop=(k == NTL - 1))
        p.inc(ins, 'T')
        p.wait(nc.scalar, 'T')
        for mh in range(2):
            ins = nc.scalar.activation(out=hh[:, mh * 16:(mh + 1) * 16], in_=pX[:, mh * 16:(mh + 1) * 16],
                                       func=AF.Gelu, bias=b1s[:, mh * 16 + e:mh * 16 + e + 1])
        p.inc(ins, 'A')
        wb2 = w2c[pr % 2]
        p.wait(nc.tensor, 'A')
        for md in range(NTL):
            for kh in range(2):
                ins = nc.tensor.matmul(pX[:, 32 + md * 16: 48 + md * 16],
                                       wb2[:, e2 * 2048 + kh * D + md * 128: e2 * 2048 + kh * D + (md + 1) * 128],
                                       hh[:, kh * 16:(kh + 1) * 16],
                                       start=(kh == 0), stop=(kh == 1))
        p.inc(ins, 'T')
        p.wait(nc.scalar, 'T')
        for md in range(NTL):
            ins = nc.scalar.activation(out=soT[:, md * ES + e * 16: md * ES + e * 16 + 16],
                                       in_=pX[:, 32 + md * 16: 48 + md * 16],
                                       func=AF.Identity, bias=b2s[:, md * 16 + e:md * 16 + e + 1])
        p.inc(ins, 'A')
        p.bkset(6 + e % 2, 'A')
        if e2 == 1 and pr + 2 <= 7:
            p.wait(nc.gpsimd, 'T')
            load_pair(pr + 2)
            pair_ld[pr + 2] = p.cnt['W']

    p.wait(nc.gpsimd, 'A')
    p.dump("soT", soT[:, :])
    soT_done = p.cnt['A']
    pTb = p.bank[4].bitcast(BF16)
    p.wait(nc.tensor, 'A', soT_done)
    for mes in range(2):
        for md in range(NTL):
            ins = nc.tensor.transpose(pTb[:, 0:128], soT[:, md * ES + mes * 128: md * ES + (mes + 1) * 128],
                                      identb[:, :])
            p.inc(ins, 'T')
            p.wait(nc.scalar, 'T')
            ins = nc.scalar.activation(out=soE[:, mes * D + md * 128: mes * D + (md + 1) * 128],
                                       in_=pTb[:, 0:128], func=AF.Identity)
            p.inc(ins, 'A')
            p.wait(nc.tensor, 'A')

    p.wait(nc.tensor, 'V')
    for c in range(NCH):
        for md in range(NTL):
            pF = p.bank[4 + md % 2]
            p.bkwait(4 + md % 2)
            for kes in range(2):
                ins = nc.tensor.matmul(pF[:, :CH],
                                       soE[:, kes * D + md * 128: kes * D + (md + 1) * 128],
                                       expLs[kes][:, c * CH: c * CH + CH],
                                       start=(kes == 0), stop=(kes == 1))
            p.inc(ins, 'T')
            p.wait(nc.scalar, 'T')
            ins = nc.scalar.activation(out=fout[:, md * CH:(md + 1) * CH], in_=pF[:, :CH],
                                       func=AF.Identity)
            p.inc(ins, 'A')
            p.bkset(4 + md % 2, 'A')
        p.wait(nc.scalar, 'A')
        p.wait(nc.scalar, 'V')
        for i in range(NTL):
            ins = nc.scalar.activation(out=sqf[:, i * CH:(i + 1) * CH], in_=fout[:, i * CH:(i + 1) * CH],
                                       func=AF.Square)
        p.inc(ins, 'A')
        p.wait(nc.tensor, 'A')
        for i in range(NTL):
            ins = nc.tensor.matmul(pS[0:1, :CH], p.onesc[:, 0:1], fout[:, i * CH:(i + 1) * CH],
                                   start=(i == 0), stop=(i == NTL - 1))
        p.inc(ins, 'T')
        for i in range(NTL):
            ins = nc.tensor.matmul(pQ[0:1, :CH], p.onesc[:, 0:1], sqf[:, i * CH:(i + 1) * CH],
                                   start=(i == 0), stop=(i == NTL - 1))
        p.inc(ins, 'T')
        emit_ln_rows(p, pS, pQ, fM, fA, fB, fT, CH)
        emit_ln_bcast(p, fA, fB, pA, pB, p.onesr, CH)
        p.wait(nc.vector, 'T')
        for i in range(NTL):
            fsl = fout[:, i * CH:(i + 1) * CH]
            xslc = xs[:, i * T + c * CH: i * T + c * CH + CH]
            nc.vector.tensor_tensor(out=fsl, in0=fsl, in1=pA[:, :CH], op=OP.mult)
            nc.vector.tensor_tensor(out=fsl, in0=fsl, in1=pB[:, :CH], op=OP.add)
            ins = nc.vector.tensor_tensor(out=xslc, in0=xslc, in1=fsl, op=OP.add)
        p.inc(ins, 'V')
        p.bkset(2, 'V')
        p.bkset(3, 'V')
        p.wait(nc.scalar, 'V')



def emit_l1_attn(p, xs, last):
    nc = p.nc
    NH, HD2, NPAIR, NB = 16, 64, 8, 8
    LH, NL, LD = 8, 64, 512
    CH = 512
    NCH = T // CH

    d_wq = p.din("l1a_wq", [D, D])
    d_wk = p.din("l1a_wk", [D, D])
    d_wv = p.din("l1a_wv", [D, D])
    d_wg = p.din("l1a_wg3", [3 * D, LD])
    d_row = p.din("l1a_rout", [LD, D])
    d_wout = p.din("l1a_wout", [D, D])
    d_gp = p.din("l1a_gatep", [128, 2 * NH])
    d_a2 = p.din("l1a_a2", [2, 128], BF16)
    d_mask = p.din("l1a_mask", [128, 1024], BF16)
    d_lmask = p.din("l1a_lmask", [64, T], BF16)
    s_x = p.dtmp("s_x_l1", [D, T], F32)
    s_comb = p.dtmp("s_comb", [D, T], BF16)

    xsb = xs.bitcast(BF16)
    xn = xsb[:, 0:NTL * T]
    GQT = xsb[:, NTL * T:NTL * T + 4 * T]
    remT = xsb[:, NTL * T + 4 * T:NTL * T + 8 * T]

    xchunk = p.sb("l1_xchunk", [128, NTL * CH], F32)
    xbc = p.sb("l1_xbc", [128, NTL * CH], BF16)
    sq = p.sb("l1_sq", [128, NTL * CH], BF16)
    wg = p.sb("l1_wg", [128, NTL * LD], BF16)
    wro = p.sb("l1_wro", [128, 4 * D], BF16)
    wpr = [p.sb("l1_wpr0", [128, 3 * D], BF16),
           p.sb("l1_wpr1", [128, 3 * D], BF16)]
    Kp = [p.sb("l1_Kp0", [128, T], BF16), p.sb("l1_Kp1", [128, T], BF16)]
    Qp = [p.sb("l1_Qp0", [128, T], BF16), p.sb("l1_Qp1", [128, T], BF16)]
    Vp = [p.sb("l1_Vp0", [128, 2 * T], BF16), p.sb("l1_Vp1", [128, 2 * T], BF16)]
    locT = p.sb("l1_locT", [128, T], BF16)
    rem2p = p.sb("l1_rem2p", [128, T], BF16)
    combp = p.sb("l1_combp", [128, T], BF16)
    rzs = p.sb("l1_rzs", [64, 512], F32)
    poolT = p.sb("l1_poolT", [128, NTL * NL], F32)
    poolB = p.sb("l1_poolB", [128, NTL * NL], BF16)
    GKT = p.sb("l1_GKT", [128, 4 * NL], BF16)
    GVo = p.sb("l1_GVo", [64, 8 * 128], BF16)
    explsc = p.sb("l1_explsc", [64, T], BF16)
    lrz = p.sb("l1_lrz", [64, CH], F32)

    gpv = p.sb("l1_gpv", [128, 2 * NH], BF16)
    a2b = p.sb("l1_a2b", [2, 128], BF16)
    maskb = p.sb("l1_maskb", [128, 1024], BF16)
    lmaskb = p.sb("l1_lmaskb", [64, T], BF16)
    pstg = [sq[:, 0:1024], sq[:, 1024:2048]]
    glp = xbc[0:2, 0:T]
    grp = xbc[0:2, T:2 * T]
    woutb = [Kp[0], Kp[1], Qp[0], Qp[1]]
    rM = p.sb("l1_rM", [1, CH], F32)
    rA = p.sb("l1_rA", [1, CH], F32)
    rB = p.sb("l1_rB", [1, CH], F32)
    rT = p.sb("l1_rT", [1, CH], F32)

    pS, pQ, pA, pB = p.bank[0], p.bank[1], p.bank[2], p.bank[3]
    bS1, bS2 = p.bank[4], p.bank[5]
    bPV = [p.bank[6], p.bank[7]]

    for dst, dd in ((a2b, d_a2), (maskb, d_mask), (lmaskb, d_lmask)):
        ins = p.dma(nc.sync, 'L', out=dst[:, :], in_=dd)
    ins = p.dma(nc.gpsimd, 'W', out=gpv[:, :], in_=d_gp)
    ins = p.dma(nc.gpsimd, 'W', out=wro[:, :], in_=d_row.rearrange("(k q) o -> q k o", q=128))

    p.wait(nc.gpsimd, 'L')
    p.wait(nc.gpsimd, 'V')
    p.wait(nc.gpsimd, 'A')
    for i in range(NTL):
        ins = p.dma(nc.gpsimd, 'W', gate=(i == 0), out=s_x[i * 128:(i + 1) * 128, :],
                    in_=xs[:, i * T:(i + 1) * T])
    spill_done = p.cnt['W']

    for c in range(NCH):
        p.wait(nc.sync, 'W', spill_done)
        if c > 0:
            p.wait(nc.sync, 'V')
        ins = p.dma(nc.sync, 'L', out=xchunk[:, :],
                                in_=s_x[:, c * CH:(c + 1) * CH].rearrange("(i q) t -> q i t", q=128))
        p.wait(nc.scalar, 'L')
        p.wait(nc.scalar, 'V')
        emit_ln_stats(p, [xchunk[:, i * CH:(i + 1) * CH] for i in range(NTL)], xbc, sq,
                      p.onesc, pS, pQ, CH)
        emit_ln_rows(p, pS, pQ, rM, rA, rB, rT, CH)
        emit_ln_bcast(p, rA, rB, pA, pB, p.onesr, CH)
        p.wait(nc.vector, 'T')
        if c == 0:
            p.wait(nc.vector, 'W', spill_done)
        for i in range(NTL):
            nc.vector.tensor_tensor(out=xchunk[:, i * CH:(i + 1) * CH],
                                    in0=xchunk[:, i * CH:(i + 1) * CH], in1=pA[:, :CH], op=OP.mult)
            ins = nc.vector.tensor_tensor(out=xn[:, i * T + c * CH: i * T + c * CH + CH],
                                          in0=xchunk[:, i * CH:(i + 1) * CH], in1=pB[:, :CH],
                                          op=OP.add)
        p.inc(ins, 'V')
        p.bkset(2, 'V')
        p.bkset(3, 'V')

    p.wait(nc.vector, 'V')
    for i in range(NTL):
        ins = nc.vector.tensor_reduce(out=poolT[:, i * NL:(i + 1) * NL],
                                      in_=xn[:, i * T:(i + 1) * T].rearrange("q (l c) -> q l c", c=32),
                                      axis=mybir.AxisListType.X, op=OP.add)
    p.inc(ins, 'V')
    p.wait(nc.vector, 'V')
    for i in range(NTL):
        ins = nc.vector.tensor_scalar(out=poolB[:, i * NL:(i + 1) * NL],
                                      in0=poolT[:, i * NL:(i + 1) * NL], scalar1=1.0 / 32,
                                      scalar2=None, op0=OP.mult, op1=OP.bypass)
    p.inc(ins, 'V')
    pool_done = p.cnt['V']
    ins = p.dma(nc.gpsimd, 'W', out=wg[:, :], in_=d_wg[0:D, :].rearrange("(k q) o -> q k o", q=128))
    p.wait(nc.tensor, 'W')
    p.wait(nc.tensor, 'V', pool_done)
    for mt in range(4):
        p.bkwait(2)
        for k in range(NTL):
            ins = nc.tensor.matmul(pA[:, 0:NL], wg[:, k * LD + mt * 128: k * LD + (mt + 1) * 128],
                                   poolB[:, k * NL:(k + 1) * NL], start=(k == 0), stop=(k == NTL - 1))
        p.inc(ins, 'T')
        p.wait(nc.scalar, 'T')
        ins = nc.scalar.activation(out=GKT[:, mt * NL:(mt + 1) * NL], in_=pA[:, 0:NL],
                                   func=AF.Identity)
        p.inc(ins, 'A')
        p.bkset(2, 'A')
    p.wait(nc.gpsimd, 'T')
    ins = p.dma(nc.gpsimd, 'W', out=wg[:, :], in_=d_wg[D:2 * D, :].rearrange("(k q) o -> q k o", q=128))
    p.wait(nc.vector, 'A')
    ins = nc.vector.memset(GVo[:, :], 1.0)
    p.inc(ins, 'V')
    p.wait(nc.tensor, 'W')
    p.wait(nc.tensor, 'V')
    p.bkwait(3)
    for k in range(NTL):
        ins = nc.tensor.matmul(pB[0:NL, 0:LD], poolB[:, k * NL:(k + 1) * NL],
                               wg[:, k * LD:(k + 1) * LD], start=(k == 0), stop=(k == NTL - 1))
    p.inc(ins, 'T')
    p.wait(nc.vector, 'T')
    ins = nc.vector.tensor_copy(
        out=GVo[:, :].rearrange("q (l c) -> q l c", c=128)[:, :, 0:64],
        in_=pB[0:NL, 0:LD].rearrange("q (l c) -> q l c", c=64))
    p.inc(ins, 'V')
    p.bkset(3, 'V')
    p.wait(nc.gpsimd, 'T')
    ins = p.dma(nc.gpsimd, 'W', out=wg[:, :], in_=d_wg[2 * D:3 * D, :].rearrange("(k q) o -> q k o", q=128))
    p.wait(nc.tensor, 'W')
    for mt in range(4):
        for c in range(NCH):
            bb = pA if (mt * NCH + c) % 2 == 0 else pB
            p.bkwait(2 + (mt * NCH + c) % 2)
            for k in range(NTL):
                ins = nc.tensor.matmul(bb[:, :CH], wg[:, k * LD + mt * 128: k * LD + (mt + 1) * 128],
                                       xn[:, k * T + c * CH: k * T + c * CH + CH],
                                       start=(k == 0), stop=(k == NTL - 1))
            p.inc(ins, 'T')
            p.wait(nc.scalar, 'T')
            ins = nc.scalar.activation(out=GQT[:, mt * T + c * CH: mt * T + c * CH + CH],
                                       in_=bb[:, :CH], func=AF.Identity)
            p.inc(ins, 'A')
            p.bkset(2 + (mt * NCH + c) % 2, 'A')
    for lh in range(LH):
        mt, half = lh // 2, (lh % 2) * 64
        for c in range(NCH):
            bb = pA if c % 2 == 0 else pB
            p.bkwait(2 + c % 2)
            ins = nc.tensor.matmul(bb[0:64, :CH], GKT[half:half + 64, mt * NL:(mt + 1) * NL],
                                   GQT[half:half + 64, mt * T + c * CH: mt * T + c * CH + CH],
                                   start=True, stop=True)
            p.inc(ins, 'T')
            p.wait(nc.scalar, 'T')
            ins = nc.scalar.activation(out=explsc[:, c * CH:(c + 1) * CH], in_=bb[0:64, :CH],
                                       func=AF.Exp, scale=0.125)
            p.inc(ins, 'A')
            p.bkset(2 + c % 2, 'A')
        p.wait(nc.vector, 'A')
        ins = nc.vector.tensor_tensor(out=explsc[:, :], in0=explsc[:, :], in1=lmaskb[:, :],
                                      op=OP.mult)
        p.inc(ins, 'V')
        for c in range(NCH):
            bb = bS1 if c % 2 == 0 else bS2
            p.bkwait(4 + c % 2)
            p.wait(nc.tensor, 'V')
            ins = nc.tensor.matmul(bb[:, :CH], GVo[:, lh * 128:(lh + 1) * 128],
                                   explsc[:, c * CH:(c + 1) * CH], start=True, stop=True)
            p.inc(ins, 'T')
            p.wait(nc.scalar, 'T')
            ins = nc.scalar.activation(out=lrz[:, :], in_=bb[64:128, :CH], func=AF.Ln)
            p.inc(ins, 'A')
            p.wait(nc.scalar, 'A')
            ins = nc.scalar.activation(out=lrz[:, :], in_=lrz[:, :], func=AF.Exp, scale=-1.0)
            p.inc(ins, 'A')
            p.wait(nc.vector, 'A')
            ins = nc.vector.tensor_tensor(
                out=remT[half:half + 64, mt * T + c * CH: mt * T + c * CH + CH],
                in0=bb[0:64, :CH], in1=lrz[:, :], op=OP.mult)
            p.inc(ins, 'V')
            p.bkset(4 + c % 2, 'V')

    d_w1g = p.din("l1f_W1T", [D, 4096])
    d_w2g = p.din("l1f_W2T", [4096, D])
    gq_free = p.cnt['T']
    gslot = [xs[:, 8192:10240], xs[:, 10240:12288]]
    gtasks = ([(d_w1g[ti * 128:(ti + 1) * 128, h * 2048:(h + 1) * 2048], 2048, 2 * ti + h)
               for ti in range(NTL) for h in range(2)]
              + [(d_w2g[tj * 128:(tj + 1) * 128, :], 1024, 16 + tj) for tj in range(32)])
    gstate = {"n": 0, "ld": {}, "rd": {}}

    def gamma_tick():
        t = gstate["n"]
        if t >= len(gtasks) + 2:
            return
        tr = t - 2
        if 0 <= tr < len(gtasks):
            _, ncols, col = gtasks[tr]
            nc.vector.wait_ge(p.sems['L'], gstate["ld"][tr])
            ins = nc.vector.tensor_reduce(out=p.gacc_g[:, col:col + 1],
                                          in_=gslot[tr % 2][:, 0:ncols],
                                          axis=mybir.AxisListType.X, op=OP.add,
                                          apply_absolute_value=True)
            p.inc(ins, 'G')
            gstate["rd"][tr] = p.cnt['G']
        if t < len(gtasks):
            src, ncols, _ = gtasks[t]
            if t < 2:
                nc.sync.wait_ge(p.sems['T'], gq_free)
            else:
                nc.sync.wait_ge(p.sems['G'], gstate["rd"][t - 2])
            ins = nc.sync.dma_start(out=gslot[t % 2][:, 0:ncols], in_=src)
            p.inc(ins, 'L', 16)
            gstate["ld"][t] = p.cnt['L']
        gstate["n"] += 1

    def load_pair(pr):
        buf = wpr[pr % 2]
        for wi, dd in enumerate((d_wq, d_wk, d_wv)):
            ins = p.dma(nc.gpsimd, 'W', gate=(wi == 0),
                out=buf[:, wi * D:(wi + 1) * D],
                in_=dd[:, pr * 128:(pr + 1) * 128].rearrange("(k q) c -> q k c", q=128))
    load_pair(0)
    pair_ld = [p.cnt['W']] + [0] * 7
    p.combw = 0

    for pr in range(NPAIR):
        if pr + 1 < NPAIR:
            p.wait(nc.gpsimd, 'T')
            load_pair(pr + 1)
            pair_ld[pr + 1] = p.cnt['W']
        gamma_tick()
        buf = wpr[pr % 2]
        kp, qp, vp = Kp[pr % 2], Qp[pr % 2], Vp[pr % 2]
        p.wait(nc.tensor, 'W', pair_ld[pr])
        p.wait(nc.scalar, 'V')
        for which, dstt in ((0, qp), (1, kp)):
            for c in range(NCH):
                bb = pA if c % 2 == 0 else pB
                p.bkwait(2 + c % 2)
                for k in range(NTL):
                    ins = nc.tensor.matmul(bb[:, :CH],
                                           buf[:, which * D + k * 128: which * D + (k + 1) * 128],
                                           xn[:, k * T + c * CH: k * T + c * CH + CH],
                                           start=(k == 0), stop=(k == NTL - 1))
                p.inc(ins, 'T')
                p.wait(nc.vector, 'T')
                ins = nc.vector.tensor_copy(out=dstt[:, c * CH:(c + 1) * CH], in_=bb[:, :CH])
                p.inc(ins, 'V')
                p.bkset(2 + c % 2, 'V')
        gamma_tick()
        p.wait(nc.vector, 'T')
        ins = nc.vector.memset(vp[:, :], 1.0)
        p.inc(ins, 'V')
        for tt in range(16):
            bb = pA if tt % 2 == 0 else pB
            p.bkwait(2 + tt % 2)
            p.wait(nc.tensor, 'V')
            for k in range(NTL):
                ins = nc.tensor.matmul(bb[:, 0:128], xn[:, k * T + tt * 128: k * T + (tt + 1) * 128],
                                       buf[:, 2 * D + k * 128: 2 * D + (k + 1) * 128],
                                       start=(k == 0), stop=(k == NTL - 1))
            p.inc(ins, 'T')
            p.wait(nc.vector, 'T')
            ins = nc.vector.tensor_copy(
                out=vp[:, tt * 256:tt * 256 + 192].rearrange("q (g c) -> q g c", c=64)[:, 0::2, :],
                in_=bb[:, 0:128].rearrange("q (g c) -> q g c", c=64))
            p.inc(ins, 'V')
            p.bkset(2 + tt % 2, 'V')
        gamma_tick()
        qev = p.cnt['V']
        for gi, gdst in ((0, glp), (1, grp)):
            for cc in range(4):
                p.wait(nc.tensor, 'V', qev)
                p.bkwait(4)
                ins = nc.tensor.matmul(bS1[0:2, 0:512],
                                       gpv[:, gi * NH + pr * 2: gi * NH + pr * 2 + 2],
                                       qp[:, cc * 512:(cc + 1) * 512], start=True, stop=True)
                p.inc(ins, 'T')
                p.wait(nc.scalar, 'T')
                ins = nc.scalar.activation(out=gdst[:, cc * 512:(cc + 1) * 512],
                                           in_=bS1[0:2, 0:512], func=AF.Sigmoid)
                p.inc(ins, 'A')
                p.bkset(4, 'A')
        gamma_tick()
        for c in range(NCH):
            p.bkwait(5)
            for kl in range(4):
                ins = nc.tensor.matmul(bS2[:, :CH], wro[:, kl * D + pr * 128: kl * D + (pr + 1) * 128],
                                       remT[:, kl * T + c * CH: kl * T + c * CH + CH],
                                       start=(kl == 0), stop=(kl == 3))
            p.inc(ins, 'T')
            p.wait(nc.scalar, 'T')
            ins = nc.scalar.activation(out=rem2p[:, c * CH:(c + 1) * CH], in_=bS2[:, :CH],
                                       func=AF.Identity)
            p.inc(ins, 'A')
            p.bkset(5, 'A')
        gamma_tick()
        qkv_done = p.cnt['V']
        SLOT_BANKS = [(4, 5), (6, 7), (0, 1)]
        hbs = [(h2, b) for h2 in range(2) for b in range(NB)]
        NP3 = 3
        st_scT = {}
        st_expA = {}
        st_maskV = {}
        st_pvT = {}
        st_rzA = {}

        def lstage_sc(i):
            h2, b = hbs[i]
            hoff = h2 * 64
            bx, by = SLOT_BANKS[i % NP3]
            jts = list(range(2, 4)) if b == 0 else list(range(4))
            p.wait(nc.tensor, 'V', qkv_done)
            p.bkwait(bx, by)
            for jt in jts:
                gt = (b - 1) * 256 + jt * 128
                bb = p.bank[bx] if jt < 2 else p.bank[by]
                ins = nc.tensor.matmul(bb[:, (jt % 2) * 256:(jt % 2) * 256 + 256],
                                       kp[hoff:hoff + 64, gt:gt + 128],
                                       qp[hoff:hoff + 64, b * 256:(b + 1) * 256],
                                       start=True, stop=True)
            p.inc(ins, 'T')
            st_scT[i] = p.cnt['T']

        def lstage_exp(i):
            h2, b = hbs[i]
            bx, by = SLOT_BANKS[i % NP3]
            stg = pstg[i % 2]
            jts = list(range(2, 4)) if b == 0 else list(range(4))
            p.wait(nc.scalar, 'T', st_scT[i])
            if i >= 2:
                p.wait(nc.scalar, 'V', st_maskV.get(i - 2, 0))
            if b != 0:
                ins = nc.scalar.activation(out=stg[:, 0:512], in_=p.bank[bx][:, 0:512],
                                           func=AF.Exp, scale=0.125)
            ins = nc.scalar.activation(out=stg[:, 512:1024], in_=p.bank[by][:, 0:512],
                                       func=AF.Exp, scale=0.125)
            p.inc(ins, 'A')
            st_expA[i] = p.cnt['A']
            p.bkset(bx, 'A')
            p.bkset(by, 'A')

        def lstage_maskpv(i):
            h2, b = hbs[i]
            bx, by = SLOT_BANKS[i % NP3]
            bv = p.bank[bx]
            stg = pstg[i % 2]
            jts = list(range(2, 4)) if b == 0 else list(range(4))
            p.wait(nc.vector, 'A', st_expA[i])
            if b == 0:
                nc.vector.memset(stg[:, 0:512], 0.0)
            ins = nc.vector.tensor_tensor(out=stg[:, :], in0=stg[:, :], in1=maskb[:, :],
                                          op=OP.mult)
            p.inc(ins, 'V')
            st_maskV[i] = p.cnt['V']
            p.wait(nc.tensor, 'V', st_maskV[i])
            for jt in jts:
                gtt = (b - 1) * 2 + jt
                ins = nc.tensor.matmul(bv[:, 0:256],
                                       vp[:, gtt * 256 + h2 * 128: gtt * 256 + h2 * 128 + 128],
                                       stg[:, jt * 256:(jt + 1) * 256],
                                       start=(jt == jts[0]), stop=(jt == jts[-1]))
            p.inc(ins, 'T')
            st_pvT[i] = p.cnt['T']

        def lstage_rzout(i):
            h2, b = hbs[i]
            hoff = h2 * 64
            bx, by = SLOT_BANKS[i % NP3]
            bv = p.bank[bx]
            rz = rzs[:, (i % 2) * 256:(i % 2) * 256 + 256]
            p.wait(nc.scalar, 'T', st_pvT[i])
            nc.scalar.activation(out=rz, in_=bv[64:128, 0:256], func=AF.Ln)
            ins = nc.scalar.activation(out=rz, in_=rz, func=AF.Exp, scale=-1.0)
            p.inc(ins, 'A')
            st_rzA[i] = p.cnt['A']
            p.wait(nc.vector, 'A', st_rzA[i])
            ins = nc.vector.tensor_tensor(out=locT[hoff:hoff + 64, b * 256:(b + 1) * 256],
                                          in0=bv[0:64, 0:256], in1=rz, op=OP.mult)
            p.inc(ins, 'V')
            p.bkset(bx, 'V')

        stages = [lstage_sc, lstage_exp, lstage_maskpv, lstage_rzout]
        NS6 = len(stages)
        for step in range(len(hbs) + NS6 - 1):
            for si in range(NS6 - 1, -1, -1):
                i = step - si
                if 0 <= i < len(hbs):
                    stages[si](i)
        gamma_tick()
        loc_done = p.cnt['V']
        if pr > 0:
            p.wait(nc.vector, 'W', p.combw)
        for c in range(NCH):
            p.bkwait(2, 3)
            ins = nc.tensor.matmul(pA[:, :CH], a2b[:, :], glp[:, c * CH:(c + 1) * CH],
                                   start=True, stop=True)
            ins = nc.tensor.matmul(pB[:, :CH], a2b[:, :], grp[:, c * CH:(c + 1) * CH],
                                   start=True, stop=True)
            p.inc(ins, 'T')
            p.wait(nc.vector, 'T')
            nc.vector.tensor_tensor(out=combp[:, c * CH:(c + 1) * CH],
                                    in0=pA[:, :CH], in1=locT[:, c * CH:(c + 1) * CH], op=OP.mult)
            nc.vector.tensor_tensor(out=rem2p[:, c * CH:(c + 1) * CH],
                                    in0=pB[:, :CH], in1=rem2p[:, c * CH:(c + 1) * CH], op=OP.mult)
            ins = nc.vector.tensor_tensor(out=combp[:, c * CH:(c + 1) * CH],
                                          in0=combp[:, c * CH:(c + 1) * CH],
                                          in1=rem2p[:, c * CH:(c + 1) * CH], op=OP.add)
            p.inc(ins, 'V')
            p.bkset(2, 'V')
            p.bkset(3, 'V')
        p.wait(nc.gpsimd, 'V')
        ins = p.dma(nc.gpsimd, 'W', out=s_comb[pr * 128:(pr + 1) * 128, :], in_=combp[:, :])
        p.combw = p.cnt['W']

    for _ in range(6):
        gamma_tick()
    p.gamma_pre = True
    p.wait(nc.gpsimd, 'T')
    p.wait(nc.gpsimd, 'A')
    for gg in range(4):
        ins = p.dma(nc.gpsimd, 'W', gate=(gg == 0), out=woutb[gg][:, :],
                                  in_=d_wout[gg * 256:(gg + 1) * 256, :]
                                  .rearrange("(k q) o -> q k o", q=128))
    wout_ld = p.cnt['W']
    cc_free = [0, 0]
    for c in range(NCH):
        cc = Vp[c % 2]
        p.wait(nc.sync, 'W', p.combw)
        p.wait(nc.sync, 'T', cc_free[c % 2] if c >= 2 else p.cnt['T'])
        ins = p.dma(nc.sync, 'L', out=cc[:, 0:NTL * CH],
                                in_=s_comb[:, c * CH:(c + 1) * CH].rearrange("(g q) t -> q g t", q=128))
        ldc = p.cnt['L']
        for j in range(NTL):
            bb = bS1 if j % 2 == 0 else bS2
            p.wait(nc.tensor, 'L', ldc)
            p.wait(nc.tensor, 'W', wout_ld)
            p.bkwait(4 + j % 2)
            for g in range(NTL):
                ins = nc.tensor.matmul(bb[:, :CH],
                                       woutb[g // 2][:, (g % 2) * D + j * 128:
                                                     (g % 2) * D + (j + 1) * 128],
                                       cc[:, g * CH:(g + 1) * CH], start=(g == 0), stop=(g == NTL - 1))
            p.inc(ins, 'T')
            p.wait(nc.scalar, 'T')
            if j == 0:
                p.wait(nc.scalar, 'V')
            ins = nc.scalar.activation(out=xbc[:, j * CH:(j + 1) * CH], in_=bb[:, :CH],
                                       func=AF.Identity)
            p.inc(ins, 'A')
            p.bkset(4 + j % 2, 'A')
        cc_free[c % 2] = p.cnt['T']
        p.wait(nc.scalar, 'A')
        for i in range(NTL):
            ins = nc.scalar.activation(out=sq[:, i * CH:(i + 1) * CH], in_=xbc[:, i * CH:(i + 1) * CH],
                                       func=AF.Square)
        p.inc(ins, 'A')
        p.wait(nc.tensor, 'A')
        for i in range(NTL):
            ins = nc.tensor.matmul(pS[0:1, :CH], p.onesc[:, 0:1], xbc[:, i * CH:(i + 1) * CH],
                                   start=(i == 0), stop=(i == NTL - 1))
        p.inc(ins, 'T')
        for i in range(NTL):
            ins = nc.tensor.matmul(pQ[0:1, :CH], p.onesc[:, 0:1], sq[:, i * CH:(i + 1) * CH],
                                   start=(i == 0), stop=(i == NTL - 1))
        p.inc(ins, 'T')
        emit_ln_rows(p, pS, pQ, rM, rA, rB, rT, CH)
        emit_ln_bcast(p, rA, rB, pA, pB, p.onesr, CH)
        p.wait(nc.sync, 'W')
        p.wait(nc.sync, 'V')
        ins = p.dma(nc.sync, 'L', out=xchunk[:, :],
                                in_=s_x[:, c * CH:(c + 1) * CH].rearrange("(i q) t -> q i t", q=128))
        p.wait(nc.vector, 'L')
        p.wait(nc.vector, 'T')
        for i in range(NTL):
            asl = xbc[:, i * CH:(i + 1) * CH]
            nc.vector.tensor_tensor(out=asl, in0=asl, in1=pA[:, :CH], op=OP.mult)
            nc.vector.tensor_tensor(out=asl, in0=asl, in1=pB[:, :CH], op=OP.add)
            ins = nc.vector.tensor_tensor(out=xchunk[:, i * CH:(i + 1) * CH],
                                          in0=xchunk[:, i * CH:(i + 1) * CH], in1=asl, op=OP.add)
        p.inc(ins, 'V')
        p.bkset(2, 'V')
        p.bkset(3, 'V')
        x3_done = p.cnt['V']
        if last:
            p.wait(nc.gpsimd, 'V', x3_done)
            for i in range(NTL):
                ins = p.dma(nc.gpsimd, 'O', gate=(i == 0),
                            out=p.d_y[i * 128:(i + 1) * 128, c * CH:(c + 1) * CH],
                            in_=xchunk[:, i * CH:(i + 1) * CH])
        else:
            p.wait(nc.gpsimd, 'V', x3_done)
            ins = p.dma(nc.gpsimd, 'W', 
                out=s_x[:, c * CH:(c + 1) * CH].rearrange("(i q) t -> q i t", q=128),
                in_=xchunk[:, :])
            p.wait(nc.scalar, 'V', x3_done)
            emit_ln_stats(p, [xchunk[:, i * CH:(i + 1) * CH] for i in range(NTL)], xbc, sq,
                          p.onesc, pS, pQ, CH)
            emit_ln_rows(p, pS, pQ, rM, rA, rB, rT, CH)
            emit_ln_bcast(p, rA, rB, pA, pB, p.onesr, CH)
            p.wait(nc.vector, 'T')
            for i in range(NTL):
                nc.vector.tensor_tensor(out=xbc[:, i * CH:(i + 1) * CH],
                                        in0=xchunk[:, i * CH:(i + 1) * CH], in1=pA[:, :CH],
                                        op=OP.mult)
                ins = nc.vector.tensor_tensor(out=xn[:, i * T + c * CH: i * T + c * CH + CH],
                                              in0=xbc[:, i * CH:(i + 1) * CH], in1=pB[:, :CH],
                                              op=OP.add)
            p.inc(ins, 'V')
            p.bkset(2, 'V')
            p.bkset(3, 'V')
    if last:
        p.output_written = True
    p.x_dram = s_x



def emit_l1_ffn(p, xs, last, from_attn=False):
    nc = p.nc
    FF = 4096
    CH = 256
    NCH = T // CH
    NTOT = float(FF * D)

    d_w1t = p.din("l1f_W1T", [D, FF])
    d_w2t = p.din("l1f_W2T", [FF, D])
    d_b1 = p.din("l1f_b1p", [128, 32])
    d_b2 = p.din("l1f_b2p", [128, NTL])
    d_of = p.din("c_ones_f32col", [128, 1])
    s_x = p.x_dram if from_attn else p.dtmp("s_x_l1f", [D, T], F32)

    xsb = xs.bitcast(BF16)
    xn = xsb[:, 0:NTL * T]
    harea = xsb[:, 16384:24576]
    fout = xsb[:, 24576:26624]
    sq = xsb[:, 26624:28672]
    xchunk = xs[:, 14336:16384]
    wtmpB = [xs[:, 8192:12288], xs[:, 12288:16384]]
    wtmp = wtmpB[0]

    wq1 = p.sb("l1f_wq1", [128, NTL * FF], BF16)
    wq2 = p.sb("l1f_wq2", [128, 32 * D], BF16)
    b1s = p.sb("l1f_b1s", [128, 32], F32)
    b2s = p.sb("l1f_b2s", [128, NTL], F32)
    onef = p.sb("l1f_onef", [128, 1], F32)
    gacc = p.sb("l1f_gacc", [128, 40], F32)
    gth = p.sb("l1f_gth", [128, 4], F32)
    grow = p.sb("l1f_grow", [1, 4], F32)
    qtmp = p.sb("l1f_qtmp", [128, 4096], BF16)
    rM = p.sb("l1f_rM", [1, CH], F32)
    rA = p.sb("l1f_rA", [1, CH], F32)
    rB = p.sb("l1f_rB", [1, CH], F32)
    rT = p.sb("l1f_rT", [1, CH], F32)

    pS, pQ, pA, pB = p.bank[0], p.bank[1], p.bank[2], p.bank[3]
    bM1 = [p.bank[4], p.bank[5]]
    bM2 = [p.bank[6], p.bank[7]]

    for dst, dd in ((b1s, d_b1), (b2s, d_b2), (onef, d_of)):
        ins = p.dma(nc.sync, 'L', out=dst[:, :], in_=dd)

    if not from_attn:
        p.wait(nc.gpsimd, 'L')
        p.wait(nc.gpsimd, 'V')
        p.wait(nc.gpsimd, 'A')
        for i in range(NTL):
            ins = p.dma(nc.gpsimd, 'W', gate=(i == 0), out=s_x[i * 128:(i + 1) * 128, :],
                        in_=xs[:, i * T:(i + 1) * T])
        spill_done = p.cnt['W']
    for c in range(NCH if not from_attn else 0):
        p.wait(nc.sync, 'W', spill_done)
        if c > 0:
            p.wait(nc.sync, 'V')
        ins = p.dma(nc.sync, 'L', out=xchunk,
                    in_=s_x[:, c * CH:(c + 1) * CH].rearrange("(i q) t -> q i t", q=128))
        p.wait(nc.scalar, 'L')
        p.wait(nc.scalar, 'V')
        if c == 0:
            p.wait(nc.scalar, 'W', spill_done)
        emit_ln_stats(p, [xchunk[:, i * CH:(i + 1) * CH] for i in range(NTL)], fout, sq,
                      p.onesc, pS, pQ, CH)
        emit_ln_rows(p, pS, pQ, rM, rA, rB, rT, CH)
        emit_ln_bcast(p, rA, rB, pA, pB, p.onesr, CH)
        p.wait(nc.vector, 'T')
        if c == 0:
            p.wait(nc.vector, 'W', spill_done)
        for i in range(NTL):
            nc.vector.tensor_tensor(out=xchunk[:, i * CH:(i + 1) * CH],
                                    in0=xchunk[:, i * CH:(i + 1) * CH], in1=pA[:, :CH], op=OP.mult)
            ins = nc.vector.tensor_tensor(out=xn[:, i * T + c * CH: i * T + c * CH + CH],
                                          in0=xchunk[:, i * CH:(i + 1) * CH], in1=pB[:, :CH],
                                          op=OP.add)
        p.inc(ins, 'V')

    red_done = [0, 0]
    gi = 0
    gpre = getattr(p, 'gamma_pre', False)
    for wi, (dd, ntile, wcols, coff) in enumerate(
            () if gpre else ((d_w1t, NTL, FF, 0), (d_w2t, 32, D, 8))):
        for ti in range(ntile):
            par = gi % 2
            wb = wtmpB[par]
            qeng, qsem = (nc.sync, 'L') if par == 0 else (nc.gpsimd, 'W')
            if gi >= 2:
                p.wait(qeng, 'V', red_done[par])
            p.dma(qeng, qsem, gate=False, out=wb[:, 0:wcols], in_=dd[ti * 128:(ti + 1) * 128, :])
            p.wait(nc.vector, 'L')
            p.wait(nc.vector, 'W')
            ins = nc.vector.tensor_reduce(out=gacc[:, coff + ti:coff + ti + 1], in_=wb[:, 0:wcols],
                                          axis=mybir.AxisListType.X, op=OP.add,
                                          apply_absolute_value=True)
            p.inc(ins, 'V')
            red_done[par] = p.cnt['V']
            gi += 1
    gsrc, nga, c1 = ((p.gacc_g, 48, 16) if gpre else (gacc, 40, 8))
    p.wait(nc.tensor, 'V')
    p.wait(nc.tensor, 'L')
    ins = nc.tensor.matmul(pA[0:1, 0:nga], onef[:, 0:1], gsrc[:, 0:nga], start=True, stop=True)
    p.inc(ins, 'T')
    p.wait(nc.vector, 'T')
    ins = nc.vector.tensor_reduce(out=grow[0:1, 0:1], in_=pA[0:1, 0:c1],
                                  axis=mybir.AxisListType.X, op=OP.add)
    ins = nc.vector.tensor_reduce(out=grow[0:1, 1:2], in_=pA[0:1, c1:nga],
                                  axis=mybir.AxisListType.X, op=OP.add)
    p.inc(ins, 'V')
    p.wait(nc.vector, 'V')
    nc.vector.tensor_scalar(out=grow[0:1, 2:4], in0=grow[0:1, 0:2], scalar1=-0.5 / NTOT,
                            scalar2=-0.5e-5, op0=OP.mult, op1=OP.add)
    ins = nc.vector.tensor_scalar(out=grow[0:1, 0:2], in0=grow[0:1, 0:2], scalar1=0.5 / NTOT,
                                  scalar2=0.5e-5, op0=OP.mult, op1=OP.add)
    p.inc(ins, 'V')
    p.wait(nc.tensor, 'V')
    ins = nc.tensor.matmul(pB[:, 0:4], p.onesr[0:1, :], grow[0:1, 0:4], start=True, stop=True)
    p.inc(ins, 'T')
    p.wait(nc.vector, 'T')
    ins = nc.vector.tensor_copy(out=gth[:, :], in_=pB[:, 0:4])
    p.inc(ins, 'V')
    p.wait(nc.vector, 'V')

    w2tmpB = [xs[:, 12288:13312], xs[:, 13312:14336]]
    for wi, (dd, ntile, wcols, wdst, dstw) in enumerate(
            ((d_w1t, NTL, FF, wq1, FF), (d_w2t, 32, D, wq2, D))):
        thc = gth[:, wi:wi + 1]
        nthc = gth[:, 2 + wi:3 + wi]
        bufs = wtmpB if wi == 0 else w2tmpB
        for ti in range(ntile):
            par = gi % 2
            wb = bufs[par]
            qeng, qsem = (nc.sync, 'L') if par == 0 else (nc.gpsimd, 'W')
            p.wait(qeng, 'V', wq1_done if (wi == 1 and ti < 2) else red_done[par])
            p.dma(qeng, qsem, gate=False, out=wb[:, 0:wcols], in_=dd[ti * 128:(ti + 1) * 128, :])
            p.wait(nc.vector, 'L')
            p.wait(nc.vector, 'W')
            nc.vector.tensor_scalar(out=qtmp[:, 0:wcols], in0=wb[:, 0:wcols], scalar1=thc,
                                    scalar2=-1.0, op0=OP.is_ge, op1=OP.add)
            ins = nc.vector.scalar_tensor_tensor(out=wdst[:, ti * dstw:(ti + 1) * dstw],
                                                 in0=wb[:, 0:wcols], scalar=nthc,
                                                 in1=qtmp[:, 0:wcols],
                                                 op0=OP.is_ge, op1=OP.add)
            p.inc(ins, 'V')
            red_done[gi % 2] = p.cnt['V']
            gi += 1
        if wi == 0:
            wq1_done = p.cnt['V']

    wq2_done = p.cnt['V']
    for c in range(NCH):
        for mf in range(32):
            bb = bM1[mf % 2]
            p.wait(nc.tensor, 'V', wq1_done)
            p.bkwait(4 + mf % 2)
            for k in range(NTL):
                ins = nc.tensor.matmul(bb[:, :CH], wq1[:, k * FF + mf * 128: k * FF + (mf + 1) * 128],
                                       xn[:, k * T + c * CH: k * T + c * CH + CH],
                                       start=(k == 0), stop=(k == NTL - 1))
            p.inc(ins, 'T')
            p.wait(nc.scalar, 'T')
            ins = nc.scalar.activation(out=harea[:, mf * CH:(mf + 1) * CH], in_=bb[:, :CH],
                                       func=AF.Relu, bias=b1s[:, mf:mf + 1])
            p.inc(ins, 'A')
            p.bkset(4 + mf % 2, 'A')
        h_done = p.cnt['A']
        for md in range(NTL):
            bb = bM2[md % 2]
            p.wait(nc.tensor, 'V', wq2_done)
            p.wait(nc.tensor, 'A', h_done)
            p.bkwait(6 + md % 2)
            for kf in range(32):
                ins = nc.tensor.matmul(bb[:, :CH], wq2[:, kf * D + md * 128: kf * D + (md + 1) * 128],
                                       harea[:, kf * CH:(kf + 1) * CH],
                                       start=(kf == 0), stop=(kf == 31))
            p.inc(ins, 'T')
            p.wait(nc.scalar, 'T')
            ins = nc.scalar.activation(out=fout[:, md * CH:(md + 1) * CH], in_=bb[:, :CH],
                                       func=AF.Identity, bias=b2s[:, md:md + 1])
            p.inc(ins, 'A')
            p.bkset(6 + md % 2, 'A')
        p.wait(nc.scalar, 'A')
        p.wait(nc.scalar, 'V')
        for i in range(NTL):
            ins = nc.scalar.activation(out=sq[:, i * CH:(i + 1) * CH], in_=fout[:, i * CH:(i + 1) * CH],
                                       func=AF.Square)
        p.inc(ins, 'A')
        p.wait(nc.tensor, 'A')
        for i in range(NTL):
            ins = nc.tensor.matmul(pS[0:1, :CH], p.onesc[:, 0:1], fout[:, i * CH:(i + 1) * CH],
                                   start=(i == 0), stop=(i == NTL - 1))
        p.inc(ins, 'T')
        for i in range(NTL):
            ins = nc.tensor.matmul(pQ[0:1, :CH], p.onesc[:, 0:1], sq[:, i * CH:(i + 1) * CH],
                                   start=(i == 0), stop=(i == NTL - 1))
        p.inc(ins, 'T')
        emit_ln_rows(p, pS, pQ, rM, rA, rB, rT, CH)
        emit_ln_bcast(p, rA, rB, pA, pB, p.onesr, CH)
        p.wait(nc.sync, 'T')
        p.wait(nc.sync, 'O')
        ins = p.dma(nc.sync, 'L', out=xchunk,
                    in_=s_x[:, c * CH:(c + 1) * CH].rearrange("(i q) t -> q i t", q=128))
        p.wait(nc.vector, 'L')
        p.wait(nc.vector, 'T')
        for i in range(NTL):
            fsl = fout[:, i * CH:(i + 1) * CH]
            nc.vector.tensor_tensor(out=fsl, in0=fsl, in1=pA[:, :CH], op=OP.mult)
            nc.vector.tensor_tensor(out=fsl, in0=fsl, in1=pB[:, :CH], op=OP.add)
            ins = nc.vector.tensor_tensor(out=xchunk[:, i * CH:(i + 1) * CH],
                                          in0=xchunk[:, i * CH:(i + 1) * CH], in1=fsl, op=OP.add)
        p.inc(ins, 'V')
        p.bkset(2, 'V')
        p.bkset(3, 'V')
        p.wait(nc.gpsimd, 'V')
        ins = p.dma(nc.gpsimd, 'O',
                    out=p.d_y[:, c * CH:(c + 1) * CH].rearrange("(i q) t -> q i t", q=128),
                    in_=xchunk)
    p.output_written = True



EMITTERS = {}


def _register():
    EMITTERS["l0_attn"] = lambda p, xs, last: emit_l0_attn(p, xs)
    EMITTERS["l0_ffn"] = lambda p, xs, last: emit_l0_ffn(p, xs)
    EMITTERS["l1_attn"] = emit_l1_attn
    EMITTERS["l1_ffn"] = emit_l1_ffn


_register()


def build_program(block_names):
    p = Prog()
    nc = p.nc
    with p.es:
        d_x = p.din("xT", [D, T])
        d_y = p.dout("yT", [D, T])
        d_onesb = p.din("c_ones_bf", [128, 1], BF16)
        d_onesf = p.din("c_ones_f", [1, 128], F32)
        d_eps = p.din("c_eps", [128, 1], F32)
        xs = p.sbuf("xs", [128, NTL * T], F32)
        p.big = p.sbuf("big_arena", [128, 72960], BF16)
        p.big_f = p.big.bitcast(F32)
        p.bank = [p.psum("bank%d" % i, [128, 512]) for i in range(8)]
        p.bk = [(None, 0)] * 8
        p.onesc = p.sbuf("onesc_bf", [128, 1], BF16)
        p.onesr = p.sbuf("onesr_f", [1, 128], F32)
        p.epscol = p.sbuf("epscol", [128, 1], F32)
        nc.const_aps.register(F32, LN_EPS, p.epscol[:, 0:1])
        p.onecf = p.sbuf("onecf", [128, 1], F32)
        nc.const_aps.register(F32, 1.0, p.onecf[:, 0:1])
        p.gacc_g = p.sbuf("gacc_g", [128, 48], F32)
        p.sem('L', 'W', 'T', 'V', 'A', 'G', 'O')
        with nc.Block():
            ins = p.dma(nc.sync, 'L', out=p.onesc[:, :], in_=d_onesb)
            ins = p.dma(nc.sync, 'L', out=p.onesr[:, :], in_=d_onesf)
            ins = p.dma(nc.sync, 'L', out=p.epscol[:, :], in_=d_eps)
            ins = p.dma(nc.sync, 'L', out=p.onecf[:, :], in_=p.din("c_ones_f32col", [128, 1]))
            p.const_L = p.cnt['L']
            for i in range(NTL):
                ins = p.dma(nc.sync, 'L', gate=(i == 0), out=xs[:, i * T:(i + 1) * T],
                                        in_=d_x[i * 128:(i + 1) * 128, :])
            p.output_written = False
            p.d_y = d_y
            for bi, b in enumerate(block_names):
                p.carve_reset()
                if bi > 0:
                    p.barrier()
                    if getattr(p, "debug", False):
                        p.dump("xs_after_%d" % bi, xs[:, :])
                if b == "l1_ffn" and bi > 0 and block_names[bi - 1] == "l1_attn":
                    emit_l1_ffn(p, xs, True, from_attn=True)
                else:
                    EMITTERS[b](p, xs, bi == len(block_names) - 1)
            if not p.output_written:
                p.wait(nc.gpsimd, 'V')
                p.wait(nc.gpsimd, 'A')
                for i in range(NTL):
                    ins = p.dma(nc.gpsimd, 'O', gate=(i == 0), out=d_y[i * 128:(i + 1) * 128, :],
                                              in_=xs[:, i * T:(i + 1) * T])
            p.wait(nc.gpsimd, 'O')
    return p



def _pack_vec(v):
    return np.ascontiguousarray(np.asarray(v, dtype=np.float32).reshape(NTL, 128).T)


def _host_tensors(inputs):
    t = {}
    vecs = np.zeros((128, 7 * NTL), np.float32)
    vecs[:, 0:NTL] = _pack_vec(inputs["L0_attn_time_decay"])
    vecs[:, NTL:2 * NTL] = _pack_vec(inputs["L0_attn_time_first"])
    vecs[:, 2 * NTL:3 * NTL] = _pack_vec(np.asarray(inputs["L0_attn_time_mix_key"]).ravel())
    vecs[:, 3 * NTL:4 * NTL] = _pack_vec(np.asarray(inputs["L0_attn_time_mix_value"]).ravel())
    vecs[:, 4 * NTL:5 * NTL] = _pack_vec(np.asarray(inputs["L0_attn_time_mix_receptance"]).ravel())
    vecs[:, 5 * NTL:6 * NTL] = _pack_vec(inputs["L0_attn_bo"])
    t["l0a_vecs"] = vecs
    t["l0a_WkT"] = np.ascontiguousarray(np.asarray(inputs["L0_attn_Wk"], np.float32).T)
    t["l0a_WvT"] = np.ascontiguousarray(np.asarray(inputs["L0_attn_Wv"], np.float32).T)
    t["l0a_WrT"] = np.ascontiguousarray(np.asarray(inputs["L0_attn_Wr"], np.float32).T)
    t["l0a_WoT"] = np.ascontiguousarray(np.asarray(inputs["L0_attn_Wo"], np.float32).T)
    E, S, HD, ES = 16, 16, 256, 256
    t["l0f_mu2d"] = np.ascontiguousarray(np.asarray(inputs["L0_ffn_mu"], np.float32).reshape(D, ES))
    t["l0f_w1"] = np.ascontiguousarray(np.asarray(inputs["L0_ffn_w1"], np.float32).reshape(E * D, HD))
    t["l0f_w2"] = np.ascontiguousarray(np.asarray(inputs["L0_ffn_w2"], np.float32).reshape(E * HD, D))
    gbp = np.zeros((128, 2 * NTL), np.float32)
    gbp[:, 0:NTL] = _pack_vec(inputs["L0_ffn_ln_gamma"])
    gbp[:, NTL:2 * NTL] = _pack_vec(inputs["L0_ffn_ln_beta"])
    t["l0f_gb"] = gbp
    b1 = np.asarray(inputs["L0_ffn_b1"], np.float32)
    b1p = np.zeros((128, 2 * E), np.float32)
    for mh in range(2):
        for e in range(E):
            b1p[:, mh * 16 + e] = b1[e, mh * 128:(mh + 1) * 128]
    t["l0f_b1p"] = b1p
    b2 = np.asarray(inputs["L0_ffn_b2"], np.float32)
    b2p = np.zeros((128, 8 * E), np.float32)
    for md in range(8):
        for e in range(E):
            b2p[:, md * 16 + e] = b2[e, md * 128:(md + 1) * 128]
    t["l0f_b2p"] = b2p
    t["l0f_scale_col"] = np.broadcast_to(np.asarray(inputs["L0_ffn_scale"], np.float32).reshape(1, 1),
                                         (128, 1)).copy()
    t["c_ident"] = np.eye(128, dtype=np.float32).astype(BF)
    t["l1a_wq"] = np.ascontiguousarray(np.asarray(inputs["L1_attn_wq"], np.float32))
    t["l1a_wk"] = np.ascontiguousarray(np.asarray(inputs["L1_attn_wk"], np.float32))
    t["l1a_wv"] = np.ascontiguousarray(np.asarray(inputs["L1_attn_wv"], np.float32))
    t["l1a_wg3"] = np.ascontiguousarray(np.concatenate([
        np.asarray(inputs["L1_attn_wgk"], np.float32),
        np.asarray(inputs["L1_attn_wgv"], np.float32),
        np.asarray(inputs["L1_attn_wgq"], np.float32)], axis=0))
    t["l1a_rout"] = np.ascontiguousarray(np.asarray(inputs["L1_attn_r_out_w"], np.float32))
    t["l1a_wout"] = np.ascontiguousarray(np.asarray(inputs["L1_attn_w_out"], np.float32))
    gl_w = np.asarray(inputs["L1_attn_gate_local_w"], np.float32)
    gr_w = np.asarray(inputs["L1_attn_gate_remote_w"], np.float32)
    gp = np.zeros((128, 32), np.float32)
    for gi, gw in ((0, gl_w), (1, gr_w)):
        for pr in range(8):
            for h2 in range(2):
                gp[h2 * 64:(h2 + 1) * 64, gi * 16 + pr * 2 + h2] = gw[pr * 2 + h2]
    t["l1a_gatep"] = gp
    a2 = np.zeros((2, 128), np.float32)
    a2[0, 0:64] = 1.0
    a2[1, 64:128] = 1.0
    t["l1a_a2"] = a2.astype(BF)
    jj = np.arange(512)[:, None]
    ii = np.arange(256)[None, :]
    allowed = (jj > ii) & (jj <= ii + 256)
    mk = np.zeros((128, 1024), np.float32)
    for jt in range(4):
        mk[:, jt * 256:(jt + 1) * 256] = allowed[jt * 128:(jt + 1) * 128, :]
    t["l1a_mask"] = mk.astype(BF)
    lm = (np.arange(64)[:, None] <= (np.arange(2048) // 32)[None, :]).astype(np.float32)
    t["l1a_lmask"] = lm.astype(BF)
    t["l1f_W1T"] = np.ascontiguousarray(np.asarray(inputs["L1_ffn_W1"], np.float32).T)
    t["l1f_W2T"] = np.ascontiguousarray(np.asarray(inputs["L1_ffn_W2"], np.float32).T)
    b1f = np.asarray(inputs["L1_ffn_b1"], np.float32)
    t["l1f_b1p"] = np.ascontiguousarray(b1f.reshape(32, 128).T)
    b2f = np.asarray(inputs["L1_ffn_b2"], np.float32)
    t["l1f_b2p"] = np.ascontiguousarray(b2f.reshape(NTL, 128).T)
    t["c_ones_f32col"] = np.ones((128, 1), np.float32)
    return t


def kernel(**inputs):
    x = np.asarray(inputs["L0_attn_x"], np.float32)
    host = _host_tensors(inputs)
    host["c_ones_bf"] = np.ones((128, 1), BF)
    host["c_ones_f"] = np.ones((1, 128), np.float32)
    host["c_eps"] = np.full((128, 1), LN_EPS, np.float32)
    xT = [np.ascontiguousarray(x[b].T) for b in range(8)]
    for blocks in LAUNCH_PLAN:
        p = build_program(blocks)
        in_maps = []
        for b in range(8):
            m = {"xT": xT[b]}
            for name in p.inputs_used:
                if name != "xT":
                    m[name] = host[name]
            in_maps.append(m)
        res = run_bass_kernel_spmd(p.nc, in_maps, core_ids=list(range(8)))
        xT = [np.asarray(res.results[b]["yT"], np.float32) for b in range(8)]
    out = np.stack([np.ascontiguousarray(xT[b].T) for b in range(8)], axis=0)
    return out


# version 22 (final)
# speedup vs baseline: 1.0510x; 1.0081x over previous
import numpy as np
import ml_dtypes
from contextlib import ExitStack
import concourse.bass as bass
import concourse.mybir as mybir
from concourse.bass_utils import run_bass_kernel_spmd
from concourse.alu_op_type import AluOpType as OP

F32 = mybir.dt.float32
BF16 = mybir.dt.bfloat16
AF = mybir.ActivationFunctionType
BF = ml_dtypes.bfloat16

T = 2048
D = 1024
NTL = 8
SEG = 512
NSEG = T // SEG
LN_EPS = 1e-5

LAUNCH_PLAN = [["l0_attn", "l0_ffn", "l1_attn", "l1_ffn"]]


class Prog:

    def __init__(self):
        self.nc = bass.Bass("TRN2", target_bir_lowering=False)
        self.es = ExitStack()
        self.sems = {}
        self.cnt = {}
        self.inputs_used = []

    def sem(self, *names):
        for n in names:
            self.sems[n] = self.es.enter_context(self.nc.semaphore(n))
            self.cnt[n] = 0

    def inc(self, instr, name, n=1):
        instr.then_inc(self.sems[name], n)
        self.cnt[name] += n

    def wait(self, eng, name, val=None):
        eng.wait_ge(self.sems[name], self.cnt[name] if val is None else val)

    def dma(self, eng, sem, gate=True, **kw):
        if gate and self.cnt[sem] > 0:
            eng.wait_ge(self.sems[sem], self.cnt[sem])
        ins = eng.dma_start(**kw)
        self.inc(ins, sem, 16)
        return ins

    def sbuf(self, name, shape, dtype):
        return self.es.enter_context(self.nc.sbuf_tensor(name, shape, dtype))

    def carve_reset(self):
        self.car = 0

    def sb(self, name, shape, dtype):
        cols = shape[1]
        self.car = (self.car + 15) & ~15
        if dtype == BF16:
            v = self.big[:, self.car:self.car + cols]
            self.car += cols
        else:
            assert dtype == F32
            o2 = self.car // 2
            v = self.big_f[:, o2:o2 + cols]
            self.car += 2 * cols
        assert self.car <= self.big.shape[1], f"arena overflow at {name}: {self.car}"
        if shape[0] != 128:
            v = v[0:shape[0], :]
        return v

    def bkwait(self, *idxs):
        for i in idxs:
            s, v = self.bk[i]
            if s is not None:
                self.wait(self.nc.tensor, s, v)

    def bkset(self, idx, sem):
        self.bk[idx] = (sem, self.cnt[sem])

    def barrier(self):
        nc = self.nc
        for eng in (nc.tensor, nc.vector, nc.scalar, nc.gpsimd, nc.sync):
            for s in self.sems:
                eng.wait_ge(self.sems[s], self.cnt[s])

    def psum(self, name, shape, dtype=F32):
        return self.es.enter_context(self.nc.psum_tensor(name, shape, dtype))

    def din(self, name, shape, dtype=F32):
        if not hasattr(self, "_dins"):
            self._dins = {}
        if name not in self._dins:
            self.inputs_used.append(name)
            self._dins[name] = self.nc.dram_tensor(name, shape, dtype, kind="ExternalInput").ap()
        return self._dins[name]

    def dout(self, name, shape, dtype=F32):
        return self.nc.dram_tensor(name, shape, dtype, kind="ExternalOutput").ap()

    def dtmp(self, name, shape, dtype):
        return self.nc.dram_tensor(name, shape, dtype, kind="Internal").ap()

    def dump(self, name, ap_2d):
        if not getattr(self, "debug", False):
            return
        w = ap_2d.shape[-1]
        d = self.nc.dram_tensor("dbg_" + name, [ap_2d.shape[0], w], ap_2d.dtype,
                                kind="ExternalOutput").ap()
        self.wait(self.nc.gpsimd, 'V')
        self.wait(self.nc.gpsimd, 'A')
        self.wait(self.nc.gpsimd, 'T')
        self.dma(self.nc.gpsimd, 'O', out=d, in_=ap_2d)



def emit_ln_stats(p, src_f32_slices, xb, sq, ones_col_bf, pS, pQ, seg_w, bf_direct=False):
    nc = p.nc
    if bf_direct:
        for i in range(NTL):
            ins = nc.scalar.activation(out=xb[:, i * seg_w:(i + 1) * seg_w],
                                       in_=src_f32_slices[i], func=AF.Square)
        s_srcs = src_f32_slices
        q_srcs = [xb[:, i * seg_w:(i + 1) * seg_w] for i in range(NTL)]
    else:
        for i in range(NTL):
            nc.scalar.activation(out=xb[:, i * seg_w:(i + 1) * seg_w], in_=src_f32_slices[i],
                                 func=AF.Identity)
            ins = nc.scalar.activation(out=sq[:, i * seg_w:(i + 1) * seg_w], in_=src_f32_slices[i],
                                       func=AF.Square)
        s_srcs = [xb[:, i * seg_w:(i + 1) * seg_w] for i in range(NTL)]
        q_srcs = [sq[:, i * seg_w:(i + 1) * seg_w] for i in range(NTL)]
    p.inc(ins, 'A')
    p.wait(nc.tensor, 'A')
    p.wait(nc.tensor, 'L', p.const_L)
    for i in range(NTL):
        ins = nc.tensor.matmul(pS[0:1, :seg_w], ones_col_bf[:, 0:1], s_srcs[i],
                               start=(i == 0), stop=(i == NTL - 1))
    p.inc(ins, 'T')
    for i in range(NTL):
        ins = nc.tensor.matmul(pQ[0:1, :seg_w], ones_col_bf[:, 0:1], q_srcs[i],
                               start=(i == 0), stop=(i == NTL - 1))
    p.inc(ins, 'T')


def emit_ln_rows(p, pS, pQ, rM, rA, rB, rT, seg_w, double_ln=False):
    nc = p.nc

    def af(ins):
        p.inc(ins, 'A')
        p.wait(nc.scalar, 'A')

    def vf(ins):
        p.inc(ins, 'V')
        p.wait(nc.vector, 'V')

    p.wait(nc.scalar, 'T')
    nc.scalar.activation(out=rM[0:1, :seg_w], in_=pS[0:1, :seg_w], func=AF.Identity,
                         scale=-1.0 / D)
    nc.scalar.activation(out=rT[0:1, :seg_w], in_=pS[0:1, :seg_w], func=AF.Square,
                         scale=1.0 / D)
    ins = nc.scalar.activation(out=rA[0:1, :seg_w], in_=pQ[0:1, :seg_w], func=AF.Identity,
                               scale=1.0 / D)
    af(ins)
    p.wait(nc.vector, 'A')
    vf(nc.vector.tensor_tensor(out=rA[0:1, :seg_w], in0=rA[0:1, :seg_w], in1=rT[0:1, :seg_w],
                               op=OP.subtract))
    p.wait(nc.scalar, 'V')
    if not double_ln:
        nc.scalar.activation(out=rT[0:1, :seg_w], in_=rA[0:1, :seg_w], func=AF.Ln,
                             bias=LN_EPS)
        af(nc.scalar.activation(out=rA[0:1, :seg_w], in_=rT[0:1, :seg_w], func=AF.Exp,
                                scale=-0.5))
        p.wait(nc.vector, 'A')
    else:
        nc.scalar.activation(out=rT[0:1, :seg_w], in_=rA[0:1, :seg_w], func=AF.Ln,
                             bias=LN_EPS)
        af(nc.scalar.activation(out=rT[0:1, :seg_w], in_=rT[0:1, :seg_w], func=AF.Exp,
                                scale=-0.5))
        p.wait(nc.vector, 'A')
        vf(nc.vector.tensor_tensor(out=rA[0:1, :seg_w], in0=rA[0:1, :seg_w], in1=rT[0:1, :seg_w],
                                   op=OP.mult))
        vf(nc.vector.tensor_tensor(out=rA[0:1, :seg_w], in0=rA[0:1, :seg_w], in1=rT[0:1, :seg_w],
                                   op=OP.mult))
        p.wait(nc.scalar, 'V')
        nc.scalar.activation(out=rA[0:1, :seg_w], in_=rA[0:1, :seg_w], func=AF.Ln,
                             bias=LN_EPS)
        af(nc.scalar.activation(out=rA[0:1, :seg_w], in_=rA[0:1, :seg_w], func=AF.Exp,
                                scale=-0.5))
        p.wait(nc.vector, 'A')
        vf(nc.vector.tensor_tensor(out=rA[0:1, :seg_w], in0=rA[0:1, :seg_w], in1=rT[0:1, :seg_w],
                                   op=OP.mult))
    vf(nc.vector.tensor_tensor(out=rB[0:1, :seg_w], in0=rM[0:1, :seg_w], in1=rA[0:1, :seg_w],
                               op=OP.mult))


def emit_ln_bcast(p, rA, rB, pA, pB, ones_row_f, seg_w, lhsA=None, lhsB=None):
    nc = p.nc
    p.wait(nc.tensor, 'V')
    p.wait(nc.tensor, 'L', p.const_L)
    ins = nc.tensor.matmul(pA[:, :seg_w], lhsA if lhsA is not None else ones_row_f[0:1, :],
                           rA[0:1, :seg_w], start=True, stop=True)
    ins = nc.tensor.matmul(pB[:, :seg_w], lhsB if lhsB is not None else ones_row_f[0:1, :],
                           rB[0:1, :seg_w], start=True, stop=True)
    p.inc(ins, 'T')



def emit_l0_attn(p, xs):
    nc = p.nc

    d_wk = p.din("l0a_WkT", [D, D])
    d_wv = p.din("l0a_WvT", [D, D])
    d_wr = p.din("l0a_WrT", [D, D])
    d_wo = p.din("l0a_WoT", [D, D])
    d_vec = p.din("l0a_vecs", [128, 7 * NTL])

    wk = p.sb("wk_sb", [128, NTL * D], BF16)
    wv = p.sb("wv_sb", [128, NTL * D], BF16)
    wr = p.sb("wr_sb", [128, NTL * D], BF16)
    wo = p.sb("wo_sb", [128, NTL * D], BF16)
    vec = p.sb("vec_sb", [128, 7 * NTL], F32)
    vtd = p.sb("vtd", [128, NTL], F32)
    vetf = p.sb("vetf", [128, NTL], F32)
    tdcur = p.sb("tdcur", [128, SEG], F32)
    xn = p.sb("xn_seg", [128, NTL * SEG], BF16)
    sq = p.sb("sq_seg", [128, NTL * SEG], BF16)
    kin = p.sb("kin_seg", [128, NTL * SEG], BF16)
    xb = p.sb("xb_seg", [128, NTL * SEG], BF16)
    vin = p.sb("vin_seg", [128, NTL * SEG], BF16)
    rin = p.sb("rin_seg", [128, NTL * SEG], BF16)
    rwkv = p.sb("rwkv_seg", [128, NTL * SEG], BF16)
    attn = sq
    pcol = p.sb("pcol", [128, NTL], BF16)
    carN = p.sb("carN", [128, NTL], F32)
    carD = p.sb("carD", [128, NTL], F32)
    ekB = [p.sb("ek_s0", [128, SEG], F32), p.sb("ek_s1", [128, SEG], F32)]
    vvB = [p.sb("vv_s0", [128, SEG], F32), p.sb("vv_s1", [128, SEG], F32)]
    rrB = [p.sb("rr_s0", [128, SEG], F32), p.sb("rr_s1", [128, SEG], F32)]
    snumB = [p.sb("snum_s0", [128, SEG], F32), p.sb("snum_s1", [128, SEG], F32)]
    ek, vv, rr, snum = ekB[0], vvB[0], rrB[0], snumB[0]
    sden = tdcur
    lnt = ek
    rM = ek[0:1, :]
    rA = vv[0:1, :]
    rB = rr[0:1, :]
    rT = snum[0:1, :]
    onesc = p.onesc
    onesr = p.onesr

    pS, pQ, pA, pB = p.bank[0], p.bank[1], p.bank[2], p.bank[3]
    pK = [p.bank[4], p.bank[5]]
    pV = [p.bank[6], p.bank[7]]

    ins = p.dma(nc.sync, 'L', out=vec[:, :], in_=d_vec)
    for w_sb, d_w in ((wk, d_wk), (wv, d_wv), (wr, d_wr), (wo, d_wo)):
        for k in range(NTL):
            ins = p.dma(nc.gpsimd, 'W', gate=(k == 0), out=w_sb[:, k * D:(k + 1) * D],
                                      in_=d_w[k * 128:(k + 1) * 128, :])

    p.wait(nc.scalar, 'L')
    nc.scalar.activation(out=vtd[:, :], in_=vec[:, 0:NTL], func=AF.Exp)
    nc.scalar.activation(out=vtd[:, :], in_=vtd[:, :], func=AF.Exp, scale=-1.0)
    ins = nc.scalar.activation(out=vetf[:, :], in_=vec[:, NTL:2 * NTL], func=AF.Exp)
    p.inc(ins, 'A')

    nc.vector.memset(pcol[:, :], 0.0)
    nc.vector.memset(carN[:, :], 0.0)
    ins = nc.vector.memset(carD[:, :], 0.0)
    p.inc(ins, 'V')

    tmk = vec[:, 2 * NTL:3 * NTL]
    tmv = vec[:, 3 * NTL:4 * NTL]
    tmr = vec[:, 4 * NTL:5 * NTL]
    bo = vec[:, 5 * NTL:6 * NTL]

    wait_x = p.cnt['L']

    def emit_ln1(s, vwait=None):
        c0 = s * SEG

        def xsl(i):
            return xs[:, i * T + c0: i * T + c0 + SEG]
        p.wait(nc.scalar, 'L', wait_x)
        p.wait(nc.scalar, 'T')
        if vwait is not None:
            p.wait(nc.scalar, 'V', vwait)
        emit_ln_stats(p, [xsl(i) for i in range(NTL)], xb, sq, onesc, pS, pQ, SEG)
        emit_ln_rows(p, pS, pQ, rM, rA, rB, rT, SEG)
        emit_ln_bcast(p, rA, rB, pA, pB, onesr, SEG)
        p.wait(nc.vector, 'T')
        for i in range(NTL):
            nc.vector.tensor_tensor(out=lnt[:, :], in0=xsl(i), in1=pA[:, :], op=OP.mult)
            ins = nc.vector.tensor_tensor(out=xn[:, i * SEG:(i + 1) * SEG], in0=lnt[:, :],
                                          in1=pB[:, :], op=OP.add)
        p.inc(ins, 'V')
        p.bkset(2, 'V')
        p.bkset(3, 'V')

    def emit_mixes(guard_T):
        p.wait(nc.vector, 'T', guard_T)
        for i in range(NTL):
            a = i * SEG
            nc.vector.tensor_tensor(out=sq[:, a:a + 1], in0=xn[:, a:a + 1],
                                    in1=pcol[:, i:i + 1], op=OP.subtract)
            nc.vector.tensor_tensor(out=sq[:, a + 1:a + SEG], in0=xn[:, a + 1:a + SEG],
                                    in1=xn[:, a:a + SEG - 1], op=OP.subtract)
            for dst, tm in ((kin, tmk), (vin, tmv), (rin, tmr)):
                nc.vector.scalar_tensor_tensor(out=dst[:, a:a + 1], in0=sq[:, a:a + 1],
                                               scalar=tm[:, i:i + 1], in1=pcol[:, i:i + 1],
                                               op0=OP.mult, op1=OP.add)
                nc.vector.scalar_tensor_tensor(out=dst[:, a + 1:a + SEG], in0=sq[:, a + 1:a + SEG],
                                               scalar=tm[:, i:i + 1], in1=xn[:, a:a + SEG - 1],
                                               op0=OP.mult, op1=OP.add)
            ins = nc.vector.tensor_copy(out=pcol[:, i:i + 1], in_=xn[:, a + SEG - 1:a + SEG])
        p.inc(ins, 'V')
        return p.cnt['V']

    emit_ln1(0)
    mix_done = emit_mixes(p.cnt['T'])
    for s in range(NSEG):
        c0 = s * SEG

        def xsl(i):
            return xs[:, i * T + c0: i * T + c0 + SEG]

        wkv_chain = []
        for m in range(NTL):
            bK = pK[m % 2]
            bV = pV[m % 2]
            ek, vv, rr, snum = ekB[m % 2], vvB[m % 2], rrB[m % 2], snumB[m % 2]
            sden = tdcur
            if s == 0 and m == 0:
                p.wait(nc.tensor, 'W')
            p.wait(nc.tensor, 'V', mix_done)
            p.bkwait(4 + m % 2, 6 + m % 2)
            for k in range(NTL):
                ins = nc.tensor.matmul(bK[:, :], wk[:, k * D + m * 128: k * D + (m + 1) * 128],
                                       kin[:, k * SEG:(k + 1) * SEG],
                                       start=(k == 0), stop=(k == NTL - 1))
            for k in range(NTL):
                ins = nc.tensor.matmul(bV[:, :], wv[:, k * D + m * 128: k * D + (m + 1) * 128],
                                       vin[:, k * SEG:(k + 1) * SEG],
                                       start=(k == 0), stop=(k == NTL - 1))
            p.inc(ins, 'T')
            tKV = p.cnt['T']
            p.wait(nc.scalar, 'T', tKV)
            if m >= 2:
                p.wait(nc.scalar, 'V', wkv_chain[m - 2])
            elif s > 0:
                p.wait(nc.scalar, 'V', seg_wkv_last)
            nc.scalar.activation(out=ek[:, :], in_=bK[:, :], func=AF.Exp)
            ins = nc.scalar.activation(out=vv[:, :], in_=bV[:, :], func=AF.Identity)
            p.inc(ins, 'A')
            eKV = p.cnt['A']
            p.bkset(6 + m % 2, 'A')
            p.wait(nc.tensor, 'A', eKV)
            for k in range(NTL):
                ins = nc.tensor.matmul(bK[:, :], wr[:, k * D + m * 128: k * D + (m + 1) * 128],
                                       rin[:, k * SEG:(k + 1) * SEG],
                                       start=(k == 0), stop=(k == NTL - 1))
            p.inc(ins, 'T')
            if m == NTL - 1:
                prev_kvr_reads = p.cnt['T']
            p.wait(nc.scalar, 'T')
            ins = nc.scalar.activation(out=rr[:, :], in_=bK[:, :], func=AF.Exp, scale=-1.0)
            p.inc(ins, 'A')
            p.bkset(4 + m % 2, 'A')
            ins = nc.scalar.activation(out=rr[:, :], in_=rr[:, :], func=AF.Ln, bias=1.0)
            p.inc(ins, 'A')
            ins = nc.scalar.activation(out=rr[:, :], in_=rr[:, :], func=AF.Exp, scale=-1.0)
            p.inc(ins, 'A')

            p.wait(nc.vector, 'A', eKV)
            nc.vector.tensor_scalar(out=tdcur[:, :], in0=xs[:, 0:SEG], scalar1=0.0,
                                    scalar2=vtd[:, m:m + 1], op0=OP.mult, op1=OP.add)
            nc.vector.tensor_tensor(out=vv[:, :], in0=ek[:, :], in1=vv[:, :], op=OP.mult)
            nc.vector.tensor_tensor_scan(out=snum[:, :], data0=tdcur[:, :],
                                         data1=vv[:, :], initial=carN[:, m:m + 1],
                                         op0=OP.mult, op1=OP.add)
            nc.vector.tensor_tensor_scan(out=sden[:, :], data0=tdcur[:, :],
                                         data1=ek[:, :], initial=carD[:, m:m + 1],
                                         op0=OP.mult, op1=OP.add)
            nc.vector.scalar_tensor_tensor(out=vv[:, 0:1], in0=vv[:, 0:1],
                                           scalar=vetf[:, m:m + 1], in1=carN[:, m:m + 1],
                                           op0=OP.mult, op1=OP.add)
            nc.vector.scalar_tensor_tensor(out=vv[:, 1:SEG], in0=vv[:, 1:SEG],
                                           scalar=vetf[:, m:m + 1], in1=snum[:, 0:SEG - 1],
                                           op0=OP.mult, op1=OP.add)
            nc.vector.scalar_tensor_tensor(out=ek[:, 0:1], in0=ek[:, 0:1],
                                           scalar=vetf[:, m:m + 1], in1=carD[:, m:m + 1],
                                           op0=OP.mult, op1=OP.add)
            nc.vector.scalar_tensor_tensor(out=ek[:, 1:SEG], in0=ek[:, 1:SEG],
                                           scalar=vetf[:, m:m + 1], in1=sden[:, 0:SEG - 1],
                                           op0=OP.mult, op1=OP.add)
            nc.vector.tensor_copy(out=carN[:, m:m + 1], in_=snum[:, SEG - 1:SEG])
            ins = nc.vector.tensor_copy(out=carD[:, m:m + 1], in_=sden[:, SEG - 1:SEG])
            p.inc(ins, 'V')
            p.wait(nc.scalar, 'V')
            nc.scalar.activation(out=ek[:, :], in_=ek[:, :], func=AF.Ln)
            ins = nc.scalar.activation(out=ek[:, :], in_=ek[:, :], func=AF.Exp, scale=-1.0)
            p.inc(ins, 'A')
            p.wait(nc.vector, 'A')
            nc.vector.tensor_tensor(out=vv[:, :], in0=vv[:, :], in1=ek[:, :], op=OP.mult)
            p.wait(nc.vector, 'A')
            ins = nc.vector.tensor_tensor(out=rwkv[:, m * SEG:(m + 1) * SEG], in0=vv[:, :],
                                          in1=rr[:, :], op=OP.mult)
            p.inc(ins, 'V')
            wkv_chain.append(p.cnt['V'])
        seg_wkv_last = wkv_chain[-1]
        if s + 1 < NSEG:
            emit_ln1(s + 1, vwait=mix_done)
            mix_done = emit_mixes(p.cnt['T'])
        rwkv_done = p.cnt['V']
        p.wait(nc.scalar, 'V', mix_done)
        for j in range(NTL):
            bO = pK[j % 2]
            p.wait(nc.tensor, 'V', rwkv_done)
            p.bkwait(4 + j % 2)
            for k in range(NTL):
                ins = nc.tensor.matmul(bO[:, :], wo[:, k * D + j * 128: k * D + (j + 1) * 128],
                                       rwkv[:, k * SEG:(k + 1) * SEG],
                                       start=(k == 0), stop=(k == NTL - 1))
            p.inc(ins, 'T')
            p.wait(nc.scalar, 'T')
            ins = nc.scalar.activation(out=attn[:, j * SEG:(j + 1) * SEG], in_=bO[:, :],
                                       func=AF.Identity, bias=bo[:, j:j + 1])
            p.inc(ins, 'A')
            p.bkset(4 + j % 2, 'A')

        p.wait(nc.scalar, 'A')
        p.wait(nc.scalar, 'T')
        emit_ln_stats(p, [attn[:, i * SEG:(i + 1) * SEG] for i in range(NTL)],
                      xb, sq, onesc, pS, pQ, SEG, bf_direct=True)
        emit_ln_rows(p, pS, pQ, rM, rA, rB, rT, SEG)
        emit_ln_bcast(p, rA, rB, pA, pB, onesr, SEG)
        p.wait(nc.vector, 'T')
        for i in range(NTL):
            asl = attn[:, i * SEG:(i + 1) * SEG]
            nc.vector.tensor_tensor(out=asl, in0=asl, in1=pA[:, :], op=OP.mult)
            nc.vector.tensor_tensor(out=asl, in0=asl, in1=pB[:, :], op=OP.add)
            ins = nc.vector.tensor_tensor(out=xsl(i), in0=xsl(i), in1=asl, op=OP.add)
        p.inc(ins, 'V')
        p.bkset(2, 'V')
        p.bkset(3, 'V')
        p.wait(nc.scalar, 'V')



def emit_l0_ffn(p, xs):
    nc = p.nc
    E, S, HD, ES = 16, 16, 256, 256
    CH = 512
    NCH = T // CH

    d_mu = p.din("l0f_mu2d", [D, ES])
    d_w1 = p.din("l0f_w1", [E * D, HD])
    d_w2 = p.din("l0f_w2", [E * HD, D])
    d_gb = p.din("l0f_gb", [128, 2 * NTL])
    d_b1 = p.din("l0f_b1p", [128, 2 * E])
    d_b2 = p.din("l0f_b2p", [128, 8 * E])
    d_sc = p.din("l0f_scale_col", [128, 1])
    d_id = p.din("c_ident", [128, 128], BF16)
    s_xn2 = p.dtmp("s_xn2", [D, T], BF16)
    s_dsp = p.dtmp("s_dsp", [ES, T], BF16)

    mu = p.sb("mu_sb", [128, NTL * ES], F32)
    muS = p.sb("muS_sb", [128, NTL * ES], BF16)
    gb = p.sb("gb_sb", [128, 2 * NTL], F32)
    b1s = p.sb("b1_sb", [128, 2 * E], F32)
    b2s = p.sb("b2_sb", [128, 8 * E], F32)
    scol = p.sb("scol", [128, 1], F32)
    nscol = p.sb("nscol", [128, 1], F32)
    identb = p.sb("identb", [128, 128], BF16)
    xn2 = p.sb("xn2_sb", [128, NTL * T], BF16)
    sqf = p.sb("fsq", [128, NTL * CH], BF16)
    expLs = [p.sb("expL0", [128, T], BF16), p.sb("expL1", [128, T], BF16)]
    dsp = p.sb("dsp", [128, 2 * T], BF16)
    lgt = p.sb("lgt", [128, CH], F32)
    slotin = p.sb("slotin", [128, NTL * ES], BF16)
    rnb = p.sb("rnb", [128, CH], F32)
    w1c = [p.sb("w1c0", [128, 2 * NTL * HD], BF16),
           p.sb("w1c1", [128, 2 * NTL * HD], BF16)]
    w2c = [p.sb("w2c0", [128, 2 * 2 * D], BF16),
           p.sb("w2c1", [128, 2 * 2 * D], BF16)]
    hh = p.sb("hh", [128, 2 * E + 16], BF16)
    soT = p.sb("soT", [128, NTL * ES], BF16)
    soE = p.sb("soE", [128, 2 * D], BF16)
    fout = p.sb("fout2", [128, NTL * CH], BF16)
    zt = p.sb("ztf", [128, 2], F32)
    rLn = p.sb("rLn", [1, T], F32)
    rZe = p.sb("rZe", [1, T], F32)
    fM = mu[0:1, 0:CH]
    fA = mu[0:1, CH:2 * CH]
    fB = mu[0:1, 2 * CH:3 * CH]
    fT = mu[0:1, 3 * CH:4 * CH]

    pS, pQ, pA, pB = p.bank[0], p.bank[1], p.bank[2], p.bank[3]

    ins = p.dma(nc.sync, 'L', out=mu[:, :], in_=d_mu.rearrange("(k q) s -> q k s", q=128))
    for dst, dd in ((gb, d_gb), (b1s, d_b1), (b2s, d_b2), (scol, d_sc), (identb, d_id)):
        ins = p.dma(nc.sync, 'L', out=dst[:, :], in_=dd)
    ld_const = p.cnt['L']
    def load_pair(pr):
        ins = p.dma(nc.gpsimd, 'W', 
            out=w1c[pr % 2][:, :],
            in_=d_w1[pr * 2 * D:(pr + 1) * 2 * D, :].rearrange("(g q) h -> q g h", q=128))
        ins = p.dma(nc.gpsimd, 'W', 
            out=w2c[pr % 2][:, :],
            in_=d_w2[pr * 2 * HD:(pr + 1) * 2 * HD, :].rearrange("(g q) d2 -> q g d2", q=128))
    load_pair(0)
    load_pair(1)
    pair_ld = [p.cnt['W'] - 32, p.cnt['W']] + [0] * 6

    p.wait(nc.scalar, 'L', ld_const)
    ins = nc.scalar.activation(out=nscol[:, :], in_=scol[:, :], func=AF.Identity, scale=-1.0)
    p.inc(ins, 'A')

    p.wait(nc.scalar, 'V')
    for i in range(NTL):
        ins = nc.scalar.activation(out=sqf[:, i * ES:(i + 1) * ES], in_=mu[:, i * ES:(i + 1) * ES],
                                   func=AF.Square)
    p.inc(ins, 'A')
    p.wait(nc.tensor, 'A')
    p.wait(nc.tensor, 'L', p.const_L)
    for i in range(NTL):
        ins = nc.tensor.matmul(pQ[0:1, 0:ES], p.onesc[:, 0:1], sqf[:, i * ES:(i + 1) * ES],
                               start=(i == 0), stop=(i == NTL - 1))
    p.inc(ins, 'T')
    p.wait(nc.scalar, 'T')
    ins = nc.scalar.activation(out=rZe[0:1, 0:ES], in_=pQ[0:1, 0:ES], func=AF.Sqrt)
    p.inc(ins, 'A')
    p.wait(nc.vector, 'A')
    ins = nc.vector.tensor_scalar(out=rZe[0:1, 0:ES], in0=rZe[0:1, 0:ES], scalar1=1e-12, scalar2=None,
                                  op0=OP.max, op1=OP.bypass)
    p.inc(ins, 'V')
    p.wait(nc.vector, 'V')
    ins = nc.vector.reciprocal(out=rZe[0:1, 0:ES], in_=rZe[0:1, 0:ES])
    p.inc(ins, 'V')
    p.wait(nc.tensor, 'V')
    ins = nc.tensor.matmul(pA[:, 0:ES], p.onesr[0:1, :], rZe[0:1, 0:ES], start=True, stop=True)
    p.inc(ins, 'T')
    p.wait(nc.vector, 'T')
    for i in range(NTL):
        ins = nc.vector.tensor_tensor(out=muS[:, i * ES:(i + 1) * ES], in0=mu[:, i * ES:(i + 1) * ES],
                                      in1=pA[:, 0:ES], op=OP.mult)
    p.inc(ins, 'V')
    muS_done = p.cnt['V']

    for c in range(NCH):
        def xsl(i):
            return xs[:, i * T + c * CH: i * T + c * CH + CH]
        p.wait(nc.scalar, 'V')
        emit_ln_stats(p, [xsl(i) for i in range(NTL)], fout, sqf, p.onesc, pS, pQ, CH)
        emit_ln_rows(p, pS, pQ, fM, fA, fB, fT, CH, double_ln=True)
        emit_ln_bcast(p, fA, fB, pA, pB, p.onesr, CH)
        p.wait(nc.vector, 'T')
        for i in range(NTL):
            nc.vector.tensor_tensor(out=lgt[:, :], in0=xsl(i), in1=pA[:, :], op=OP.mult)
            nc.vector.tensor_tensor(out=lgt[:, :], in0=lgt[:, :], in1=pB[:, :], op=OP.add)
            ins = nc.vector.tensor_scalar(out=xn2[:, i * T + c * CH: i * T + c * CH + CH],
                                          in0=lgt[:, :], scalar1=gb[:, i:i + 1],
                                          scalar2=gb[:, NTL + i:NTL + i + 1],
                                          op0=OP.mult, op1=OP.add)
        p.inc(ins, 'V')
        p.bkset(2, 'V')
        p.bkset(3, 'V')
        p.wait(nc.scalar, 'V')
        for i in range(NTL):
            ins = nc.scalar.activation(out=sqf[:, i * CH:(i + 1) * CH],
                                       in_=xn2[:, i * T + c * CH: i * T + c * CH + CH],
                                       func=AF.Square)
        p.inc(ins, 'A')
        p.wait(nc.tensor, 'A')
        for i in range(NTL):
            ins = nc.tensor.matmul(pQ[0:1, :CH], p.onesc[:, 0:1], sqf[:, i * CH:(i + 1) * CH],
                                   start=(i == 0), stop=(i == NTL - 1))
        p.inc(ins, 'T')
        p.wait(nc.scalar, 'T')
        ins = nc.scalar.activation(out=fT[0:1, :], in_=pQ[0:1, :CH], func=AF.Sqrt)
        p.inc(ins, 'A')
        p.wait(nc.vector, 'A')
        ins = nc.vector.tensor_scalar(out=fT[0:1, :], in0=fT[0:1, :], scalar1=1e-12, scalar2=None,
                                      op0=OP.max, op1=OP.bypass)
        p.inc(ins, 'V')
        p.wait(nc.vector, 'V')
        ins = nc.vector.reciprocal(out=fT[0:1, :], in_=fT[0:1, :])
        p.inc(ins, 'V')
        p.wait(nc.vector, 'V')
        ins = nc.vector.tensor_scalar(out=rLn[0:1, c * CH:(c + 1) * CH], in0=fT[0:1, :],
                                      scalar1=scol[0:1, 0:1], scalar2=None,
                                      op0=OP.mult, op1=OP.bypass)
        p.inc(ins, 'V')

    p.dump("xn2", xn2[:, :])
    p.dump("rLn", rLn[0:1, :])
    p.dump("muS", muS[:, :])
    p.wait(nc.sync, 'V')
    for k in range(NTL):
        ins = p.dma(nc.sync, 'L', gate=(k == 0), out=s_xn2[k * 128:(k + 1) * 128, :],
                    in_=xn2[:, k * T:(k + 1) * T])
    xn2_store = p.cnt['L']

    for c in range(NCH):
        p.wait(nc.tensor, 'V')
        ins = nc.tensor.matmul(pA[:, :CH], p.onesr[0:1, :], rLn[0:1, c * CH:(c + 1) * CH],
                               start=True, stop=True)
        p.inc(ins, 'T')
        p.wait(nc.vector, 'T')
        ins = nc.vector.tensor_copy(out=rnb[:, :], in_=pA[:, :CH])
        p.inc(ins, 'V')
        for m in range(2):
            pL = p.bank[4 + m]
            p.bkwait(4 + m)
            for k in range(NTL):
                ins = nc.tensor.matmul(pL[:, :CH], muS[:, k * ES + m * 128: k * ES + (m + 1) * 128],
                                       xn2[:, k * T + c * CH: k * T + c * CH + CH],
                                       start=(k == 0), stop=(k == NTL - 1))
            p.inc(ins, 'T')
            p.wait(nc.vector, 'T')
            p.wait(nc.vector, 'A')
            ins = nc.vector.tensor_tensor(out=lgt[:, :], in0=pL[:, :CH], in1=rnb[:, :], op=OP.mult)
            p.inc(ins, 'V')
            p.bkset(4 + m, 'V')
            p.wait(nc.scalar, 'V')
            ins = nc.scalar.activation(out=expLs[m][:, c * CH: c * CH + CH],
                                       in_=lgt[:, :], func=AF.Exp, bias=nscol[:, 0:1])
            p.inc(ins, 'A')
        p.wait(nc.tensor, 'A')
        for m in range(2):
            ins = nc.tensor.matmul(pB[0:1, :CH], p.onesc[:, 0:1],
                                   expLs[m][:, c * CH: c * CH + CH],
                                   start=(m == 0), stop=(m == 1))
        p.inc(ins, 'T')
        p.wait(nc.vector, 'T')
        ins = nc.vector.reciprocal(out=rZe[0:1, c * CH:(c + 1) * CH], in_=pB[0:1, :CH])
        p.inc(ins, 'V')

    p.dump("expL0", expLs[0][:, :])
    p.dump("expL1", expLs[1][:, :])
    for m in range(2):
        ins = nc.vector.tensor_reduce(out=zt[:, m:m + 1], in_=expLs[m][:, :],
                                      axis=mybir.AxisListType.X, op=OP.add)
        p.inc(ins, 'V')
        p.wait(nc.vector, 'V')
        ins = nc.vector.reciprocal(out=zt[:, m:m + 1], in_=zt[:, m:m + 1])
        p.inc(ins, 'V')
        p.wait(nc.vector, 'V')
        ins = nc.vector.tensor_scalar(out=dsp[:, m * T:(m + 1) * T], in0=expLs[m][:, :],
                                      scalar1=zt[:, m:m + 1], scalar2=None,
                                      op0=OP.mult, op1=OP.bypass)
    p.inc(ins, 'V')
    dsp_done = p.cnt['V']
    for c in range(NCH):
        p.wait(nc.tensor, 'V')
        ins = nc.tensor.matmul(pA[:, :CH], p.onesr[0:1, :], rZe[0:1, c * CH:(c + 1) * CH],
                               start=True, stop=True)
        p.inc(ins, 'T')
        p.wait(nc.vector, 'T')
        for m in range(2):
            ins = nc.vector.tensor_tensor(out=expLs[m][:, c * CH: c * CH + CH],
                                          in0=expLs[m][:, c * CH: c * CH + CH],
                                          in1=pA[:, :CH], op=OP.mult)
        p.inc(ins, 'V')

    p.dump("dsp", dsp[:, :])
    p.dump("cmb0", expLs[0][:, :])
    p.dump("cmb1", expLs[1][:, :])
    p.wait(nc.sync, 'V', dsp_done)
    for m in range(2):
        ins = p.dma(nc.sync, 'L', out=s_dsp[m * 128:(m + 1) * 128, :], in_=dsp[:, m * T:(m + 1) * T])
    dsp_store = p.cnt['L']
    p.wait(nc.sync, 'T')
    p.wait(nc.sync, 'L', xn2_store)
    xtok = xn2
    for tt in range(16):
        ins = p.dma(nc.sync, 'L', gate=(tt == 0), out=xtok[:, tt * D:(tt + 1) * D],
                    in_=s_xn2[:, tt * 128:(tt + 1) * 128], transpose=True)
    p.wait(nc.sync, 'L', dsp_store)
    dspT = dsp
    for tt in range(16):
        ins = p.dma(nc.sync, 'L', gate=(tt == 0), out=dspT[:, tt * ES:(tt + 1) * ES],
                    in_=s_dsp[:, tt * 128:(tt + 1) * 128], transpose=True)
    tok_ld = p.cnt['L']

    p.wait(nc.tensor, 'L', tok_ld)
    for md in range(NTL):
        pSI = p.bank[4 + md % 2]
        p.bkwait(4 + md % 2)
        for tt in range(16):
            ins = nc.tensor.matmul(pSI[:, 0:ES], xtok[:, tt * D + md * 128: tt * D + (md + 1) * 128],
                                   dspT[:, tt * ES:(tt + 1) * ES],
                                   start=(tt == 0), stop=(tt == 15))
        p.inc(ins, 'T')
        p.wait(nc.scalar, 'T')
        ins = nc.scalar.activation(out=slotin[:, md * ES:(md + 1) * ES], in_=pSI[:, 0:ES],
                                   func=AF.Identity)
        p.inc(ins, 'A')
        p.bkset(4 + md % 2, 'A')

    p.wait(nc.gpsimd, 'A')
    p.dump("slotin", slotin[:, :])
    p.dump("xtok", xtok[:, :])
    p.dump("dspT", dspT[:, :])
    slot_ready = p.cnt['A']
    for e in range(E):
        pr, e2 = e // 2, e % 2
        pX = p.bank[6 + e % 2]
        p.wait(nc.tensor, 'A', slot_ready)
        p.wait(nc.tensor, 'W', pair_ld[pr])
        p.bkwait(6 + e % 2)
        wb = w1c[pr % 2]
        for mh in range(2):
            for k in range(NTL):
                ins = nc.tensor.matmul(pX[:, mh * 16:(mh + 1) * 16],
                                       wb[:, e2 * 2048 + k * HD + mh * 128: e2 * 2048 + k * HD + (mh + 1) * 128],
                                       slotin[:, k * ES + e * 16: k * ES + e * 16 + 16],
                                       start=(k == 0), stop=(k == NTL - 1))
        p.inc(ins, 'T')
        p.wait(nc.scalar, 'T')
        for mh in range(2):
            ins = nc.scalar.activation(out=hh[:, mh * 16:(mh + 1) * 16], in_=pX[:, mh * 16:(mh + 1) * 16],
                                       func=AF.Gelu, bias=b1s[:, mh * 16 + e:mh * 16 + e + 1])
        p.inc(ins, 'A')
        wb2 = w2c[pr % 2]
        p.wait(nc.tensor, 'A')
        for md in range(NTL):
            for kh in range(2):
                ins = nc.tensor.matmul(pX[:, 32 + md * 16: 48 + md * 16],
                                       wb2[:, e2 * 2048 + kh * D + md * 128: e2 * 2048 + kh * D + (md + 1) * 128],
                                       hh[:, kh * 16:(kh + 1) * 16],
                                       start=(kh == 0), stop=(kh == 1))
        p.inc(ins, 'T')
        p.wait(nc.scalar, 'T')
        for md in range(NTL):
            ins = nc.scalar.activation(out=soT[:, md * ES + e * 16: md * ES + e * 16 + 16],
                                       in_=pX[:, 32 + md * 16: 48 + md * 16],
                                       func=AF.Identity, bias=b2s[:, md * 16 + e:md * 16 + e + 1])
        p.inc(ins, 'A')
        p.bkset(6 + e % 2, 'A')
        if e2 == 1 and pr + 2 <= 7:
            p.wait(nc.gpsimd, 'T')
            load_pair(pr + 2)
            pair_ld[pr + 2] = p.cnt['W']

    p.wait(nc.gpsimd, 'A')
    p.dump("soT", soT[:, :])
    soT_done = p.cnt['A']
    pTb = p.bank[4].bitcast(BF16)
    p.wait(nc.tensor, 'A', soT_done)
    for mes in range(2):
        for md in range(NTL):
            ins = nc.tensor.transpose(pTb[:, 0:128], soT[:, md * ES + mes * 128: md * ES + (mes + 1) * 128],
                                      identb[:, :])
            p.inc(ins, 'T')
            p.wait(nc.scalar, 'T')
            ins = nc.scalar.activation(out=soE[:, mes * D + md * 128: mes * D + (md + 1) * 128],
                                       in_=pTb[:, 0:128], func=AF.Identity)
            p.inc(ins, 'A')
            p.wait(nc.tensor, 'A')

    p.wait(nc.tensor, 'V')
    for c in range(NCH):
        for md in range(NTL):
            pF = p.bank[4 + md % 2]
            p.bkwait(4 + md % 2)
            for kes in range(2):
                ins = nc.tensor.matmul(pF[:, :CH],
                                       soE[:, kes * D + md * 128: kes * D + (md + 1) * 128],
                                       expLs[kes][:, c * CH: c * CH + CH],
                                       start=(kes == 0), stop=(kes == 1))
            p.inc(ins, 'T')
            p.wait(nc.scalar, 'T')
            ins = nc.scalar.activation(out=fout[:, md * CH:(md + 1) * CH], in_=pF[:, :CH],
                                       func=AF.Identity)
            p.inc(ins, 'A')
            p.bkset(4 + md % 2, 'A')
        p.wait(nc.scalar, 'A')
        p.wait(nc.scalar, 'V')
        for i in range(NTL):
            ins = nc.scalar.activation(out=sqf[:, i * CH:(i + 1) * CH], in_=fout[:, i * CH:(i + 1) * CH],
                                       func=AF.Square)
        p.inc(ins, 'A')
        p.wait(nc.tensor, 'A')
        for i in range(NTL):
            ins = nc.tensor.matmul(pS[0:1, :CH], p.onesc[:, 0:1], fout[:, i * CH:(i + 1) * CH],
                                   start=(i == 0), stop=(i == NTL - 1))
        p.inc(ins, 'T')
        for i in range(NTL):
            ins = nc.tensor.matmul(pQ[0:1, :CH], p.onesc[:, 0:1], sqf[:, i * CH:(i + 1) * CH],
                                   start=(i == 0), stop=(i == NTL - 1))
        p.inc(ins, 'T')
        emit_ln_rows(p, pS, pQ, fM, fA, fB, fT, CH)
        emit_ln_bcast(p, fA, fB, pA, pB, p.onesr, CH)
        p.wait(nc.vector, 'T')
        for i in range(NTL):
            fsl = fout[:, i * CH:(i + 1) * CH]
            xslc = xs[:, i * T + c * CH: i * T + c * CH + CH]
            nc.vector.tensor_tensor(out=fsl, in0=fsl, in1=pA[:, :CH], op=OP.mult)
            nc.vector.tensor_tensor(out=fsl, in0=fsl, in1=pB[:, :CH], op=OP.add)
            ins = nc.vector.tensor_tensor(out=xslc, in0=xslc, in1=fsl, op=OP.add)
        p.inc(ins, 'V')
        p.bkset(2, 'V')
        p.bkset(3, 'V')
        p.wait(nc.scalar, 'V')



def emit_l1_attn(p, xs, last):
    nc = p.nc
    NH, HD2, NPAIR, NB = 16, 64, 8, 8
    LH, NL, LD = 8, 64, 512
    CH = 512
    NCH = T // CH

    d_wq = p.din("l1a_wq", [D, D])
    d_wk = p.din("l1a_wk", [D, D])
    d_wv = p.din("l1a_wv", [D, D])
    d_wg = p.din("l1a_wg3", [3 * D, LD])
    d_row = p.din("l1a_rout", [LD, D])
    d_wout = p.din("l1a_wout", [D, D])
    d_gp = p.din("l1a_gatep", [128, 2 * NH])
    d_a2 = p.din("l1a_a2", [2, 128], BF16)
    d_mask = p.din("l1a_mask", [128, 1024], BF16)
    d_lmask = p.din("l1a_lmask", [64, T], BF16)
    s_x = p.dtmp("s_x_l1", [D, T], F32)
    s_comb = p.dtmp("s_comb", [D, T], BF16)

    xsb = xs.bitcast(BF16)
    xn = xsb[:, 0:NTL * T]
    GQT = xsb[:, NTL * T:NTL * T + 4 * T]
    remT = xsb[:, NTL * T + 4 * T:NTL * T + 8 * T]

    xchunk = p.sb("l1_xchunk", [128, NTL * CH], F32)
    xbc = p.sb("l1_xbc", [128, NTL * CH], BF16)
    sq = p.sb("l1_sq", [128, NTL * CH], BF16)
    wg = p.sb("l1_wg", [128, NTL * LD], BF16)
    wro = p.sb("l1_wro", [128, 4 * D], BF16)
    wpr = [p.sb("l1_wpr0", [128, 3 * D], BF16),
           p.sb("l1_wpr1", [128, 3 * D], BF16)]
    Kp = [p.sb("l1_Kp0", [128, T], BF16), p.sb("l1_Kp1", [128, T], BF16)]
    Qp = [p.sb("l1_Qp0", [128, T], BF16), p.sb("l1_Qp1", [128, T], BF16)]
    Vp = [p.sb("l1_Vp0", [128, 2 * T], BF16), p.sb("l1_Vp1", [128, 2 * T], BF16)]
    locT = p.sb("l1_locT", [128, T], BF16)
    rem2p = p.sb("l1_rem2p", [128, T], BF16)
    combp = p.sb("l1_combp", [128, T], BF16)
    rzs = p.sb("l1_rzs", [64, 512], F32)
    poolT = p.sb("l1_poolT", [128, NTL * NL], F32)
    poolB = p.sb("l1_poolB", [128, NTL * NL], BF16)
    GKT = p.sb("l1_GKT", [128, 4 * NL], BF16)
    GVo = p.sb("l1_GVo", [64, 8 * 128], BF16)
    explsc = p.sb("l1_explsc", [64, T], BF16)
    lrz = p.sb("l1_lrz", [64, CH], F32)

    gpv = p.sb("l1_gpv", [128, 2 * NH], BF16)
    a2b = p.sb("l1_a2b", [2, 128], BF16)
    maskb = p.sb("l1_maskb", [128, 1024], BF16)
    lmaskb = p.sb("l1_lmaskb", [64, T], BF16)
    pstg = [sq[:, 0:1024], sq[:, 1024:2048]]
    glp = xbc[0:2, 0:T]
    grp = xbc[0:2, T:2 * T]
    woutb = [Kp[0], Kp[1], Qp[0], Qp[1]]
    rM = p.sb("l1_rM", [1, CH], F32)
    rA = p.sb("l1_rA", [1, CH], F32)
    rB = p.sb("l1_rB", [1, CH], F32)
    rT = p.sb("l1_rT", [1, CH], F32)

    pS, pQ, pA, pB = p.bank[0], p.bank[1], p.bank[2], p.bank[3]
    bS1, bS2 = p.bank[4], p.bank[5]
    bPV = [p.bank[6], p.bank[7]]

    for dst, dd in ((a2b, d_a2), (maskb, d_mask), (lmaskb, d_lmask)):
        ins = p.dma(nc.sync, 'L', out=dst[:, :], in_=dd)
    ins = p.dma(nc.gpsimd, 'W', out=gpv[:, :], in_=d_gp)
    ins = p.dma(nc.gpsimd, 'W', out=wro[:, :], in_=d_row.rearrange("(k q) o -> q k o", q=128))

    p.wait(nc.gpsimd, 'L')
    p.wait(nc.gpsimd, 'V')
    p.wait(nc.gpsimd, 'A')
    for i in range(NTL):
        ins = p.dma(nc.gpsimd, 'W', gate=(i == 0), out=s_x[i * 128:(i + 1) * 128, :],
                    in_=xs[:, i * T:(i + 1) * T])
    spill_done = p.cnt['W']

    for c in range(NCH):
        p.wait(nc.sync, 'W', spill_done)
        if c > 0:
            p.wait(nc.sync, 'V')
        ins = p.dma(nc.sync, 'L', out=xchunk[:, :],
                                in_=s_x[:, c * CH:(c + 1) * CH].rearrange("(i q) t -> q i t", q=128))
        p.wait(nc.scalar, 'L')
        p.wait(nc.scalar, 'V')
        emit_ln_stats(p, [xchunk[:, i * CH:(i + 1) * CH] for i in range(NTL)], xbc, sq,
                      p.onesc, pS, pQ, CH)
        emit_ln_rows(p, pS, pQ, rM, rA, rB, rT, CH)
        emit_ln_bcast(p, rA, rB, pA, pB, p.onesr, CH)
        p.wait(nc.vector, 'T')
        if c == 0:
            p.wait(nc.vector, 'W', spill_done)
        for i in range(NTL):
            nc.vector.tensor_tensor(out=xchunk[:, i * CH:(i + 1) * CH],
                                    in0=xchunk[:, i * CH:(i + 1) * CH], in1=pA[:, :CH], op=OP.mult)
            ins = nc.vector.tensor_tensor(out=xn[:, i * T + c * CH: i * T + c * CH + CH],
                                          in0=xchunk[:, i * CH:(i + 1) * CH], in1=pB[:, :CH],
                                          op=OP.add)
        p.inc(ins, 'V')
        p.bkset(2, 'V')
        p.bkset(3, 'V')

    p.wait(nc.vector, 'V')
    for i in range(NTL):
        ins = nc.vector.tensor_reduce(out=poolT[:, i * NL:(i + 1) * NL],
                                      in_=xn[:, i * T:(i + 1) * T].rearrange("q (l c) -> q l c", c=32),
                                      axis=mybir.AxisListType.X, op=OP.add)
    p.inc(ins, 'V')
    p.wait(nc.vector, 'V')
    for i in range(NTL):
        ins = nc.vector.tensor_scalar(out=poolB[:, i * NL:(i + 1) * NL],
                                      in0=poolT[:, i * NL:(i + 1) * NL], scalar1=1.0 / 32,
                                      scalar2=None, op0=OP.mult, op1=OP.bypass)
    p.inc(ins, 'V')
    pool_done = p.cnt['V']
    ins = p.dma(nc.gpsimd, 'W', out=wg[:, :], in_=d_wg[0:D, :].rearrange("(k q) o -> q k o", q=128))
    p.wait(nc.tensor, 'W')
    p.wait(nc.tensor, 'V', pool_done)
    for mt in range(4):
        p.bkwait(2)
        for k in range(NTL):
            ins = nc.tensor.matmul(pA[:, 0:NL], wg[:, k * LD + mt * 128: k * LD + (mt + 1) * 128],
                                   poolB[:, k * NL:(k + 1) * NL], start=(k == 0), stop=(k == NTL - 1))
        p.inc(ins, 'T')
        p.wait(nc.scalar, 'T')
        ins = nc.scalar.activation(out=GKT[:, mt * NL:(mt + 1) * NL], in_=pA[:, 0:NL],
                                   func=AF.Identity)
        p.inc(ins, 'A')
        p.bkset(2, 'A')
    p.wait(nc.gpsimd, 'T')
    ins = p.dma(nc.gpsimd, 'W', out=wg[:, :], in_=d_wg[D:2 * D, :].rearrange("(k q) o -> q k o", q=128))
    p.wait(nc.vector, 'A')
    ins = nc.vector.memset(GVo[:, :], 1.0)
    p.inc(ins, 'V')
    p.wait(nc.tensor, 'W')
    p.wait(nc.tensor, 'V')
    p.bkwait(3)
    for k in range(NTL):
        ins = nc.tensor.matmul(pB[0:NL, 0:LD], poolB[:, k * NL:(k + 1) * NL],
                               wg[:, k * LD:(k + 1) * LD], start=(k == 0), stop=(k == NTL - 1))
    p.inc(ins, 'T')
    p.wait(nc.vector, 'T')
    ins = nc.vector.tensor_copy(
        out=GVo[:, :].rearrange("q (l c) -> q l c", c=128)[:, :, 0:64],
        in_=pB[0:NL, 0:LD].rearrange("q (l c) -> q l c", c=64))
    p.inc(ins, 'V')
    p.bkset(3, 'V')
    p.wait(nc.gpsimd, 'T')
    ins = p.dma(nc.gpsimd, 'W', out=wg[:, :], in_=d_wg[2 * D:3 * D, :].rearrange("(k q) o -> q k o", q=128))
    p.wait(nc.tensor, 'W')
    for mt in range(4):
        for c in range(NCH):
            bb = pA if (mt * NCH + c) % 2 == 0 else pB
            p.bkwait(2 + (mt * NCH + c) % 2)
            for k in range(NTL):
                ins = nc.tensor.matmul(bb[:, :CH], wg[:, k * LD + mt * 128: k * LD + (mt + 1) * 128],
                                       xn[:, k * T + c * CH: k * T + c * CH + CH],
                                       start=(k == 0), stop=(k == NTL - 1))
            p.inc(ins, 'T')
            p.wait(nc.scalar, 'T')
            ins = nc.scalar.activation(out=GQT[:, mt * T + c * CH: mt * T + c * CH + CH],
                                       in_=bb[:, :CH], func=AF.Identity)
            p.inc(ins, 'A')
            p.bkset(2 + (mt * NCH + c) % 2, 'A')
    for lh in range(LH):
        mt, half = lh // 2, (lh % 2) * 64
        for c in range(NCH):
            bb = pA if c % 2 == 0 else pB
            p.bkwait(2 + c % 2)
            ins = nc.tensor.matmul(bb[0:64, :CH], GKT[half:half + 64, mt * NL:(mt + 1) * NL],
                                   GQT[half:half + 64, mt * T + c * CH: mt * T + c * CH + CH],
                                   start=True, stop=True)
            p.inc(ins, 'T')
            p.wait(nc.scalar, 'T')
            ins = nc.scalar.activation(out=explsc[:, c * CH:(c + 1) * CH], in_=bb[0:64, :CH],
                                       func=AF.Exp, scale=0.125)
            p.inc(ins, 'A')
            p.bkset(2 + c % 2, 'A')
        p.wait(nc.vector, 'A')
        ins = nc.vector.tensor_tensor(out=explsc[:, :], in0=explsc[:, :], in1=lmaskb[:, :],
                                      op=OP.mult)
        p.inc(ins, 'V')
        for c in range(NCH):
            bb = bS1 if c % 2 == 0 else bS2
            p.bkwait(4 + c % 2)
            p.wait(nc.tensor, 'V')
            ins = nc.tensor.matmul(bb[:, :CH], GVo[:, lh * 128:(lh + 1) * 128],
                                   explsc[:, c * CH:(c + 1) * CH], start=True, stop=True)
            p.inc(ins, 'T')
            p.wait(nc.scalar, 'T')
            ins = nc.scalar.activation(out=lrz[:, :], in_=bb[64:128, :CH], func=AF.Ln)
            p.inc(ins, 'A')
            p.wait(nc.scalar, 'A')
            ins = nc.scalar.activation(out=lrz[:, :], in_=lrz[:, :], func=AF.Exp, scale=-1.0)
            p.inc(ins, 'A')
            p.wait(nc.vector, 'A')
            ins = nc.vector.tensor_tensor(
                out=remT[half:half + 64, mt * T + c * CH: mt * T + c * CH + CH],
                in0=bb[0:64, :CH], in1=lrz[:, :], op=OP.mult)
            p.inc(ins, 'V')
            p.bkset(4 + c % 2, 'V')

    d_w1g = p.din("l1f_W1T", [D, 4096])
    d_w2g = p.din("l1f_W2T", [4096, D])
    gq_free = p.cnt['T']
    gslot = [xs[:, 8192:10240], xs[:, 10240:12288]]
    gtasks = ([(d_w1g[ti * 128:(ti + 1) * 128, h * 2048:(h + 1) * 2048], 2048, 2 * ti + h)
               for ti in range(NTL) for h in range(2)]
              + [(d_w2g[tj * 128:(tj + 1) * 128, :], 1024, 16 + tj) for tj in range(32)])
    gstate = {"n": 0, "ld": {}, "rd": {}}

    def gamma_tick():
        t = gstate["n"]
        if t >= len(gtasks) + 2:
            return
        tr = t - 2
        if 0 <= tr < len(gtasks):
            _, ncols, col = gtasks[tr]
            nc.vector.wait_ge(p.sems['L'], gstate["ld"][tr])
            ins = nc.vector.tensor_reduce(out=p.gacc_g[:, col:col + 1],
                                          in_=gslot[tr % 2][:, 0:ncols],
                                          axis=mybir.AxisListType.X, op=OP.add,
                                          apply_absolute_value=True)
            p.inc(ins, 'G')
            gstate["rd"][tr] = p.cnt['G']
        if t < len(gtasks):
            src, ncols, _ = gtasks[t]
            if t < 2:
                nc.sync.wait_ge(p.sems['T'], gq_free)
            else:
                nc.sync.wait_ge(p.sems['G'], gstate["rd"][t - 2])
            ins = nc.sync.dma_start(out=gslot[t % 2][:, 0:ncols], in_=src)
            p.inc(ins, 'L', 16)
            gstate["ld"][t] = p.cnt['L']
        gstate["n"] += 1

    def load_pair(pr):
        buf = wpr[pr % 2]
        for wi, dd in enumerate((d_wq, d_wk, d_wv)):
            ins = p.dma(nc.gpsimd, 'W', gate=(wi == 0),
                out=buf[:, wi * D:(wi + 1) * D],
                in_=dd[:, pr * 128:(pr + 1) * 128].rearrange("(k q) c -> q k c", q=128))
    load_pair(0)
    pair_ld = [p.cnt['W']] + [0] * 7
    p.combw = 0

    for pr in range(NPAIR):
        if pr + 1 < NPAIR:
            p.wait(nc.gpsimd, 'T')
            load_pair(pr + 1)
            pair_ld[pr + 1] = p.cnt['W']
        gamma_tick()
        buf = wpr[pr % 2]
        kp, qp, vp = Kp[pr % 2], Qp[pr % 2], Vp[pr % 2]
        p.wait(nc.tensor, 'W', pair_ld[pr])
        p.wait(nc.scalar, 'V')
        for which, dstt in ((0, qp), (1, kp)):
            for c in range(NCH):
                bb = pA if c % 2 == 0 else pB
                p.bkwait(2 + c % 2)
                for k in range(NTL):
                    ins = nc.tensor.matmul(bb[:, :CH],
                                           buf[:, which * D + k * 128: which * D + (k + 1) * 128],
                                           xn[:, k * T + c * CH: k * T + c * CH + CH],
                                           start=(k == 0), stop=(k == NTL - 1))
                p.inc(ins, 'T')
                p.wait(nc.vector, 'T')
                ins = nc.vector.tensor_copy(out=dstt[:, c * CH:(c + 1) * CH], in_=bb[:, :CH])
                p.inc(ins, 'V')
                p.bkset(2 + c % 2, 'V')
        gamma_tick()
        p.wait(nc.vector, 'T')
        ins = nc.vector.memset(vp[:, :], 1.0)
        p.inc(ins, 'V')
        for tt in range(16):
            bb = pA if tt % 2 == 0 else pB
            p.bkwait(2 + tt % 2)
            p.wait(nc.tensor, 'V')
            for k in range(NTL):
                ins = nc.tensor.matmul(bb[:, 0:128], xn[:, k * T + tt * 128: k * T + (tt + 1) * 128],
                                       buf[:, 2 * D + k * 128: 2 * D + (k + 1) * 128],
                                       start=(k == 0), stop=(k == NTL - 1))
            p.inc(ins, 'T')
            p.wait(nc.vector, 'T')
            ins = nc.vector.tensor_copy(
                out=vp[:, tt * 256:tt * 256 + 192].rearrange("q (g c) -> q g c", c=64)[:, 0::2, :],
                in_=bb[:, 0:128].rearrange("q (g c) -> q g c", c=64))
            p.inc(ins, 'V')
            p.bkset(2 + tt % 2, 'V')
        gamma_tick()
        qev = p.cnt['V']
        for gi, gdst in ((0, glp), (1, grp)):
            for cc in range(4):
                p.wait(nc.tensor, 'V', qev)
                p.bkwait(4)
                ins = nc.tensor.matmul(bS1[0:2, 0:512],
                                       gpv[:, gi * NH + pr * 2: gi * NH + pr * 2 + 2],
                                       qp[:, cc * 512:(cc + 1) * 512], start=True, stop=True)
                p.inc(ins, 'T')
                p.wait(nc.scalar, 'T')
                ins = nc.scalar.activation(out=gdst[:, cc * 512:(cc + 1) * 512],
                                           in_=bS1[0:2, 0:512], func=AF.Sigmoid)
                p.inc(ins, 'A')
                p.bkset(4, 'A')
        gamma_tick()
        for c in range(NCH):
            p.bkwait(5)
            for kl in range(4):
                ins = nc.tensor.matmul(bS2[:, :CH], wro[:, kl * D + pr * 128: kl * D + (pr + 1) * 128],
                                       remT[:, kl * T + c * CH: kl * T + c * CH + CH],
                                       start=(kl == 0), stop=(kl == 3))
            p.inc(ins, 'T')
            p.wait(nc.scalar, 'T')
            ins = nc.scalar.activation(out=rem2p[:, c * CH:(c + 1) * CH], in_=bS2[:, :CH],
                                       func=AF.Identity)
            p.inc(ins, 'A')
            p.bkset(5, 'A')
        gamma_tick()
        qkv_done = p.cnt['V']
        SLOT_BANKS = [(4, 5), (6, 7), (0, 1)]
        hbs = [(h2, b) for h2 in range(2) for b in range(NB)]
        NP3 = 3
        st_scT = {}
        st_expA = {}
        st_maskV = {}
        st_pvT = {}
        st_rzA = {}

        def lstage_sc(i):
            h2, b = hbs[i]
            hoff = h2 * 64
            bx, by = SLOT_BANKS[i % NP3]
            jts = list(range(2, 4)) if b == 0 else list(range(4))
            p.wait(nc.tensor, 'V', qkv_done)
            p.bkwait(bx, by)
            for jt in jts:
                gt = (b - 1) * 256 + jt * 128
                bb = p.bank[bx] if jt < 2 else p.bank[by]
                ins = nc.tensor.matmul(bb[:, (jt % 2) * 256:(jt % 2) * 256 + 256],
                                       kp[hoff:hoff + 64, gt:gt + 128],
                                       qp[hoff:hoff + 64, b * 256:(b + 1) * 256],
                                       start=True, stop=True)
            p.inc(ins, 'T')
            st_scT[i] = p.cnt['T']

        def lstage_exp(i):
            h2, b = hbs[i]
            bx, by = SLOT_BANKS[i % NP3]
            stg = pstg[i % 2]
            jts = list(range(2, 4)) if b == 0 else list(range(4))
            p.wait(nc.scalar, 'T', st_scT[i])
            if i >= 2:
                p.wait(nc.scalar, 'V', st_maskV.get(i - 2, 0))
            if b != 0:
                ins = nc.scalar.activation(out=stg[:, 0:512], in_=p.bank[bx][:, 0:512],
                                           func=AF.Exp, scale=0.125)
            ins = nc.scalar.activation(out=stg[:, 512:1024], in_=p.bank[by][:, 0:512],
                                       func=AF.Exp, scale=0.125)
            p.inc(ins, 'A')
            st_expA[i] = p.cnt['A']
            p.bkset(bx, 'A')
            p.bkset(by, 'A')

        def lstage_maskpv(i):
            h2, b = hbs[i]
            bx, by = SLOT_BANKS[i % NP3]
            bv = p.bank[bx]
            stg = pstg[i % 2]
            jts = list(range(2, 4)) if b == 0 else list(range(4))
            p.wait(nc.vector, 'A', st_expA[i])
            if b == 0:
                nc.vector.memset(stg[:, 0:512], 0.0)
            ins = nc.vector.tensor_tensor(out=stg[:, :], in0=stg[:, :], in1=maskb[:, :],
                                          op=OP.mult)
            p.inc(ins, 'V')
            st_maskV[i] = p.cnt['V']
            p.wait(nc.tensor, 'V', st_maskV[i])
            for jt in jts:
                gtt = (b - 1) * 2 + jt
                ins = nc.tensor.matmul(bv[:, 0:256],
                                       vp[:, gtt * 256 + h2 * 128: gtt * 256 + h2 * 128 + 128],
                                       stg[:, jt * 256:(jt + 1) * 256],
                                       start=(jt == jts[0]), stop=(jt == jts[-1]))
            p.inc(ins, 'T')
            st_pvT[i] = p.cnt['T']

        def lstage_rzout(i):
            h2, b = hbs[i]
            hoff = h2 * 64
            bx, by = SLOT_BANKS[i % NP3]
            bv = p.bank[bx]
            rz = rzs[:, (i % 2) * 256:(i % 2) * 256 + 256]
            p.wait(nc.scalar, 'T', st_pvT[i])
            nc.scalar.activation(out=rz, in_=bv[64:128, 0:256], func=AF.Ln)
            ins = nc.scalar.activation(out=rz, in_=rz, func=AF.Exp, scale=-1.0)
            p.inc(ins, 'A')
            st_rzA[i] = p.cnt['A']
            p.wait(nc.vector, 'A', st_rzA[i])
            ins = nc.vector.tensor_tensor(out=locT[hoff:hoff + 64, b * 256:(b + 1) * 256],
                                          in0=bv[0:64, 0:256], in1=rz, op=OP.mult)
            p.inc(ins, 'V')
            p.bkset(bx, 'V')

        stages = [lstage_sc, lstage_exp, lstage_maskpv, lstage_rzout]
        NS6 = len(stages)
        for step in range(len(hbs) + NS6 - 1):
            for si in range(NS6 - 1, -1, -1):
                i = step - si
                if 0 <= i < len(hbs):
                    stages[si](i)
        gamma_tick()
        loc_done = p.cnt['V']
        if pr > 0:
            p.wait(nc.vector, 'W', p.combw)
        for c in range(NCH):
            p.bkwait(2, 3)
            ins = nc.tensor.matmul(pA[:, :CH], a2b[:, :], glp[:, c * CH:(c + 1) * CH],
                                   start=True, stop=True)
            ins = nc.tensor.matmul(pB[:, :CH], a2b[:, :], grp[:, c * CH:(c + 1) * CH],
                                   start=True, stop=True)
            p.inc(ins, 'T')
            p.wait(nc.vector, 'T')
            nc.vector.tensor_tensor(out=combp[:, c * CH:(c + 1) * CH],
                                    in0=pA[:, :CH], in1=locT[:, c * CH:(c + 1) * CH], op=OP.mult)
            nc.vector.tensor_tensor(out=rem2p[:, c * CH:(c + 1) * CH],
                                    in0=pB[:, :CH], in1=rem2p[:, c * CH:(c + 1) * CH], op=OP.mult)
            ins = nc.vector.tensor_tensor(out=combp[:, c * CH:(c + 1) * CH],
                                          in0=combp[:, c * CH:(c + 1) * CH],
                                          in1=rem2p[:, c * CH:(c + 1) * CH], op=OP.add)
            p.inc(ins, 'V')
            p.bkset(2, 'V')
            p.bkset(3, 'V')
        p.wait(nc.gpsimd, 'V')
        ins = p.dma(nc.gpsimd, 'W', out=s_comb[pr * 128:(pr + 1) * 128, :], in_=combp[:, :])
        p.combw = p.cnt['W']

    for _ in range(6):
        gamma_tick()
    p.gamma_pre = True
    p.wait(nc.gpsimd, 'T')
    p.wait(nc.gpsimd, 'A')
    for gg in range(4):
        ins = p.dma(nc.gpsimd, 'W', gate=(gg == 0), out=woutb[gg][:, :],
                                  in_=d_wout[gg * 256:(gg + 1) * 256, :]
                                  .rearrange("(k q) o -> q k o", q=128))
    wout_ld = p.cnt['W']
    cc_free = [0, 0]
    for c in range(NCH):
        cc = Vp[c % 2]
        p.wait(nc.sync, 'W', p.combw)
        p.wait(nc.sync, 'T', cc_free[c % 2] if c >= 2 else p.cnt['T'])
        ins = p.dma(nc.sync, 'L', out=cc[:, 0:NTL * CH],
                                in_=s_comb[:, c * CH:(c + 1) * CH].rearrange("(g q) t -> q g t", q=128))
        ldc = p.cnt['L']
        for j in range(NTL):
            bb = bS1 if j % 2 == 0 else bS2
            p.wait(nc.tensor, 'L', ldc)
            p.wait(nc.tensor, 'W', wout_ld)
            p.bkwait(4 + j % 2)
            for g in range(NTL):
                ins = nc.tensor.matmul(bb[:, :CH],
                                       woutb[g // 2][:, (g % 2) * D + j * 128:
                                                     (g % 2) * D + (j + 1) * 128],
                                       cc[:, g * CH:(g + 1) * CH], start=(g == 0), stop=(g == NTL - 1))
            p.inc(ins, 'T')
            p.wait(nc.scalar, 'T')
            if j == 0:
                p.wait(nc.scalar, 'V')
            ins = nc.scalar.activation(out=xbc[:, j * CH:(j + 1) * CH], in_=bb[:, :CH],
                                       func=AF.Identity)
            p.inc(ins, 'A')
            p.bkset(4 + j % 2, 'A')
        cc_free[c % 2] = p.cnt['T']
        p.wait(nc.scalar, 'A')
        for i in range(NTL):
            ins = nc.scalar.activation(out=sq[:, i * CH:(i + 1) * CH], in_=xbc[:, i * CH:(i + 1) * CH],
                                       func=AF.Square)
        p.inc(ins, 'A')
        p.wait(nc.tensor, 'A')
        for i in range(NTL):
            ins = nc.tensor.matmul(pS[0:1, :CH], p.onesc[:, 0:1], xbc[:, i * CH:(i + 1) * CH],
                                   start=(i == 0), stop=(i == NTL - 1))
        p.inc(ins, 'T')
        for i in range(NTL):
            ins = nc.tensor.matmul(pQ[0:1, :CH], p.onesc[:, 0:1], sq[:, i * CH:(i + 1) * CH],
                                   start=(i == 0), stop=(i == NTL - 1))
        p.inc(ins, 'T')
        emit_ln_rows(p, pS, pQ, rM, rA, rB, rT, CH)
        emit_ln_bcast(p, rA, rB, pA, pB, p.onesr, CH)
        p.wait(nc.sync, 'W')
        p.wait(nc.sync, 'V')
        ins = p.dma(nc.sync, 'L', out=xchunk[:, :],
                                in_=s_x[:, c * CH:(c + 1) * CH].rearrange("(i q) t -> q i t", q=128))
        p.wait(nc.vector, 'L')
        p.wait(nc.vector, 'T')
        for i in range(NTL):
            asl = xbc[:, i * CH:(i + 1) * CH]
            nc.vector.tensor_tensor(out=asl, in0=asl, in1=pA[:, :CH], op=OP.mult)
            nc.vector.tensor_tensor(out=asl, in0=asl, in1=pB[:, :CH], op=OP.add)
            ins = nc.vector.tensor_tensor(out=xchunk[:, i * CH:(i + 1) * CH],
                                          in0=xchunk[:, i * CH:(i + 1) * CH], in1=asl, op=OP.add)
        p.inc(ins, 'V')
        p.bkset(2, 'V')
        p.bkset(3, 'V')
        x3_done = p.cnt['V']
        if last:
            p.wait(nc.gpsimd, 'V', x3_done)
            for i in range(NTL):
                ins = p.dma(nc.gpsimd, 'O', gate=(i == 0),
                            out=p.d_y[i * 128:(i + 1) * 128, c * CH:(c + 1) * CH],
                            in_=xchunk[:, i * CH:(i + 1) * CH])
        else:
            p.wait(nc.gpsimd, 'V', x3_done)
            ins = p.dma(nc.gpsimd, 'W', 
                out=s_x[:, c * CH:(c + 1) * CH].rearrange("(i q) t -> q i t", q=128),
                in_=xchunk[:, :])
            p.wait(nc.scalar, 'V', x3_done)
            emit_ln_stats(p, [xchunk[:, i * CH:(i + 1) * CH] for i in range(NTL)], xbc, sq,
                          p.onesc, pS, pQ, CH)
            emit_ln_rows(p, pS, pQ, rM, rA, rB, rT, CH)
            emit_ln_bcast(p, rA, rB, pA, pB, p.onesr, CH)
            p.wait(nc.vector, 'T')
            for i in range(NTL):
                nc.vector.tensor_tensor(out=xbc[:, i * CH:(i + 1) * CH],
                                        in0=xchunk[:, i * CH:(i + 1) * CH], in1=pA[:, :CH],
                                        op=OP.mult)
                ins = nc.vector.tensor_tensor(out=xn[:, i * T + c * CH: i * T + c * CH + CH],
                                              in0=xbc[:, i * CH:(i + 1) * CH], in1=pB[:, :CH],
                                              op=OP.add)
            p.inc(ins, 'V')
            p.bkset(2, 'V')
            p.bkset(3, 'V')
    if last:
        p.output_written = True
    p.x_dram = s_x



def emit_l1_ffn(p, xs, last, from_attn=False):
    nc = p.nc
    FF = 4096
    CH = 256
    NCH = T // CH
    NTOT = float(FF * D)

    d_w1t = p.din("l1f_W1T", [D, FF])
    d_w2t = p.din("l1f_W2T", [FF, D])
    d_b1 = p.din("l1f_b1p", [128, 32])
    d_b2 = p.din("l1f_b2p", [128, NTL])
    d_of = p.din("c_ones_f32col", [128, 1])
    s_x = p.x_dram if from_attn else p.dtmp("s_x_l1f", [D, T], F32)

    xsb = xs.bitcast(BF16)
    xn = xsb[:, 0:NTL * T]
    harea = xsb[:, 16384:24576]
    fout = xsb[:, 24576:26624]
    sq = xsb[:, 26624:28672]
    xchunk = xs[:, 14336:16384]
    wtmpB = [xs[:, 8192:12288], xs[:, 12288:16384]]
    wtmp = wtmpB[0]

    wq1 = p.sb("l1f_wq1", [128, NTL * FF], BF16)
    wq2 = p.sb("l1f_wq2", [128, 32 * D], BF16)
    b1s = p.sb("l1f_b1s", [128, 32], F32)
    b2s = p.sb("l1f_b2s", [128, NTL], F32)
    onef = p.sb("l1f_onef", [128, 1], F32)
    gacc = p.sb("l1f_gacc", [128, 40], F32)
    gth = p.sb("l1f_gth", [128, 4], F32)
    grow = p.sb("l1f_grow", [1, 4], F32)
    qtmp = p.sb("l1f_qtmp", [128, 4096], BF16)
    rM = p.sb("l1f_rM", [1, CH], F32)
    rA = p.sb("l1f_rA", [1, CH], F32)
    rB = p.sb("l1f_rB", [1, CH], F32)
    rT = p.sb("l1f_rT", [1, CH], F32)

    pS, pQ, pA, pB = p.bank[0], p.bank[1], p.bank[2], p.bank[3]
    bM1 = [p.bank[4], p.bank[5]]
    bM2 = [p.bank[6], p.bank[7]]

    for dst, dd in ((b1s, d_b1), (b2s, d_b2), (onef, d_of)):
        ins = p.dma(nc.sync, 'L', out=dst[:, :], in_=dd)

    if not from_attn:
        p.wait(nc.gpsimd, 'L')
        p.wait(nc.gpsimd, 'V')
        p.wait(nc.gpsimd, 'A')
        for i in range(NTL):
            ins = p.dma(nc.gpsimd, 'W', gate=(i == 0), out=s_x[i * 128:(i + 1) * 128, :],
                        in_=xs[:, i * T:(i + 1) * T])
        spill_done = p.cnt['W']
    for c in range(NCH if not from_attn else 0):
        p.wait(nc.sync, 'W', spill_done)
        if c > 0:
            p.wait(nc.sync, 'V')
        ins = p.dma(nc.sync, 'L', out=xchunk,
                    in_=s_x[:, c * CH:(c + 1) * CH].rearrange("(i q) t -> q i t", q=128))
        p.wait(nc.scalar, 'L')
        p.wait(nc.scalar, 'V')
        if c == 0:
            p.wait(nc.scalar, 'W', spill_done)
        emit_ln_stats(p, [xchunk[:, i * CH:(i + 1) * CH] for i in range(NTL)], fout, sq,
                      p.onesc, pS, pQ, CH)
        emit_ln_rows(p, pS, pQ, rM, rA, rB, rT, CH)
        emit_ln_bcast(p, rA, rB, pA, pB, p.onesr, CH)
        p.wait(nc.vector, 'T')
        if c == 0:
            p.wait(nc.vector, 'W', spill_done)
        for i in range(NTL):
            nc.vector.tensor_tensor(out=xchunk[:, i * CH:(i + 1) * CH],
                                    in0=xchunk[:, i * CH:(i + 1) * CH], in1=pA[:, :CH], op=OP.mult)
            ins = nc.vector.tensor_tensor(out=xn[:, i * T + c * CH: i * T + c * CH + CH],
                                          in0=xchunk[:, i * CH:(i + 1) * CH], in1=pB[:, :CH],
                                          op=OP.add)
        p.inc(ins, 'V')

    red_done = [0, 0]
    gi = 0
    gpre = getattr(p, 'gamma_pre', False)
    for wi, (dd, ntile, wcols, coff) in enumerate(
            () if gpre else ((d_w1t, NTL, FF, 0), (d_w2t, 32, D, 8))):
        for ti in range(ntile):
            par = gi % 2
            wb = wtmpB[par]
            qeng, qsem = (nc.sync, 'L') if par == 0 else (nc.gpsimd, 'W')
            if gi >= 2:
                p.wait(qeng, 'V', red_done[par])
            p.dma(qeng, qsem, gate=False, out=wb[:, 0:wcols], in_=dd[ti * 128:(ti + 1) * 128, :])
            p.wait(nc.vector, 'L')
            p.wait(nc.vector, 'W')
            ins = nc.vector.tensor_reduce(out=gacc[:, coff + ti:coff + ti + 1], in_=wb[:, 0:wcols],
                                          axis=mybir.AxisListType.X, op=OP.add,
                                          apply_absolute_value=True)
            p.inc(ins, 'V')
            red_done[par] = p.cnt['V']
            gi += 1
    gsrc, nga, c1 = ((p.gacc_g, 48, 16) if gpre else (gacc, 40, 8))
    p.wait(nc.tensor, 'V')
    p.wait(nc.tensor, 'L')
    ins = nc.tensor.matmul(pA[0:1, 0:nga], onef[:, 0:1], gsrc[:, 0:nga], start=True, stop=True)
    p.inc(ins, 'T')
    p.wait(nc.vector, 'T')
    ins = nc.vector.tensor_reduce(out=grow[0:1, 0:1], in_=pA[0:1, 0:c1],
                                  axis=mybir.AxisListType.X, op=OP.add)
    ins = nc.vector.tensor_reduce(out=grow[0:1, 1:2], in_=pA[0:1, c1:nga],
                                  axis=mybir.AxisListType.X, op=OP.add)
    p.inc(ins, 'V')
    p.wait(nc.vector, 'V')
    nc.vector.tensor_scalar(out=grow[0:1, 2:4], in0=grow[0:1, 0:2], scalar1=-0.5 / NTOT,
                            scalar2=-0.5e-5, op0=OP.mult, op1=OP.add)
    ins = nc.vector.tensor_scalar(out=grow[0:1, 0:2], in0=grow[0:1, 0:2], scalar1=0.5 / NTOT,
                                  scalar2=0.5e-5, op0=OP.mult, op1=OP.add)
    p.inc(ins, 'V')
    p.wait(nc.tensor, 'V')
    ins = nc.tensor.matmul(pB[:, 0:4], p.onesr[0:1, :], grow[0:1, 0:4], start=True, stop=True)
    p.inc(ins, 'T')
    p.wait(nc.vector, 'T')
    ins = nc.vector.tensor_copy(out=gth[:, :], in_=pB[:, 0:4])
    p.inc(ins, 'V')
    p.wait(nc.vector, 'V')

    w2tmpB = [xs[:, 12288:13312], xs[:, 13312:14336]]
    for wi, (dd, ntile, wcols, wdst, dstw) in enumerate(
            ((d_w1t, NTL, FF, wq1, FF), (d_w2t, 32, D, wq2, D))):
        thc = gth[:, wi:wi + 1]
        nthc = gth[:, 2 + wi:3 + wi]
        bufs = wtmpB if wi == 0 else w2tmpB
        for ti in range(ntile):
            par = gi % 2
            wb = bufs[par]
            qeng, qsem = (nc.sync, 'L') if par == 0 else (nc.gpsimd, 'W')
            p.wait(qeng, 'V', wq1_done if (wi == 1 and ti < 2) else red_done[par])
            p.dma(qeng, qsem, gate=False, out=wb[:, 0:wcols], in_=dd[ti * 128:(ti + 1) * 128, :])
            p.wait(nc.vector, 'L')
            p.wait(nc.vector, 'W')
            nc.vector.tensor_scalar(out=qtmp[:, 0:wcols], in0=wb[:, 0:wcols], scalar1=thc,
                                    scalar2=-1.0, op0=OP.is_ge, op1=OP.add)
            ins = nc.vector.scalar_tensor_tensor(out=wdst[:, ti * dstw:(ti + 1) * dstw],
                                                 in0=wb[:, 0:wcols], scalar=nthc,
                                                 in1=qtmp[:, 0:wcols],
                                                 op0=OP.is_ge, op1=OP.add)
            p.inc(ins, 'V')
            red_done[gi % 2] = p.cnt['V']
            gi += 1
        if wi == 0:
            wq1_done = p.cnt['V']

    wq2_done = p.cnt['V']
    for c in range(NCH):
        for mf in range(32):
            bb = bM1[mf % 2]
            p.wait(nc.tensor, 'V', wq1_done)
            p.bkwait(4 + mf % 2)
            for k in range(NTL):
                ins = nc.tensor.matmul(bb[:, :CH], wq1[:, k * FF + mf * 128: k * FF + (mf + 1) * 128],
                                       xn[:, k * T + c * CH: k * T + c * CH + CH],
                                       start=(k == 0), stop=(k == NTL - 1))
            p.inc(ins, 'T')
            p.wait(nc.scalar, 'T')
            ins = nc.scalar.activation(out=harea[:, mf * CH:(mf + 1) * CH], in_=bb[:, :CH],
                                       func=AF.Relu, bias=b1s[:, mf:mf + 1])
            p.inc(ins, 'A')
            p.bkset(4 + mf % 2, 'A')
        h_done = p.cnt['A']
        for md in range(NTL):
            bb = bM2[md % 2]
            p.wait(nc.tensor, 'V', wq2_done)
            p.wait(nc.tensor, 'A', h_done)
            p.bkwait(6 + md % 2)
            for kf in range(32):
                ins = nc.tensor.matmul(bb[:, :CH], wq2[:, kf * D + md * 128: kf * D + (md + 1) * 128],
                                       harea[:, kf * CH:(kf + 1) * CH],
                                       start=(kf == 0), stop=(kf == 31))
            p.inc(ins, 'T')
            p.wait(nc.scalar, 'T')
            ins = nc.scalar.activation(out=fout[:, md * CH:(md + 1) * CH], in_=bb[:, :CH],
                                       func=AF.Identity, bias=b2s[:, md:md + 1])
            p.inc(ins, 'A')
            p.bkset(6 + md % 2, 'A')
        p.wait(nc.scalar, 'A')
        p.wait(nc.scalar, 'V')
        for i in range(NTL):
            ins = nc.scalar.activation(out=sq[:, i * CH:(i + 1) * CH], in_=fout[:, i * CH:(i + 1) * CH],
                                       func=AF.Square)
        p.inc(ins, 'A')
        p.wait(nc.tensor, 'A')
        for i in range(NTL):
            ins = nc.tensor.matmul(pS[0:1, :CH], p.onesc[:, 0:1], fout[:, i * CH:(i + 1) * CH],
                                   start=(i == 0), stop=(i == NTL - 1))
        p.inc(ins, 'T')
        for i in range(NTL):
            ins = nc.tensor.matmul(pQ[0:1, :CH], p.onesc[:, 0:1], sq[:, i * CH:(i + 1) * CH],
                                   start=(i == 0), stop=(i == NTL - 1))
        p.inc(ins, 'T')
        emit_ln_rows(p, pS, pQ, rM, rA, rB, rT, CH)
        emit_ln_bcast(p, rA, rB, pA, pB, p.onesr, CH)
        p.wait(nc.sync, 'T')
        p.wait(nc.sync, 'O')
        ins = p.dma(nc.sync, 'L', out=xchunk,
                    in_=s_x[:, c * CH:(c + 1) * CH].rearrange("(i q) t -> q i t", q=128))
        p.wait(nc.vector, 'L')
        p.wait(nc.vector, 'T')
        for i in range(NTL):
            fsl = fout[:, i * CH:(i + 1) * CH]
            nc.vector.tensor_tensor(out=fsl, in0=fsl, in1=pA[:, :CH], op=OP.mult)
            nc.vector.tensor_tensor(out=fsl, in0=fsl, in1=pB[:, :CH], op=OP.add)
            ins = nc.vector.tensor_tensor(out=xchunk[:, i * CH:(i + 1) * CH],
                                          in0=xchunk[:, i * CH:(i + 1) * CH], in1=fsl, op=OP.add)
        p.inc(ins, 'V')
        p.bkset(2, 'V')
        p.bkset(3, 'V')
        p.wait(nc.gpsimd, 'V')
        ins = p.dma(nc.gpsimd, 'O',
                    out=p.d_y[:, c * CH:(c + 1) * CH].rearrange("(i q) t -> q i t", q=128),
                    in_=xchunk)
    p.output_written = True



EMITTERS = {}


def _register():
    EMITTERS["l0_attn"] = lambda p, xs, last: emit_l0_attn(p, xs)
    EMITTERS["l0_ffn"] = lambda p, xs, last: emit_l0_ffn(p, xs)
    EMITTERS["l1_attn"] = emit_l1_attn
    EMITTERS["l1_ffn"] = emit_l1_ffn


_register()


def build_program(block_names):
    p = Prog()
    nc = p.nc
    with p.es:
        d_x = p.din("xT", [D, T])
        d_y = p.dout("yT", [D, T])
        d_onesb = p.din("c_ones_bf", [128, 1], BF16)
        d_onesf = p.din("c_ones_f", [1, 128], F32)
        d_eps = p.din("c_eps", [128, 1], F32)
        xs = p.sbuf("xs", [128, NTL * T], F32)
        p.big = p.sbuf("big_arena", [128, 72960], BF16)
        p.big_f = p.big.bitcast(F32)
        p.bank = [p.psum("bank%d" % i, [128, 512]) for i in range(8)]
        p.bk = [(None, 0)] * 8
        p.onesc = p.sbuf("onesc_bf", [128, 1], BF16)
        p.onesr = p.sbuf("onesr_f", [1, 128], F32)
        p.epscol = p.sbuf("epscol", [128, 1], F32)
        nc.const_aps.register(F32, LN_EPS, p.epscol[:, 0:1])
        p.onecf = p.sbuf("onecf", [128, 1], F32)
        nc.const_aps.register(F32, 1.0, p.onecf[:, 0:1])
        p.gacc_g = p.sbuf("gacc_g", [128, 48], F32)
        p.sem('L', 'W', 'T', 'V', 'A', 'G', 'O')
        with nc.Block():
            ins = p.dma(nc.sync, 'L', out=p.onesc[:, :], in_=d_onesb)
            ins = p.dma(nc.sync, 'L', out=p.onesr[:, :], in_=d_onesf)
            ins = p.dma(nc.sync, 'L', out=p.epscol[:, :], in_=d_eps)
            ins = p.dma(nc.sync, 'L', out=p.onecf[:, :], in_=p.din("c_ones_f32col", [128, 1]))
            p.const_L = p.cnt['L']
            for i in range(NTL):
                ins = p.dma(nc.sync, 'L', gate=(i == 0), out=xs[:, i * T:(i + 1) * T],
                                        in_=d_x[i * 128:(i + 1) * 128, :])
            p.output_written = False
            p.d_y = d_y
            for bi, b in enumerate(block_names):
                p.carve_reset()
                if bi > 0:
                    p.barrier()
                    if getattr(p, "debug", False):
                        p.dump("xs_after_%d" % bi, xs[:, :])
                if b == "l1_ffn" and bi > 0 and block_names[bi - 1] == "l1_attn":
                    emit_l1_ffn(p, xs, True, from_attn=True)
                else:
                    EMITTERS[b](p, xs, bi == len(block_names) - 1)
            if not p.output_written:
                p.wait(nc.gpsimd, 'V')
                p.wait(nc.gpsimd, 'A')
                for i in range(NTL):
                    ins = p.dma(nc.gpsimd, 'O', gate=(i == 0), out=d_y[i * 128:(i + 1) * 128, :],
                                              in_=xs[:, i * T:(i + 1) * T])
            p.wait(nc.gpsimd, 'O')
    return p



def _pack_vec(v):
    return np.ascontiguousarray(np.asarray(v, dtype=np.float32).reshape(NTL, 128).T)


def _host_tensors(inputs):
    t = {}
    vecs = np.zeros((128, 7 * NTL), np.float32)
    vecs[:, 0:NTL] = _pack_vec(inputs["L0_attn_time_decay"])
    vecs[:, NTL:2 * NTL] = _pack_vec(inputs["L0_attn_time_first"])
    vecs[:, 2 * NTL:3 * NTL] = _pack_vec(np.asarray(inputs["L0_attn_time_mix_key"]).ravel())
    vecs[:, 3 * NTL:4 * NTL] = _pack_vec(np.asarray(inputs["L0_attn_time_mix_value"]).ravel())
    vecs[:, 4 * NTL:5 * NTL] = _pack_vec(np.asarray(inputs["L0_attn_time_mix_receptance"]).ravel())
    vecs[:, 5 * NTL:6 * NTL] = _pack_vec(inputs["L0_attn_bo"])
    t["l0a_vecs"] = vecs
    t["l0a_WkT"] = np.ascontiguousarray(np.asarray(inputs["L0_attn_Wk"], np.float32).T)
    t["l0a_WvT"] = np.ascontiguousarray(np.asarray(inputs["L0_attn_Wv"], np.float32).T)
    t["l0a_WrT"] = np.ascontiguousarray(np.asarray(inputs["L0_attn_Wr"], np.float32).T)
    t["l0a_WoT"] = np.ascontiguousarray(np.asarray(inputs["L0_attn_Wo"], np.float32).T)
    E, S, HD, ES = 16, 16, 256, 256
    t["l0f_mu2d"] = np.ascontiguousarray(np.asarray(inputs["L0_ffn_mu"], np.float32).reshape(D, ES))
    t["l0f_w1"] = np.ascontiguousarray(np.asarray(inputs["L0_ffn_w1"], np.float32).reshape(E * D, HD))
    t["l0f_w2"] = np.ascontiguousarray(np.asarray(inputs["L0_ffn_w2"], np.float32).reshape(E * HD, D))
    gbp = np.zeros((128, 2 * NTL), np.float32)
    gbp[:, 0:NTL] = _pack_vec(inputs["L0_ffn_ln_gamma"])
    gbp[:, NTL:2 * NTL] = _pack_vec(inputs["L0_ffn_ln_beta"])
    t["l0f_gb"] = gbp
    b1 = np.asarray(inputs["L0_ffn_b1"], np.float32)
    b1p = np.zeros((128, 2 * E), np.float32)
    for mh in range(2):
        for e in range(E):
            b1p[:, mh * 16 + e] = b1[e, mh * 128:(mh + 1) * 128]
    t["l0f_b1p"] = b1p
    b2 = np.asarray(inputs["L0_ffn_b2"], np.float32)
    b2p = np.zeros((128, 8 * E), np.float32)
    for md in range(8):
        for e in range(E):
            b2p[:, md * 16 + e] = b2[e, md * 128:(md + 1) * 128]
    t["l0f_b2p"] = b2p
    t["l0f_scale_col"] = np.broadcast_to(np.asarray(inputs["L0_ffn_scale"], np.float32).reshape(1, 1),
                                         (128, 1)).copy()
    t["c_ident"] = np.eye(128, dtype=np.float32).astype(BF)
    t["l1a_wq"] = np.ascontiguousarray(np.asarray(inputs["L1_attn_wq"], np.float32))
    t["l1a_wk"] = np.ascontiguousarray(np.asarray(inputs["L1_attn_wk"], np.float32))
    t["l1a_wv"] = np.ascontiguousarray(np.asarray(inputs["L1_attn_wv"], np.float32))
    t["l1a_wg3"] = np.ascontiguousarray(np.concatenate([
        np.asarray(inputs["L1_attn_wgk"], np.float32),
        np.asarray(inputs["L1_attn_wgv"], np.float32),
        np.asarray(inputs["L1_attn_wgq"], np.float32)], axis=0))
    t["l1a_rout"] = np.ascontiguousarray(np.asarray(inputs["L1_attn_r_out_w"], np.float32))
    t["l1a_wout"] = np.ascontiguousarray(np.asarray(inputs["L1_attn_w_out"], np.float32))
    gl_w = np.asarray(inputs["L1_attn_gate_local_w"], np.float32)
    gr_w = np.asarray(inputs["L1_attn_gate_remote_w"], np.float32)
    gp = np.zeros((128, 32), np.float32)
    for gi, gw in ((0, gl_w), (1, gr_w)):
        for pr in range(8):
            for h2 in range(2):
                gp[h2 * 64:(h2 + 1) * 64, gi * 16 + pr * 2 + h2] = gw[pr * 2 + h2]
    t["l1a_gatep"] = gp
    a2 = np.zeros((2, 128), np.float32)
    a2[0, 0:64] = 1.0
    a2[1, 64:128] = 1.0
    t["l1a_a2"] = a2.astype(BF)
    jj = np.arange(512)[:, None]
    ii = np.arange(256)[None, :]
    allowed = (jj > ii) & (jj <= ii + 256)
    mk = np.zeros((128, 1024), np.float32)
    for jt in range(4):
        mk[:, jt * 256:(jt + 1) * 256] = allowed[jt * 128:(jt + 1) * 128, :]
    t["l1a_mask"] = mk.astype(BF)
    lm = (np.arange(64)[:, None] <= (np.arange(2048) // 32)[None, :]).astype(np.float32)
    t["l1a_lmask"] = lm.astype(BF)
    t["l1f_W1T"] = np.ascontiguousarray(np.asarray(inputs["L1_ffn_W1"], np.float32).T)
    t["l1f_W2T"] = np.ascontiguousarray(np.asarray(inputs["L1_ffn_W2"], np.float32).T)
    b1f = np.asarray(inputs["L1_ffn_b1"], np.float32)
    t["l1f_b1p"] = np.ascontiguousarray(b1f.reshape(32, 128).T)
    b2f = np.asarray(inputs["L1_ffn_b2"], np.float32)
    t["l1f_b2p"] = np.ascontiguousarray(b2f.reshape(NTL, 128).T)
    t["c_ones_f32col"] = np.ones((128, 1), np.float32)
    return t


def kernel(**inputs):
    x = np.asarray(inputs["L0_attn_x"], np.float32)
    host = _host_tensors(inputs)
    host["c_ones_bf"] = np.ones((128, 1), BF)
    host["c_ones_f"] = np.ones((1, 128), np.float32)
    host["c_eps"] = np.full((128, 1), LN_EPS, np.float32)
    xT = [np.ascontiguousarray(x[b].T) for b in range(8)]
    for blocks in LAUNCH_PLAN:
        p = build_program(blocks)
        in_maps = []
        for b in range(8):
            m = {"xT": xT[b]}
            for name in p.inputs_used:
                if name != "xT":
                    m[name] = host[name]
            in_maps.append(m)
        res = run_bass_kernel_spmd(p.nc, in_maps, core_ids=list(range(8)))
        xT = [np.asarray(res.results[b]["yT"], np.float32) for b in range(8)]
    out = np.stack([np.ascontiguousarray(xT[b].T) for b in range(8)], axis=0)
    return out
